# version 04
# baseline (speedup 1.0000x reference)
from contextlib import ExitStack

import numpy as np
import concourse.bass as bass
import concourse.mybir as mybir
from concourse.bass_utils import run_bass_kernel_spmd

F32 = mybir.dt.float32
BF16 = mybir.dt.bfloat16
I32 = mybir.dt.int32
AF = mybir.ActivationFunctionType
ALU = mybir.AluOpType
AX = mybir.AxisListType
AP = bass.AP

T = 4096
D = 1024
NT = T // 128
ND = D // 128
MAGIC = float(3 * 2 ** 22)
N_CORES = 8
DBG = False
UPTO = 99
SUB2A = 99


def emit_dump(cx, dram, name, src_ap, shape, marker):
    if not DBG:
        return
    t = cx.nc.dram_tensor("dbg_" + name, list(shape), F32, kind="ExternalOutput")

    def d_g(g):
        sem, val = marker
        cx.wait(g, sem, val)
        cx.inc("sd_o2", g.dma_start(out=t[:, :], in_=src_ap), 16)
    cx.blk.gpsimd(d_g)


class Ctx:

    def __init__(self, nc, blk):
        self.nc = nc
        self.blk = blk
        self._sems = {}
        self._cnt = {}
        self._names = {}

    def sem(self, name):
        h = self.nc.alloc_semaphore(name=name)
        self._sems[name] = h
        self._cnt[name] = 0

    def h(self, name):
        return self._sems[name]

    def inc(self, name, instr, n=1):
        instr.then_inc(self._sems[name], n)
        self._cnt[name] += n
        return self._cnt[name]

    def val(self, name):
        return self._cnt[name]

    def wait(self, eng, name, v=None):
        eng.wait_ge(self._sems[name], self._cnt[name] if v is None else v)

    def uname(self, base):
        i = self._names.get(base, 0)
        self._names[base] = i + 1
        return f"{base}{i}"



def emit_tmfm_transpose(cx, src, dst, ident, pst, src_ready_sv, nchunks=8):
    ngrp = nchunks * 2

    def tr_t(t):
        cx.wait(t, "s_v", src_ready_sv)
        for grp in range(ngrp):
            if grp >= 2:
                t.wait_ge(cx.h("s_v"), src_ready_sv + grp - 1)
            for j in range(4):
                idx = grp * 4 + j
                tc, dc = idx // 8, idx % 8
                slot = (grp % 2) * 8 + j
                t.transpose(pst[:, slot * 128:(slot + 1) * 128],
                            src[:, tc * 1024 + dc * 128: tc * 1024 + (dc + 1) * 128],
                            ident[:, :])
            cx.inc("s_t", t.drain())
    cx.blk.tensor(tr_t)
    t0 = cx.val("s_t") - ngrp

    def tr_v(v):
        for grp in range(ngrp):
            cx.wait(v, "s_t", t0 + grp + 1)
            for j in range(4):
                idx = grp * 4 + j
                tc, dc = idx // 8, idx % 8
                slot = (grp % 2) * 8 + j
                v.tensor_copy(dst[:, dc * 1024 + tc * 128: dc * 1024 + (tc + 1) * 128],
                              pst[:, slot * 128:(slot + 1) * 128])
            cx.inc("s_v", v.drain())
    cx.blk.vector(tr_v)


def emit_ln(cx, xs, stat, wait_marker, eps=1e-5):
    nc = cx.nc
    sem, val = wait_marker
    with nc.sbuf_tensor(cx.uname("lnsq"), [128, 1024], BF16) as lnsq:
        def ln_v(v):
            cx.wait(v, sem, val)
            if DBG:
                cx.wait(v, "sd_o2")
            for tc in range(NT):
                xc = xs[:, tc * 1024:(tc + 1) * 1024]
                v.reduce_sum(stat[:, 104 + tc:105 + tc], xc, AX.X)
                v.scalar_tensor_tensor(lnsq[:, :], xc, 1.0, xc, ALU.mult, ALU.mult,
                                       accum_out=stat[:, 136 + tc:137 + tc])
                v.drain()
            v.tensor_scalar_mul(stat[:, 104:136], stat[:, 104:136], 1.0 / D)
            v.tensor_scalar_mul(stat[:, 136:168], stat[:, 136:168], 1.0 / D)
            v.drain()
            v.scalar_tensor_tensor(stat[:, 72:104], stat[:, 104:136], 1.0,
                                   stat[:, 104:136], ALU.mult, ALU.mult)
            v.drain()
            v.tensor_sub(stat[:, 136:168], stat[:, 136:168], stat[:, 72:104])
            v.drain()
            v.tensor_scalar_add(stat[:, 136:168], stat[:, 136:168], eps)
            v.drain()
            v.reciprocal(stat[:, 72:104], stat[:, 136:168])
            cx.inc("s_v", v.drain())
        cx.blk.vector(ln_v)
        sready = cx.val("s_v")

        def ln_s(s):
            cx.wait(s, "s_v", sready)
            s.activation(stat[:, 136:168], stat[:, 72:104], AF.Sqrt)
            cx.inc("s_a", s.drain())
        cx.blk.scalar(ln_s)
        rstd_ready = cx.val("s_a")

        def ln_v2(v):
            cx.wait(v, "s_a", rstd_ready)
            for tc in range(NT):
                v.tensor_scalar(xs[:, tc * 1024:(tc + 1) * 1024],
                                xs[:, tc * 1024:(tc + 1) * 1024],
                                stat[:, 104 + tc:105 + tc],
                                stat[:, 136 + tc:137 + tc],
                                ALU.subtract, ALU.mult)
            cx.inc("s_v", v.drain())
        cx.blk.vector(ln_v2)



def phase0_setup(cx, C, dram):
    nc = cx.nc
    es = ExitStack()
    iot = es.enter_context(nc.sbuf_tensor("iot", [128, 128], F32))
    brow = es.enter_context(nc.sbuf_tensor("brow", [1, D], F32))
    onesf = es.enter_context(nc.sbuf_tensor("onesf", [1, 128], F32))
    psb = es.enter_context(nc.psum_tensor("psb", [128, 1024], F32))

    def p0_g(g):
        g.iota(iot[:, :], [[1, 128]], channel_multiplier=-1,
               allow_small_or_imprecise_dtypes=True)
        g.drain()
        cx.inc("s_g", g.drain())
        cx.inc("sd_g", g.dma_start(
            out=AP(C["wqk"], 0, [[C["wqk"].shape[1], 128], [128, ND], [1, 128]]),
            in_=AP(dram["wqk0"], 0, [[128, 128], [128 * 128, ND], [1, 128]])), 16)
        cx.inc("sd_g", g.dma_start(
            out=AP(C["wv"], 0, [[C["wv"].shape[1], 128], [64, ND], [1, 64]]),
            in_=AP(dram["wv0"], 0, [[64, 128], [128 * 64, ND], [1, 64]])), 16)
        cx.inc("sd_g", g.dma_start(out=brow[:, :], in_=dram["bias0"][:, :]), 16)
        cx.inc("sd_g", g.dma_start(out=C["scal"][0:1, 0:1], in_=dram["ws0"][:, :]), 16)
    cx.blk.gpsimd(p0_g)
    C["w0ld"] = cx.val("sd_g")
    ld_done = cx.val("sd_g")

    def p0_v(v):
        cx.wait(v, "s_g", 1)
        v.tensor_single_scalar(C["ident"][:, :], iot[:, :], 0.0, ALU.is_equal)
        v.tensor_single_scalar(C["identf"][:, :], iot[:, :], 0.0, ALU.is_equal)
        v.memset(C["onesb"][:, :], 1.0)
        v.memset(onesf[:, :], 1.0)
        v.memset(C["onesf"][:, :], 1.0)
        cx.inc("s_v", v.drain())
    cx.blk.vector(p0_v)
    C["const_done"] = cx.val("s_v")

    def p0_g2(g):
        cx.wait(g, "s_v", C["const_done"])
        g.iota(iot[:, :], [[1, 128]], channel_multiplier=1,
               allow_small_or_imprecise_dtypes=True)
        g.drain()
        cx.inc("s_g", g.drain())
    cx.blk.gpsimd(p0_g2)

    def p0_v2(v):
        cx.wait(v, "s_g", 2)
        v.tensor_single_scalar(C["identJ"][:, :], iot[:, :], 127.0, ALU.is_equal)
        cx.inc("s_v", v.drain())
    cx.blk.vector(p0_v2)
    C["const_done"] = cx.val("s_v")

    def p0_t(t):
        cx.wait(t, "s_v", C["const_done"])
        cx.wait(t, "sd_g", ld_done)
        for nb in range(2):
            t.matmul(psb[:, nb * 512:(nb + 1) * 512], onesf[:, :],
                     brow[:, nb * 512:(nb + 1) * 512], start=True, stop=True)
        cx.inc("s_t", t.drain())
    cx.blk.tensor(p0_t)
    bias_mm = cx.val("s_t")

    def p0_v2(v):
        cx.wait(v, "s_t", bias_mm)
        v.tensor_copy(C["b0row"][:, :], psb[:, :])
        cx.inc("s_v", v.drain())
    cx.blk.vector(p0_v2)

    def p0_t2(t):
        cx.wait(t, "s_v", cx.val("s_v"))
        t.matmul(psb[:, 0:1], onesf[:, :], C["scal"][0:1, 0:1], start=True, stop=True)
        cx.inc("s_t", t.drain())
    cx.blk.tensor(p0_t2)
    ws_mm = cx.val("s_t")

    def p0_v3(v):
        cx.wait(v, "s_t", ws_mm)
        v.tensor_copy(C["scal"][:, 0:1], psb[:, 0:1])
        cx.inc("s_v", v.drain())
    cx.blk.vector(p0_v3)
    C["bcast_done"] = cx.val("s_v")
    C["const_done"] = cx.val("s_v")
    return es


def phase2_l0_attn(cx, C, dram, xs):
    nc, blk = cx.nc, cx.blk
    stat, ident, identf = C["stat"], C["ident"], C["identf"]

    es_p = ExitStack()
    q2 = es_p.enter_context(nc.sbuf_tensor("q2", [128, T], BF16))
    k2 = es_p.enter_context(nc.sbuf_tensor("k2", [128, T], BF16))
    vtm = es_p.enter_context(nc.sbuf_tensor("vtm", [128, NT * 64], BF16))
    outtm = es_p.enter_context(nc.sbuf_tensor("outtm", [128, NT * 64], F32))

    es_a = ExitStack()
    xb = es_a.enter_context(nc.sbuf_tensor("xb", [128, 8 * 1024], BF16))
    xfm = es_a.enter_context(nc.sbuf_tensor("xfm", [128, 8 * 1024], BF16))
    vfm = es_a.enter_context(nc.sbuf_tensor("vfm", [64, T], BF16))
    pst = es_a.enter_context(nc.psum_tensor("pst", [128, 2048], BF16))
    psqk = es_a.enter_context(nc.psum_tensor("psqk", [128, 1024], F32))
    psv = es_a.enter_context(nc.psum_tensor("psv", [64, 1024], F32))

    for sl in range(4 if SUB2A >= 3 else 1):
        tc0 = sl * 8

        def cast_v(v, sl=sl, tc0=tc0):
            if sl == 0:
                cx.wait(v, "sd_s", C["xs_loaded"])
                cx.wait(v, "s_v", C["const_done"])
            for c in range(8):
                v.tensor_copy(xb[:, c * 1024:(c + 1) * 1024],
                              xs[:, (tc0 + c) * 1024:(tc0 + c + 1) * 1024])
            cx.inc("s_v", v.drain())
        blk.vector(cast_v)
        if SUB2A <= 0:
            break
        emit_tmfm_transpose(cx, xb, xfm, ident, pst, cx.val("s_v"))
        xfm_ready = cx.val("s_v")
        if SUB2A <= 1:
            break

        def proj_t(t, sl=sl, xfm_ready=xfm_ready):
            cx.wait(t, "s_v", xfm_ready)
            if sl == 0:
                cx.wait(t, "sd_g", C["w0ld"])
            for nb in range(2):
                for e in range(ND):
                    t.matmul(psqk[:, nb * 512:(nb + 1) * 512],
                             C["wqk"][:, e * 128:(e + 1) * 128],
                             xfm[:, e * 1024 + nb * 512: e * 1024 + (nb + 1) * 512],
                             start=(e == 0), stop=(e == ND - 1))
            cx.inc("s_t", t.drain())
            for nb in range(2):
                for e in range(ND):
                    t.matmul(psv[:, nb * 512:(nb + 1) * 512],
                             C["wv"][:, e * 64:(e + 1) * 64],
                             xfm[:, e * 1024 + nb * 512: e * 1024 + (nb + 1) * 512],
                             start=(e == 0), stop=(e == ND - 1))
            cx.inc("s_t", t.drain())
        blk.tensor(proj_t)
        qk_ready = cx.val("s_t") - 1
        v_ready = cx.val("s_t")

        def evac_v(v, sl=sl, qk_ready=qk_ready, v_ready=v_ready):
            cols = slice(sl * 1024, (sl + 1) * 1024)
            cx.wait(v, "s_t", qk_ready)
            v.tensor_copy(q2[0:64, cols], psqk[0:64, :])
            v.tensor_copy(q2[64:128, cols], psqk[0:64, :])
            v.tensor_copy(k2[0:64, cols], psqk[64:128, :])
            v.tensor_copy(k2[64:128, cols], psqk[64:128, :])
            cx.wait(v, "s_t", v_ready)
            v.tensor_copy(vfm[:, cols], psv[:, :])
            cx.inc("s_v", v.drain())
        blk.vector(evac_v)
    qkv_done = cx.val("s_v")
    if SUB2A <= 3:
        es_a.close(); es_p.close()
        return cx.val("s_v")
    if UPTO <= 2:
        es_a.close(); es_p.close()
        return cx.val("s_v")
    emit_dump(cx, dram, "q2", q2[:, :], [128, T], ("s_v", qkv_done))
    emit_dump(cx, dram, "k2", k2[:, :], [128, T], ("s_v", qkv_done))

    def vT_t(t):
        cx.wait(t, "s_v", qkv_done)
        for grp in range(8):
            if grp >= 2:
                t.wait_ge(cx.h("s_v"), qkv_done + grp - 1)
            for j in range(4):
                tc = grp * 4 + j
                t.transpose(pst[:, (grp % 2) * 1024 + j * 128:(grp % 2) * 1024 + j * 128 + 64],
                            vfm[:, tc * 128:(tc + 1) * 128], ident[0:64, 0:64])
            cx.inc("s_t", t.drain())
    blk.tensor(vT_t)
    vt0 = cx.val("s_t") - 8

    def vT_v(v):
        for grp in range(8):
            cx.wait(v, "s_t", vt0 + grp + 1)
            for j in range(4):
                tc = grp * 4 + j
                v.tensor_copy(vtm[:, tc * 64:(tc + 1) * 64],
                              pst[:, (grp % 2) * 1024 + j * 128:(grp % 2) * 1024 + j * 128 + 64])
            cx.inc("s_v", v.drain())
    blk.vector(vT_v)
    vtm_done = cx.val("s_v")
    emit_dump(cx, dram, "vtm", vtm[:, :], [128, NT * 64], ("s_v", vtm_done))
    es_a.close()

    es_b = ExitStack()
    EA = es_b.enter_context(nc.sbuf_tensor("EA", [128, T], BF16))
    EB = es_b.enter_context(nc.sbuf_tensor("EB", [128, T], BF16))
    vsc = es_b.enter_context(nc.sbuf_tensor("vsc", [128, 128], BF16))
    acc = es_b.enter_context(nc.sbuf_tensor("acc", [64, T], F32))

    es_ps = ExitStack()
    psA = es_ps.enter_context(nc.psum_tensor("psA", [128, 1024], F32))
    psB = es_ps.enter_context(nc.psum_tensor("psB", [128, 1024], F32))
    pso = es_ps.enter_context(nc.psum_tensor("pso", [64, 2048], F32))

    for kp in range(16):
        ca, cb = 2 * kp, 2 * kp + 1

        def qkmm(t, isp, ca=ca, cb=cb):
            for nb in range(2):
                t.matmul(psA[:, nb * 512:(nb + 1) * 512],
                         k2[0:64, ca * 128:(ca + 1) * 128],
                         q2[0:64, isp * 1024 + nb * 512: isp * 1024 + (nb + 1) * 512],
                         start=True, stop=True, tile_position=(0, 0))
                t.matmul(psB[:, nb * 512:(nb + 1) * 512],
                         k2[64:128, cb * 128:(cb + 1) * 128],
                         q2[64:128, isp * 1024 + nb * 512: isp * 1024 + (nb + 1) * 512],
                         start=True, stop=True, tile_position=(64, 0))

        last_e = cx.val("s_a")
        svA = cx.val("s_v")
        tA0 = cx.val("s_t")

        def pA_t(t, kp=kp, qkmm=qkmm, last_e=last_e, svA=svA):
            if kp == 0:
                cx.wait(t, "s_v", vtm_done)
            else:
                cx.wait(t, "s_a", last_e)
            for isp in range(4):
                if isp >= 1:
                    t.wait_ge(cx.h("s_v"), svA + isp)
                qkmm(t, isp)
                cx.inc("s_t", t.drain())
        blk.tensor(pA_t)

        def pA_v(v, kp=kp, tA0=tA0):
            for isp in range(4):
                cx.wait(v, "s_t", tA0 + isp + 1)
                v.tensor_reduce(stat[:, isp:isp + 1], psA[:, :], AX.X, ALU.max)
                v.tensor_reduce(stat[:, 4 + isp:5 + isp], psB[:, :], AX.X, ALU.max)
                cx.inc("s_v", v.drain())
            v.tensor_reduce(stat[:, 8:9], stat[:, 0:4], AX.X, ALU.max, negate=True)
            v.tensor_reduce(stat[:, 9:10], stat[:, 4:8], AX.X, ALU.max, negate=True)
            cx.inc("s_v", v.drain())
        blk.vector(pA_v)
        mA_done = cx.val("s_v")

        tB0 = cx.val("s_t")
        saB = cx.val("s_a")

        def pB_t(t, kp=kp, qkmm=qkmm, mA_done=mA_done, saB=saB):
            cx.wait(t, "s_v", mA_done)
            for isp in range(4):
                if isp >= 1:
                    t.wait_ge(cx.h("s_a"), saB + isp)
                qkmm(t, isp)
                cx.inc("s_t", t.drain())
        blk.tensor(pB_t)

        def pB_s(s, kp=kp, tB0=tB0):
            for isp in range(4):
                cx.wait(s, "s_t", tB0 + isp + 1)
                s.activation(EA[:, isp * 1024:(isp + 1) * 1024], psA[:, :],
                             AF.Exp, bias=stat[:, 8:9], scale=1.0,
                             accum_out=stat[:, 10 + isp:11 + isp])
                s.activation(EB[:, isp * 1024:(isp + 1) * 1024], psB[:, :],
                             AF.Exp, bias=stat[:, 9:10], scale=1.0,
                             accum_out=stat[:, 14 + isp:15 + isp])
                cx.inc("s_a", s.drain())
        blk.scalar(pB_s)
        e_done = cx.val("s_a")

        def lrec_v(v, kp=kp, ca=ca, e_done=e_done):
            cx.wait(v, "s_a", e_done)
            v.reduce_sum(stat[:, 18:19], stat[:, 10:14], AX.X)
            v.reduce_sum(stat[:, 19:20], stat[:, 14:18], AX.X)
            v.drain()
            v.reciprocal(stat[:, 20:21], stat[:, 18:19])
            v.reciprocal(stat[:, 21:22], stat[:, 19:20])
            v.drain()
            v.tensor_scalar(vsc[:, 0:64], vtm[:, ca * 64:(ca + 1) * 64],
                            stat[:, 20:21], None, ALU.mult)
            v.tensor_scalar(vsc[:, 64:128], vtm[:, (ca + 1) * 64:(ca + 2) * 64],
                            stat[:, 21:22], None, ALU.mult)
            cx.inc("s_v", v.drain())
        blk.vector(lrec_v)
        vsc_ready = cx.val("s_v")

        def pv_t(t, kp=kp, vsc_ready=vsc_ready):
            cx.wait(t, "s_v", vsc_ready)
            for ih in range(2):
                if ih == 1:
                    t.wait_ge(cx.h("s_v"), vsc_ready + 1)
                for nb in range(4):
                    t.matmul(pso[:, nb * 512:(nb + 1) * 512], vsc[:, 0:64],
                             EA[:, ih * 2048 + nb * 512: ih * 2048 + (nb + 1) * 512],
                             start=True, stop=False)
                    t.matmul(pso[:, nb * 512:(nb + 1) * 512], vsc[:, 64:128],
                             EB[:, ih * 2048 + nb * 512: ih * 2048 + (nb + 1) * 512],
                             start=False, stop=True)
                cx.inc("s_t", t.drain())
        blk.tensor(pv_t)
        pvh0 = cx.val("s_t") - 2

        def pvacc_v(v, kp=kp, pvh0=pvh0):
            for ih in range(2):
                cx.wait(v, "s_t", pvh0 + ih + 1)
                if kp == 0:
                    v.tensor_copy(acc[:, ih * 2048:(ih + 1) * 2048], pso[:, :])
                else:
                    v.tensor_add(acc[:, ih * 2048:(ih + 1) * 2048],
                                 acc[:, ih * 2048:(ih + 1) * 2048], pso[:, :])
                cx.inc("s_v", v.drain())
        blk.vector(pvacc_v)
    acc_done = cx.val("s_v")
    if UPTO <= 3:
        es_ps.close(); es_b.close(); es_p.close()
        return cx.val("s_v")
    emit_dump(cx, dram, "acc", acc[:, :], [64, T], ("s_v", acc_done))
    emit_dump(cx, dram, "EA15", EA[:, :], [128, T], ("s_v", acc_done))
    emit_dump(cx, dram, "stat22", stat[:, 0:22], [128, 22], ("s_v", acc_done))
    es_ps.close()

    with nc.psum_tensor("pst2", [128, 1024], F32) as pst2:
        def oT_t(t):
            cx.wait(t, "s_v", acc_done)
            for grp in range(8):
                if grp >= 2:
                    t.wait_ge(cx.h("s_v"), acc_done + grp - 1)
                for j in range(4):
                    tc = grp * 4 + j
                    t.transpose(pst2[:, (grp % 2) * 512 + j * 128:(grp % 2) * 512 + j * 128 + 64],
                                acc[:, tc * 128:(tc + 1) * 128], identf[0:64, 0:64])
                cx.inc("s_t", t.drain())
        blk.tensor(oT_t)
        ot0 = cx.val("s_t") - 8

        def oT_v(v):
            for grp in range(8):
                cx.wait(v, "s_t", ot0 + grp + 1)
                for j in range(4):
                    tc = grp * 4 + j
                    v.tensor_copy(outtm[:, tc * 64:(tc + 1) * 64],
                                  pst2[:, (grp % 2) * 512 + j * 128:(grp % 2) * 512 + j * 128 + 64])
                cx.inc("s_v", v.drain())
        blk.vector(oT_v)
    outtm_done = cx.val("s_v")
    emit_dump(cx, dram, "outtm", outtm[:, :], [128, NT * 64], ("s_v", outtm_done))
    es_b.close()

    def resid_v(v):
        cx.wait(v, "s_v", outtm_done)
        for tc in range(NT):
            v.tensor_add(xs[:, tc * 1024:(tc + 1) * 1024],
                         xs[:, tc * 1024:(tc + 1) * 1024],
                         AP(outtm, tc * 64, [[NT * 64, 128], [0, 16], [1, 64]]))
        cx.inc("s_v", v.drain())
    blk.vector(resid_v)
    emit_ln(cx, xs, stat, ("s_v", cx.val("s_v")))
    emit_dump(cx, dram, "x1", xs[:, :], [128, NT * 1024], ("s_v", cx.val("s_v")))
    es_p.close()
    return cx.val("s_v")


def phase3_l0_ffn(cx, C, dram, xs, ln0_done, safe_prefetch_sv):
    nc, blk = cx.nc, cx.blk
    stat = C["stat"]

    es = ExitStack()
    xqb = es.enter_context(nc.sbuf_tensor("xqb", [128, 8 * 1024], BF16))
    xqf = es.enter_context(nc.sbuf_tensor("xqf", [128, 8 * 1024], BF16))
    w0full = es.enter_context(nc.sbuf_tensor("w0full", [128, ND * 1024], BF16))
    ffo = es.enter_context(nc.sbuf_tensor("ffo", [128, 1024], F32))
    qsc = es.enter_context(nc.sbuf_tensor("qsc", [128, 1024], F32))
    pst3 = es.enter_context(nc.psum_tensor("pst3", [128, 2048], BF16))
    psf = es.enter_context(nc.psum_tensor("psf", [128, 2048], F32))

    def w0_g(g):
        cx.wait(g, "s_v", safe_prefetch_sv)
        if DBG:
            cx.wait(g, "sd_o2")
        for e in range(ND):
            cx.inc("sd_g", g.dma_start(
                out=w0full[:, e * 1024:(e + 1) * 1024],
                in_=dram["wint0"][e * 128:(e + 1) * 128, :]), 16)
    blk.gpsimd(w0_g)
    w0_loaded = cx.val("sd_g")

    for sl in range(4):
        tc0 = sl * 8

        def quant_v(v, sl=sl, tc0=tc0):
            if sl == 0:
                cx.wait(v, "s_v", ln0_done)
                cx.wait(v, "s_v", C["bcast_done"])
            if DBG:
                cx.wait(v, "sd_o2")
            for c in range(8):
                xc = xs[:, (tc0 + c) * 1024:(tc0 + c + 1) * 1024]
                v.scalar_tensor_tensor(xqb[:, c * 1024:(c + 1) * 1024], xc, 1.0, xc,
                                       ALU.mult, ALU.mult,
                                       accum_out=stat[:, 32 + c:33 + c])
                v.tensor_reduce(stat[:, 40 + c:41 + c], xc, AX.X, ALU.max,
                                apply_absolute_value=True)
            v.drain()
            v.tensor_scalar(stat[:, 48:56], stat[:, 32:40], 1.0 / D, 1e-6,
                            ALU.mult, ALU.add)
            v.drain()
            v.reciprocal(stat[:, 56:64], stat[:, 48:56])
            cx.inc("s_v", v.drain())
        blk.vector(quant_v)
        st_ready = cx.val("s_v")

        def rr_s(s, st_ready=st_ready):
            cx.wait(s, "s_v", st_ready)
            s.activation(stat[:, 64:72], stat[:, 56:64], AF.Sqrt)
            cx.inc("s_a", s.drain())
        blk.scalar(rr_s)
        rr_ready = cx.val("s_a")

        def quant2_v(v, sl=sl, tc0=tc0, rr_ready=rr_ready):
            cx.wait(v, "s_a", rr_ready)
            v.tensor_tensor(stat[:, 72:80], stat[:, 40:48], stat[:, 64:72], ALU.mult)
            v.drain()
            v.tensor_scalar_max(stat[:, 72:80], stat[:, 72:80], 1e-5)
            v.drain()
            v.reciprocal(stat[:, 80:88], stat[:, 72:80])
            v.drain()
            v.tensor_scalar_mul(stat[:, 80:88], stat[:, 80:88], 127.0)
            v.drain()
            v.tensor_tensor(stat[:, 88:96], stat[:, 64:72], stat[:, 80:88], ALU.mult)
            v.drain()
            for c in range(8):
                xc = xs[:, (tc0 + c) * 1024:(tc0 + c + 1) * 1024]
                v.tensor_scalar(qsc[:, :], xc,
                                stat[:, 88 + c:89 + c], MAGIC, ALU.mult, ALU.add)
                v.drain()
                v.tensor_scalar(xqb[:, c * 1024:(c + 1) * 1024], qsc[:, :],
                                MAGIC, 127.0, ALU.subtract, ALU.min)
                v.drain()
                v.tensor_scalar_max(xqb[:, c * 1024:(c + 1) * 1024],
                                    xqb[:, c * 1024:(c + 1) * 1024], -128.0)
                v.drain()
            v.reciprocal(stat[:, 96:104], stat[:, 80:88])
            v.drain()
            v.tensor_scalar(stat[:, 96:104], stat[:, 96:104],
                            C["scal"][:, 0:1], None, ALU.mult)
            cx.inc("s_v", v.drain())
        blk.vector(quant2_v)
        if sl == 0:
            emit_dump(cx, dram, "xqb0", xqb[:, :], [128, 8 * 1024], ("s_v", cx.val("s_v")))
            emit_dump(cx, dram, "qstat0", stat[:, 32:104], [128, 72], ("s_v", cx.val("s_v")))
        emit_tmfm_transpose(cx, xqb, xqf, C["ident"], pst3, cx.val("s_v"))
        xqf_ready = cx.val("s_v")

        for c in range(8):
            tc = tc0 + c
            half = c % 2
            prev_resid = cx.val("s_v")

            def mm_t(t, sl=sl, c=c, half=half, xqf_ready=xqf_ready,
                     prev_resid=prev_resid):
                if c == 0:
                    cx.wait(t, "s_v", xqf_ready)
                    if sl == 0:
                        cx.wait(t, "sd_g", w0_loaded)
                if c >= 2:
                    cx.wait(t, "s_v", prev_resid)
                for nb in range(2):
                    for e in range(ND):
                        t.matmul(psf[:, half * 1024 + nb * 512: half * 1024 + (nb + 1) * 512],
                                 xqf[:, e * 1024 + c * 128: e * 1024 + (c + 1) * 128],
                                 w0full[:, e * 1024 + nb * 512: e * 1024 + (nb + 1) * 512],
                                 start=(e == 0), stop=(e == ND - 1))
                cx.inc("s_t", t.drain())
            blk.tensor(mm_t)
            mm_done = cx.val("s_t")

            def epi_v(v, c=c, tc=tc, half=half, mm_done=mm_done):
                cx.wait(v, "s_t", mm_done)
                v.scalar_tensor_tensor(ffo[:, :], psf[:, half * 1024:(half + 1) * 1024],
                                       stat[:, 96 + c:97 + c], C["b0row"][:, :],
                                       ALU.mult, ALU.add)
                v.drain()
                v.tensor_add(xs[:, tc * 1024:(tc + 1) * 1024],
                             xs[:, tc * 1024:(tc + 1) * 1024], ffo[:, :])
                cx.inc("s_v", v.drain())
            blk.vector(epi_v)

    emit_ln(cx, xs, stat, ("s_v", cx.val("s_v")))
    es.close()
    return cx.val("s_v")



def build_program():
    nc = bass.Bass("TRN2", target_bir_lowering=False)

    def DI(name, shape, dt=F32):
        return nc.dram_tensor(name, list(shape), dt, kind="ExternalInput")

    dram = dict(
        x0=DI("x0", (T, D)),
        wqk0=DI("wqk0", (D, 128)),
        wv0=DI("wv0", (D, 64)),
        wint0=DI("wint0", (D, D), I32),
        ws0=DI("ws0", (1, 1)),
        bias0=DI("bias0", (1, D)),
        wqkv=DI("wqkv", (D, 3072)),
        bqkv=DI("bqkv", (1, 3072)),
        wg1=DI("wg1", (D, D)),
        bg1=DI("bg1", (1, D)),
        relpos_t=DI("relpos_t", (16, 512)),
        w1=DI("w1", (D, 4096)),
        b1=DI("b1", (1, 4096)),
        w2=DI("w2", (4096, 16)),
        b2=DI("b2", (1, 16)),
        wo=DI("wo", (D, D)),
        bo=DI("bo", (1, D)),
        clips=DI("clips", (1, 2)),
        wgn=DI("wgn", (2816, D)),
        wun=DI("wun", (2816, D)),
        wdn=DI("wdn", (D, 2816)),
    )
    out = nc.dram_tensor("out", [T, D], F32, kind="ExternalOutput")
    x1s = nc.dram_tensor("x1s", [T, D], F32)
    x2s = nc.dram_tensor("x2s", [T, D], F32)
    hs_scr = nc.dram_tensor("hs_scr", [2816, T], BF16)
    wgq = nc.dram_tensor("wgq", [2816, D], BF16)
    wuq = nc.dram_tensor("wuq", [2816, D], BF16)
    wdq = nc.dram_tensor("wdq", [D, 2816], BF16)
    wgT = nc.dram_tensor("wgT", [D, 2816], BF16)
    wuT = nc.dram_tensor("wuT", [D, 2816], BF16)
    wdT = nc.dram_tensor("wdT", [2816, D], BF16)
    cc1_in = nc.dram_tensor("cc1_in", [1, 16], F32)
    cc1_out = nc.dram_tensor("cc1_out", [1, 16], F32)
    cc2_in = nc.dram_tensor("cc2_in", [1, 16], F32)
    cc2_out = nc.dram_tensor("cc2_out", [1, 16], F32)
    q_scr = nc.dram_tensor("q_scr", [D, T], BF16)
    g_scr = nc.dram_tensor("g_scr", [D, T], BF16)
    k_scr = nc.dram_tensor("k_scr", [D, T], BF16)
    rtil_scr = nc.dram_tensor("rtil_scr", [16, 1792], BF16)
    scr = dict(x1s=x1s, x2s=x2s, hs=hs_scr, wgq=wgq, wuq=wuq, wdq=wdq,
               wgT=wgT, wuT=wuT, wdT=wdT, cc1_in=cc1_in, cc1_out=cc1_out,
               cc2_in=cc2_in, cc2_out=cc2_out, out=out,
               q_scr=q_scr, g_scr=g_scr, k_scr=k_scr, rtil=rtil_scr)

    es_top = ExitStack()
    with es_top:
        blk = es_top.enter_context(nc.Block())
        cx = Ctx(nc, blk)
        for s in ["sd_g", "sd_s", "sd_o", "sd_o2", "s_t", "s_v", "s_a", "s_g",
                  "sq0", "sq1", "st0", "st1", "gx0", "gx1", "gw0", "gw1"]:
            cx.sem(s)

        sb = lambda n, sh, dt: es_top.enter_context(nc.sbuf_tensor(n, sh, dt))
        C = dict(
            ident=sb("ident", [128, 128], BF16),
            identJ=sb("identJ", [128, 128], BF16),
            identf=sb("identf", [128, 128], F32),
            onesb=sb("onesb", [1, 128], BF16),
            onesf=sb("onesf2", [1, 128], F32),
            scal=sb("scal", [128, 16], F32),
            stat=sb("stat", [128, 192], F32),
        )
        es_l0 = ExitStack()
        C["b0row"] = es_l0.enter_context(nc.sbuf_tensor("b0row", [128, D], F32))
        C["wqk"] = es_l0.enter_context(nc.sbuf_tensor("wqk", [128, ND * 128], BF16))
        C["wv"] = es_l0.enter_context(nc.sbuf_tensor("wv", [128, ND * 64], BF16))

        es0 = phase0_setup(cx, C, dram)
        es0.close()

        es_xs = ExitStack()
        xs = es_xs.enter_context(nc.sbuf_tensor("xs", [128, NT * 1024], F32))

        def p1_s(s):
            cx.wait(s, "s_v", C["const_done"])
            cx.wait(s, "s_v", C["bcast_done"])
            for i in range(8):
                c0 = i * 4
                cx.inc("sd_s", s.dma_start(
                    out=xs[:, c0 * 1024:(c0 + 4) * 1024],
                    in_=AP(dram["x0"], c0 * 128 * D, [[D, 128], [128 * D, 4], [1, D]])), 16)
        blk.sync(p1_s)
        C["xs_loaded"] = cx.val("sd_s")

        if UPTO >= 2:
            ln0_done = phase2_l0_attn(cx, C, dram, xs)
        else:
            ln0_done = cx.val("s_v")
        if UPTO >= 5:
            ln1_done = phase3_l0_ffn(cx, C, dram, xs, ln0_done,
                                     safe_prefetch_sv=ln0_done - 2)
        else:
            ln1_done = ln0_done

        def pout_s(s):
            cx.wait(s, "s_v", ln1_done)
            for i in range(8):
                c0 = i * 4
                cx.inc("sd_o", s.dma_start(
                    out=AP(x1s, c0 * 128 * D, [[D, 128], [128 * D, 4], [1, D]]),
                    in_=xs[:, c0 * 1024:(c0 + 4) * 1024]), 16)
                if UPTO < 6:
                    cx.inc("sd_o", s.dma_start(
                        out=AP(out, c0 * 128 * D, [[D, 128], [128 * D, 4], [1, D]]),
                        in_=xs[:, c0 * 1024:(c0 + 4) * 1024]), 16)
            cx.wait(s, "sd_o")
        blk.sync(pout_s)
        x1s_done = cx.val("sd_o")

        if DBG:
            def dbg_fin(g):
                cx.wait(g, "sd_o2")
            blk.gpsimd(dbg_fin)

        es_xs.close()
        es_l0.close()

        if UPTO >= 6:
            wq_done = phase_wprep(cx, C, dram, scr)
            l1 = phase4_l1a(cx, C, dram, scr, x1s_done)
            if UPTO >= 7:
                x2 = phase5_l1b(cx, C, dram, scr, l1, wq_done,
                                write_out_dbg=(UPTO == 7))
                if UPTO >= 8:
                    l1["es"].close()
                    phase6_l1ffn(cx, C, dram, scr, l1, x2, wq_done)
                else:
                    l1["es"].close()

                    def fin_o(s):
                        cx.wait(s, "sd_o")
                        cx.wait(s, "sd_s")
                        cx.wait(s, "sd_g")
                    blk.sync(fin_o)
            else:
                l1["es"].close()

                def fin_o(s):
                    cx.wait(s, "sd_o")
                    cx.wait(s, "sd_s")
                    cx.wait(s, "sd_g")
                blk.sync(fin_o)
    return nc


def kernel(L0_attn_x, L0_attn_Wq, L0_attn_Wk, L0_attn_Wv, L0_ffn_w_int8,
           L0_ffn_weight_scale, L0_ffn_bias, L1_attn_Wqkv, L1_attn_bqkv,
           L1_attn_Wg, L1_attn_bg, L1_attn_relpos_emb, L1_attn_W1, L1_attn_b1,
           L1_attn_W2, L1_attn_b2, L1_attn_Wo, L1_attn_bo, L1_ffn_clip_x,
           L1_ffn_clip_down, L1_ffn_w_gate, L1_ffn_w_up, L1_ffn_w_down):
    a = lambda x: np.ascontiguousarray(np.asarray(x))
    x = a(L0_attn_x)

    shared = dict(
        wqk0=a(np.concatenate([np.asarray(L0_attn_Wq).T, np.asarray(L0_attn_Wk).T], axis=1)),
        wv0=a(np.asarray(L0_attn_Wv).T),
        wint0=a(np.asarray(L0_ffn_w_int8).T),
        ws0=a(np.asarray(L0_ffn_weight_scale, dtype=np.float32).reshape(1, 1)),
        bias0=a(np.asarray(L0_ffn_bias).reshape(1, D)),
        wqkv=a(L1_attn_Wqkv),
        bqkv=a(np.asarray(L1_attn_bqkv).reshape(1, 3072)),
        wg1=a(L1_attn_Wg),
        bg1=a(np.asarray(L1_attn_bg).reshape(1, D)),
        relpos_t=a(np.asarray(L1_attn_relpos_emb).T),
        w1=a(L1_attn_W1),
        b1=a(np.asarray(L1_attn_b1).reshape(1, 4096)),
        w2=a(L1_attn_W2),
        b2=a(np.asarray(L1_attn_b2).reshape(1, 16)),
        wo=a(L1_attn_Wo),
        bo=a(np.asarray(L1_attn_bo).reshape(1, D)),
        clips=a(np.concatenate([np.asarray(L1_ffn_clip_x).reshape(1),
                                np.asarray(L1_ffn_clip_down).reshape(1)]).reshape(1, 2)),
        wgn=a(L1_ffn_w_gate),
        wun=a(L1_ffn_w_up),
        wdn=a(L1_ffn_w_down),
    )
    nc = build_program()
    in_maps = [dict(x0=a(x[i]), **shared) for i in range(N_CORES)]
    res = run_bass_kernel_spmd(nc, in_maps, core_ids=list(range(N_CORES)))
    return np.stack([np.asarray(r["out"]) for r in res.results], axis=0)




def phase_wprep(cx, C, dram, scr):
    nc, blk = cx.nc, cx.blk
    es = ExitStack()
    wnat = es.enter_context(nc.sbuf_tensor("wnat", [128, 2 * 2816], F32))
    wtmp = es.enter_context(nc.sbuf_tensor("wtmp", [128, 2816], F32))
    wdeq = es.enter_context(nc.sbuf_tensor("wdeq", [128, 2816], BF16))
    wst = es.enter_context(nc.sbuf_tensor("wst", [128, 4], F32))
    tbuf = es.enter_context(nc.sbuf_tensor("tbuf", [128, 2816], BF16))

    specs = [(dram["wgn"], scr["wgq"], scr["wgT"], 2816, D),
             (dram["wun"], scr["wuq"], scr["wuT"], 2816, D),
             (dram["wdn"], scr["wdq"], scr["wdT"], D, 2816)]

    q_marks = []
    st_marks = []
    ci = 0
    for nat, qn, qt, R, Dc in specs:
        for ch in range(R // 128):
            buf = ci % 2

            def ld_s(s, nat=nat, ch=ch, buf=buf, Dc=Dc, ci=ci):
                if ci >= 2:
                    s.wait_ge(cx.h("s_v"), q_marks[ci - 2])
                cx.inc("sq%d" % buf, s.dma_start(
                    out=wnat[:, buf * 2816:buf * 2816 + Dc],
                    in_=nat[ch * 128:(ch + 1) * 128, :]), 16)
            blk.sync(ld_s)
            ld_done = cx.val("sq%d" % buf)

            def qv(v, buf=buf, Dc=Dc, ld_done=ld_done, ci=ci):
                cx.wait(v, "sq%d" % buf, ld_done)
                if ci >= 1:
                    cx.wait(v, "sd_g", st_marks[ci - 1])
                w = wnat[:, buf * 2816:buf * 2816 + Dc]
                v.tensor_reduce(wst[:, 0:1], w, AX.X, ALU.max,
                                apply_absolute_value=True)
                v.drain()
                v.reciprocal(wst[:, 1:2], wst[:, 0:1])
                v.drain()
                v.tensor_scalar_mul(wst[:, 1:2], wst[:, 1:2], 127.0)
                v.tensor_scalar_mul(wst[:, 2:3], wst[:, 0:1], 1.0 / 127.0)
                v.drain()
                v.tensor_scalar(wtmp[:, 0:Dc], w, wst[:, 1:2], MAGIC,
                                ALU.mult, ALU.add)
                v.drain()
                v.tensor_scalar(wtmp[:, 0:Dc], wtmp[:, 0:Dc], MAGIC, 127.0,
                                ALU.subtract, ALU.min)
                v.drain()
                v.tensor_scalar(wdeq[:, 0:Dc], wtmp[:, 0:Dc], -128.0,
                                wst[:, 2:3], ALU.max, ALU.mult)
                cx.inc("s_v", v.drain())
            blk.vector(qv)
            q_marks.append(cx.val("s_v"))

            def st_g(g, qn=qn, ch=ch, Dc=Dc, qd=q_marks[-1]):
                cx.wait(g, "s_v", qd)
                cx.inc("sd_g", g.dma_start(
                    out=qn[ch * 128:(ch + 1) * 128, :], in_=wdeq[:, 0:Dc]), 16)
            blk.gpsimd(st_g)
            st_marks.append(cx.val("sd_g"))
            ci += 1
    qn_all = cx.val("sd_g")

    tb_marks = []
    for nat, qn, qt, R, Dc in specs:
        for j in range(Dc // 128):
            def trs(s, qn=qn, j=j, R=R, first=(len(tb_marks) == 0)):
                if first:
                    cx.wait(s, "sd_g", qn_all)
                else:
                    cx.wait(s, "sd_g", tb_marks[-1])
                cx.inc("sd_s", s.dma_start_transpose(
                    tbuf[:, 0:R], qn[:, j * 128:(j + 1) * 128]), 16)
            blk.sync(trs)
            tr_done = cx.val("sd_s")

            def tst(g, qt=qt, j=j, R=R, tr_done=tr_done):
                cx.wait(g, "sd_s", tr_done)
                cx.inc("sd_g", g.dma_start(
                    out=qt[j * 128:(j + 1) * 128, :], in_=tbuf[:, 0:R]), 16)
            blk.gpsimd(tst)
            tb_marks.append(cx.val("sd_g"))
    wq_done_g = cx.val("sd_g")

    def wq_fin(g):
        cx.wait(g, "sd_g", wq_done_g)
    blk.gpsimd(wq_fin)
    es.close()
    return wq_done_g


def phase4_l1a(cx, C, dram, scr, x1s_done):
    nc, blk = cx.nc, cx.blk
    stat, ident = C["stat"], C["ident"]
    VW = 1040

    es_l1 = ExitStack()
    vtm = es_l1.enter_context(nc.sbuf_tensor("vtm65", [128, NT * VW], BF16))
    edgeb = es_l1.enter_context(nc.sbuf_tensor("edgeb", [16, T], BF16))
    rtil = es_l1.enter_context(nc.sbuf_tensor("rtil", [16, 1792], BF16))
    relsb = es_l1.enter_context(nc.sbuf_tensor("relsb", [16, 512], F32))
    l1st = es_l1.enter_context(nc.sbuf_tensor("l1st", [16, 16], F32))
    qkrow = es_l1.enter_context(nc.sbuf_tensor("qkrow", [1, 16], F32))
    bcol = es_l1.enter_context(nc.sbuf_tensor("bcol", [128, 64], F32))
    kpre = es_l1.enter_context(nc.sbuf_tensor("kpre", [128, 8 * 128], BF16))

    es_a = ExitStack()
    browb = es_a.enter_context(nc.sbuf_tensor("browb", [1, 2048], BF16))
    brow = es_a.enter_context(nc.sbuf_tensor("brow2", [1, 1024], F32))
    wbuf = es_a.enter_context(nc.sbuf_tensor("wbuf", [128, 2 * 4096], BF16))
    xbt = es_a.enter_context(nc.sbuf_tensor("xbt", [128, 2 * 1024], BF16))
    xfm = es_a.enter_context(nc.sbuf_tensor("xfm4", [128, 8 * 1024], BF16))
    st2 = es_a.enter_context(nc.sbuf_tensor("st2", [128, 2 * 1024], BF16))
    egs = es_a.enter_context(nc.sbuf_tensor("egs", [128, 2 * 1024], BF16))
    w2sb = es_a.enter_context(nc.sbuf_tensor("w2sb", [128, 32 * 16], BF16))
    pst = es_a.enter_context(nc.psum_tensor("pst4", [128, 2048], BF16))
    psp = es_a.enter_context(nc.psum_tensor("psp", [128, 2048], F32))
    psE = es_a.enter_context(nc.psum_tensor("psE", [128, 1024], F32))

    def s4_g(g):
        cx.wait(g, "sd_o", x1s_done)
        cx.inc("sd_g", g.dma_start(out=browb[:, 0:1024],
                                   in_=dram["bqkv"][:, 2048:3072]), 16)
        cx.inc("sd_g", g.dma_start(out=relsb[:, :], in_=dram["relpos_t"][:, :]), 16)
        cx.inc("sd_g", g.dma_start(
            out=AP(w2sb, 0, [[w2sb.shape[1], 128], [16, 32], [1, 16]]),
            in_=AP(dram["w2"], 0, [[16, 128], [128 * 16, 32], [1, 16]])), 16)
        cx.inc("sd_g", g.dma_start(
            out=AP(l1st, 0, [[16, 16], [1, 1]]), in_=dram["b2"][:, :]), 16)
        cx.inc("sd_g", g.dma_start(out=browb[:, 1024:2048], in_=dram["bo"][:, :]), 16)
    blk.gpsimd(s4_g)
    s4_ld = cx.val("sd_g")

    def s4_v(v):
        cx.wait(v, "sd_g", s4_ld)
        v.tensor_reduce(l1st[:, 1:2], relsb[:, :], AX.X, ALU.max)
        v.tensor_scalar(rtil[:, 0:638], AP(relsb, 0, [[512, 16], [0, 638]]),
                        1.0, None, ALU.mult)
        v.tensor_copy(rtil[:, 638:1149], relsb[:, 0:511])
        v.tensor_scalar(rtil[:, 1149:1792], AP(relsb, 510, [[512, 16], [0, 643]]),
                        1.0, None, ALU.mult)
        for tc in range(NT):
            v.memset(AP(vtm, tc * VW + 64, [[vtm.shape[1], 128], [65, 16], [1, 1]]), 1.0)
        cx.inc("s_v", v.drain())
    blk.vector(s4_v)
    s4_prep = cx.val("s_v")

    def rt_g(g):
        cx.wait(g, "s_v", s4_prep)
        cx.inc("sd_g", g.dma_start(out=scr["rtil"][:, :], in_=rtil[:, :]), 16)
    blk.gpsimd(rt_g)

    rowsrc = [(dram["bqkv"], 0, 0), (dram["bqkv"], 1024, 8), (dram["bg1"], 0, 16),
              (dram["w1"], -1, 24), (dram["w1"], -1, 28), (dram["w1"], -1, 32),
              (dram["w1"], -1, 36)]
    rowsrc = [(dram["bqkv"], 0, 0), (dram["bqkv"], 1024, 8), (dram["bg1"], 0, 16),
              (dram["b1"], 0, 24), (dram["b1"], 1024, 32), (dram["b1"], 2048, 40),
              (dram["b1"], 3072, 48)]
    prev_tr = None
    sv_br0 = cx.val("s_v")
    for gi, (ten, c0, b0) in enumerate(rowsrc):
        def br_g(g, ten=ten, c0=c0, prev_tr=prev_tr):
            if prev_tr is not None:
                g.wait_ge(cx.h("s_t"), prev_tr)
            else:
                cx.wait(g, "s_v", s4_prep)
            cx.inc("sd_g", g.dma_start(out=brow[:, :], in_=ten[0:1, c0:c0 + 1024]), 16)
        blk.gpsimd(br_g)
        br_ld = cx.val("sd_g")

        def br_t(t, br_ld=br_ld, gi=gi):
            cx.wait(t, "sd_g", br_ld)
            for i in range(8):
                n = gi * 8 + i
                if n >= 2:
                    t.wait_ge(cx.h("s_v"), sv_br0 + n - 1)
                t.transpose(psp[:, (i % 2) * 1024:(i % 2) * 1024 + 1],
                            brow[0:1, i * 128:(i + 1) * 128], C["onesf"][0:1, 0:1])
                cx.inc("s_t", t.drain())
        blk.tensor(br_t)
        tr0 = cx.val("s_t") - 8
        prev_tr = cx.val("s_t")

        def br_v(v, b0=b0, tr0=tr0):
            for i in range(8):
                cx.wait(v, "s_t", tr0 + i + 1)
                v.tensor_copy(bcol[:, b0 + i:b0 + i + 1],
                              psp[:, (i % 2) * 1024:(i % 2) * 1024 + 1])
                cx.inc("s_v", v.drain())
        blk.vector(br_v)

    def s4_v2(v):
        v.tensor_scalar_mul(bcol[:, 0:8], bcol[:, 0:8], 0.125)
        cx.inc("s_v", v.drain())
    blk.vector(s4_v2)
    bcol_done = cx.val("s_v")

    prev_edge_mark = None
    prev_g_store_marks = None
    for sl in range(4):
        tc0 = sl * 8

        st_base = cx.val("s_t")
        sv_base = cx.val("s_v")

        xld_marks = []

        def x_g(g, tc0=tc0, st_base=st_base):
            for c in range(8):
                if c >= 2:
                    g.wait_ge(cx.h("s_t"), st_base + (c - 1) * 2)
                xld_marks.append(cx.inc("gx%d" % (c % 2), g.dma_start(
                    out=xbt[:, (c % 2) * 1024:(c % 2 + 1) * 1024],
                    in_=scr["x1s"][(tc0 + c) * 128:(tc0 + c + 1) * 128, :]), 16))
        blk.gpsimd(x_g)

        def x_t(t, sl=sl, sv_base=sv_base, xld_marks=xld_marks):
            if sl == 0:
                cx.wait(t, "s_v", bcol_done)
            for c in range(8):
                t.wait_ge(cx.h("gx%d" % (c % 2)), xld_marks[c])
                for grp in range(2):
                    gi = c * 2 + grp
                    if gi >= 2:
                        t.wait_ge(cx.h("s_v"), sv_base + gi - 1)
                    for j in range(4):
                        dc = grp * 4 + j
                        slot = (gi % 2) * 8 + j
                        t.transpose(pst[:, slot * 128:(slot + 1) * 128],
                                    xbt[:, (c % 2) * 1024 + dc * 128:(c % 2) * 1024 + (dc + 1) * 128],
                                    ident[:, :])
                    cx.inc("s_t", t.drain())
        blk.tensor(x_t)

        def x_v(v, st_base=st_base):
            for gi in range(16):
                c, grp = gi // 2, gi % 2
                cx.wait(v, "s_t", st_base + gi + 1)
                for j in range(4):
                    dc = grp * 4 + j
                    slot = (gi % 2) * 8 + j
                    v.tensor_copy(xfm[:, dc * 1024 + c * 128: dc * 1024 + (c + 1) * 128],
                                  pst[:, slot * 128:(slot + 1) * 128])
                cx.inc("s_v", v.drain())
        blk.vector(x_v)
        xfm_ready = cx.val("s_v")

        wseq = [(dram["wqkv"], 1024), (dram["wqkv"], 1536),
                (dram["wqkv"], 2048), (dram["wqkv"], 2560),
                (dram["wqkv"], 0), (dram["wqkv"], 512),
                (dram["wg1"], 0), (dram["wg1"], 512)]
        incs_per_half = [4, 4, 8, 8, 9, 9, 4, 4] if sl == 0 else [4, 4, 8, 8, 4, 4, 4, 4]
        if sl == 0:
            incs_per_half = [4, 4, 8, 8, 4 + 4 * 3, 4 + 4 * 3, 4, 4]
        proj_st0 = cx.val("s_t")
        half_marks = [proj_st0 + sum(incs_per_half[:i + 1]) for i in range(8)]

        wld_marks = []

        def w_g(g, wseq=wseq, half_marks=half_marks):
            for i, (ten, c0) in enumerate(wseq):
                if i >= 2:
                    g.wait_ge(cx.h("s_t"), half_marks[i - 2])
                wld_marks.append(cx.inc("gw%d" % (i % 2), g.dma_start(
                    out=AP(wbuf, (i % 2) * 4096, [[wbuf.shape[1], 128], [512, 8], [1, 512]]),
                    in_=AP(ten, c0, [[ten.shape[1], 128], [128 * ten.shape[1], 8], [1, 512]])), 16))
        blk.gpsimd(w_g)

        sa_k = cx.val("s_a")

        def k_t(t, xfm_ready=xfm_ready, sa_k=sa_k, wld_marks=wld_marks):
            cx.wait(t, "s_v", xfm_ready)
            for dc in range(8):
                half = dc // 4
                if dc % 4 == 0:
                    t.wait_ge(cx.h("gw%d" % (half % 2)), wld_marks[half])
                if dc >= 2:
                    t.wait_ge(cx.h("s_a"), sa_k + dc - 1)
                for nb in range(2):
                    for e in range(ND):
                        t.matmul(psp[:, (dc % 2) * 1024 + nb * 512:(dc % 2) * 1024 + (nb + 1) * 512],
                                 wbuf[:, half * 4096 + e * 512 + (dc % 4) * 128: half * 4096 + e * 512 + (dc % 4) * 128 + 128],
                                 xfm[:, e * 1024 + nb * 512: e * 1024 + (nb + 1) * 512],
                                 start=(e == 0), stop=(e == ND - 1))
                cx.inc("s_t", t.drain())
        blk.tensor(k_t)
        kst0 = cx.val("s_t") - 8

        k_store_marks = []
        k_marks_a = []
        for dc in range(8):
            def k_a(s, sl=sl, dc=dc, kst0=kst0, pgs=prev_g_store_marks):
                cx.wait(s, "s_t", kst0 + dc + 1)
                if dc >= 2:
                    cx.wait(s, "st%d" % (dc % 2), k_store_marks[dc - 2])
                elif pgs is not None:
                    cx.wait(s, "st%d" % (dc % 2), pgs[6 + dc])
                s.activation(st2[:, (dc % 2) * 1024:(dc % 2 + 1) * 1024],
                             psp[:, (dc % 2) * 1024:(dc % 2 + 1) * 1024],
                             AF.Identity, bias=bcol[:, 8 + dc:9 + dc], scale=1.0)
                if sl == 0:
                    s.drain()
                    s.copy(kpre[:, dc * 128:(dc + 1) * 128],
                           st2[:, (dc % 2) * 1024:(dc % 2) * 1024 + 128])
                cx.inc("s_a", s.drain())
            blk.scalar(k_a)
            k_marks_a.append(cx.val("s_a"))

            def k_s(s, sl=sl, dc=dc, ka=k_marks_a[-1]):
                cx.wait(s, "s_a", ka)
                cx.inc("st%d" % (dc % 2), s.dma_start(
                    out=scr["k_scr"][dc * 128:(dc + 1) * 128, sl * 1024:(sl + 1) * 1024],
                    in_=st2[:, (dc % 2) * 1024:(dc % 2 + 1) * 1024]), 16)
            blk.sync(k_s)
            k_store_marks.append(cx.val("st%d" % (dc % 2)))
        k_done_a = cx.val("s_a")

        sv_v = cx.val("s_v")

        def v_t(t, sv_v=sv_v, k_done_a=k_done_a, wld_marks=wld_marks):
            cx.wait(t, "s_a", k_done_a)
            for vdb in range(2):
                t.wait_ge(cx.h("gw%d" % ((vdb + 2) % 2)), wld_marks[vdb + 2])
                for tc in range(8):
                    i = vdb * 8 + tc
                    if i >= 2:
                        t.wait_ge(cx.h("s_v"), sv_v + i - 1)
                    t.matmul(psp[:, (i % 2) * 1024:(i % 2) * 1024 + 512],
                             C["onesb"][0:1, 0:128],
                             browb[0:1, vdb * 512:(vdb + 1) * 512],
                             start=True, stop=False)
                    for e in range(ND):
                        t.matmul(psp[:, (i % 2) * 1024:(i % 2) * 1024 + 512],
                                 xfm[:, e * 1024 + tc * 128: e * 1024 + (tc + 1) * 128],
                                 wbuf[:, (vdb % 2) * 4096 + e * 512: (vdb % 2) * 4096 + (e + 1) * 512],
                                 start=False, stop=(e == ND - 1))
                    cx.inc("s_t", t.drain())
        blk.tensor(v_t)
        vst0 = cx.val("s_t") - 16

        def v_v(v, tc0=tc0, vst0=vst0):
            for vdb in range(2):
                for tc in range(8):
                    i = vdb * 8 + tc
                    cx.wait(v, "s_t", vst0 + i + 1)
                    v.tensor_copy(
                        AP(vtm, (tc0 + tc) * VW + vdb * 8 * 65,
                           [[vtm.shape[1], 128], [65, 8], [1, 64]]),
                        AP(psp, (i % 2) * 1024, [[psp.shape[1], 128], [64, 8], [1, 64]]))
                    cx.inc("s_v", v.drain())
        blk.vector(v_v)
        v_done_v = cx.val("s_v")

        q_marks_a = []
        q_store_marks = []
        pre_marks = []
        first_q = True
        for dc in range(8):
            half4 = 4 + dc // 4

            def q_t(t, dc=dc, half4=half4, first_q=first_q,
                    v_done_v=v_done_v, wld_marks=wld_marks):
                if first_q:
                    cx.wait(t, "s_v", v_done_v)
                if dc % 4 == 0:
                    t.wait_ge(cx.h("gw%d" % (half4 % 2)), wld_marks[half4])
                if dc >= 2:
                    cx.wait(t, "s_a", q_marks_a[dc - 2])
                for nb in range(2):
                    for e in range(ND):
                        t.matmul(psp[:, (dc % 2) * 1024 + nb * 512:(dc % 2) * 1024 + (nb + 1) * 512],
                                 wbuf[:, (half4 % 2) * 4096 + e * 512 + (dc % 4) * 128: (half4 % 2) * 4096 + e * 512 + (dc % 4) * 128 + 128],
                                 xfm[:, e * 1024 + nb * 512: e * 1024 + (nb + 1) * 512],
                                 start=(e == 0), stop=(e == ND - 1))
                cx.inc("s_t", t.drain())
            blk.tensor(q_t)
            first_q = False
            q_mm = cx.val("s_t")

            def q_a(s, sl=sl, dc=dc, q_mm=q_mm):
                cx.wait(s, "s_t", q_mm)
                if dc < 2:
                    cx.wait(s, "st%d" % (dc % 2), k_store_marks[6 + dc])
                else:
                    cx.wait(s, "st%d" % (dc % 2), q_store_marks[dc - 2])
                    if sl == 0:
                        cx.wait(s, "s_t", pre_marks[dc - 2])
                s.activation(st2[:, (dc % 2) * 1024:(dc % 2 + 1) * 1024],
                             psp[:, (dc % 2) * 1024:(dc % 2 + 1) * 1024],
                             AF.Identity, bias=bcol[:, dc:dc + 1], scale=0.125)
                cx.inc("s_a", s.drain())
            blk.scalar(q_a)
            q_marks_a.append(cx.val("s_a"))

            def q_s(s, sl=sl, dc=dc, qa=q_marks_a[-1]):
                cx.wait(s, "s_a", qa)
                cx.inc("st%d" % (dc % 2), s.dma_start(
                    out=scr["q_scr"][dc * 128:(dc + 1) * 128, sl * 1024:(sl + 1) * 1024],
                    in_=st2[:, (dc % 2) * 1024:(dc % 2 + 1) * 1024]), 16)
            blk.sync(q_s)
            q_store_marks.append(cx.val("st%d" % (dc % 2)))

            if sl == 0:
                def pre_t(t, dc=dc, qa=q_marks_a[-1]):
                    cx.wait(t, "s_a", qa)
                    t.matmul(psE[:, 0:512], kpre[0:64, dc * 128:(dc + 1) * 128],
                             st2[0:64, (dc % 2) * 1024:(dc % 2) * 1024 + 512],
                             start=True, stop=True, tile_position=(0, 0))
                    t.matmul(psE[:, 512:1024], kpre[64:128, dc * 128:(dc + 1) * 128],
                             st2[64:128, (dc % 2) * 1024:(dc % 2) * 1024 + 512],
                             start=True, stop=True, tile_position=(64, 0))
                    cx.inc("s_t", t.drain())
                blk.tensor(pre_t)
                pre_mm = cx.val("s_t")

                def pre_v(v, pre_mm=pre_mm):
                    cx.wait(v, "s_t", pre_mm)
                    v.tensor_reduce(stat[:, 90:91], psE[:, 0:512], AX.X, ALU.max)
                    v.tensor_reduce(stat[:, 91:92], psE[:, 512:1024], AX.X, ALU.max)
                    cx.inc("s_v", v.drain())
                blk.vector(pre_v)
                pre_red = cx.val("s_v")

                def pre_t2(t, pre_red=pre_red):
                    cx.wait(t, "s_v", pre_red)
                    t.transpose(psE[0:1, 0:128], stat[:, 90:91], C["identf"][:, :])
                    t.transpose(psE[0:1, 512:640], stat[:, 91:92], C["identf"][:, :])
                    cx.inc("s_t", t.drain())
                blk.tensor(pre_t2)
                pre_tr = cx.val("s_t")

                def pre_v2(v, dc=dc, pre_tr=pre_tr):
                    cx.wait(v, "s_t", pre_tr)
                    v.tensor_reduce(qkrow[0:1, 2 * dc:2 * dc + 1], psE[0:1, 0:128],
                                    AX.X, ALU.max)
                    v.tensor_reduce(qkrow[0:1, 2 * dc + 1:2 * dc + 2],
                                    psE[0:1, 512:640], AX.X, ALU.max)
                    cx.inc("s_v", v.drain())
                blk.vector(pre_v2)

                def pre_fin(t, fin=cx.val("s_v")):
                    cx.wait(t, "s_v", fin)
                    cx.inc("s_t", t.drain())
                blk.tensor(pre_fin)
                pre_marks.append(cx.val("s_t"))
            else:
                pre_marks.append(cx.val("s_t"))
        q_done_a = cx.val("s_a")

        g_marks_a = []
        g_store_marks = []
        for dc in range(8):
            half6 = 6 + dc // 4

            def g_t(t, dc=dc, half6=half6, q_done_a=q_done_a,
                    wld_marks=wld_marks):
                if dc == 0:
                    cx.wait(t, "s_a", q_done_a)
                if dc % 4 == 0:
                    t.wait_ge(cx.h("gw%d" % (half6 % 2)), wld_marks[half6])
                if dc >= 2:
                    cx.wait(t, "s_a", g_marks_a[dc - 2])
                for nb in range(2):
                    for e in range(ND):
                        t.matmul(psp[:, (dc % 2) * 1024 + nb * 512:(dc % 2) * 1024 + (nb + 1) * 512],
                                 wbuf[:, (half6 % 2) * 4096 + e * 512 + (dc % 4) * 128: (half6 % 2) * 4096 + e * 512 + (dc % 4) * 128 + 128],
                                 xfm[:, e * 1024 + nb * 512: e * 1024 + (nb + 1) * 512],
                                 start=(e == 0), stop=(e == ND - 1))
                cx.inc("s_t", t.drain())
            blk.tensor(g_t)
            g_mm = cx.val("s_t")

            def g_a(s, g_mm=g_mm, dc=dc):
                cx.wait(s, "s_t", g_mm)
                if dc < 2:
                    cx.wait(s, "st%d" % (dc % 2), q_store_marks[6 + dc])
                else:
                    cx.wait(s, "st%d" % (dc % 2), g_store_marks[dc - 2])
                s.activation(st2[:, (dc % 2) * 1024:(dc % 2 + 1) * 1024],
                             psp[:, (dc % 2) * 1024:(dc % 2 + 1) * 1024],
                             AF.Sigmoid, bias=bcol[:, 16 + dc:17 + dc], scale=1.0)
                cx.inc("s_a", s.drain())
            blk.scalar(g_a)
            g_marks_a.append(cx.val("s_a"))

            def g_s(s, sl=sl, dc=dc, ga=g_marks_a[-1]):
                cx.wait(s, "s_a", ga)
                cx.inc("st%d" % (dc % 2), s.dma_start(
                    out=scr["g_scr"][dc * 128:(dc + 1) * 128, sl * 1024:(sl + 1) * 1024],
                    in_=st2[:, (dc % 2) * 1024:(dc % 2 + 1) * 1024]), 16)
            blk.sync(g_s)
            g_store_marks.append(cx.val("st%d" % (dc % 2)))
        g_done_a = cx.val("s_a")
        prev_g_store_marks = g_store_marks

        w1_st0 = cx.val("s_t")
        w1_half_marks = [w1_st0 + (4 * (i + 1)) * 2 for i in range(8)]
        g_last_st = cx.val("s_t")

        w1ld_marks = []

        def w1_g(g, w1_half_marks=w1_half_marks, g_last_st=g_last_st):
            for i in range(8):
                if i < 2:
                    g.wait_ge(cx.h("s_t"), g_last_st)
                else:
                    g.wait_ge(cx.h("s_t"), w1_half_marks[i - 2])
                w1ld_marks.append(cx.inc("gw%d" % (i % 2), g.dma_start(
                    out=AP(wbuf, (i % 2) * 4096, [[wbuf.shape[1], 128], [512, 8], [1, 512]]),
                    in_=AP(dram["w1"], i * 512, [[4096, 128], [128 * 4096, 8], [1, 512]])), 16))
        blk.gpsimd(w1_g)

        sa_e0 = cx.val("s_a")

        def e_t(t, sa_e0=sa_e0, g_done_a=g_done_a,
                prev_edge_mark=prev_edge_mark, w1ld_marks=w1ld_marks):
            cx.wait(t, "s_a", g_done_a)
            if prev_edge_mark is not None:
                cx.wait(t, "s_v", prev_edge_mark)
            for ic in range(32):
                half = (ic // 4) % 2
                if ic % 4 == 0:
                    t.wait_ge(cx.h("gw%d" % (half % 2)), w1ld_marks[ic // 4])
                if ic >= 2:
                    t.wait_ge(cx.h("s_a"), sa_e0 + ic - 1)
                for nb in range(2):
                    for e in range(ND):
                        t.matmul(psp[:, (ic % 2) * 1024 + nb * 512:(ic % 2) * 1024 + (nb + 1) * 512],
                                 wbuf[:, half * 4096 + e * 512 + (ic % 4) * 128: half * 4096 + e * 512 + (ic % 4) * 128 + 128],
                                 xfm[:, e * 1024 + nb * 512: e * 1024 + (nb + 1) * 512],
                                 start=(e == 0), stop=(e == ND - 1))
                cx.inc("s_t", t.drain())
                if ic >= 1:
                    t.wait_ge(cx.h("s_a"), sa_e0 + ic)
                    icp = ic - 1
                    for nb in range(2):
                        t.matmul(psE[0:16, nb * 512:(nb + 1) * 512],
                                 w2sb[:, icp * 16:(icp + 1) * 16],
                                 egs[:, (icp % 2) * 1024 + nb * 512:(icp % 2) * 1024 + (nb + 1) * 512],
                                 start=(icp == 0), stop=False)
                cx.inc("s_t", t.drain())
            t.wait_ge(cx.h("s_a"), sa_e0 + 32)
            for nb in range(2):
                t.matmul(psE[0:16, nb * 512:(nb + 1) * 512],
                         w2sb[:, 31 * 16:32 * 16],
                         egs[:, 1024 + nb * 512:1024 + (nb + 1) * 512],
                         start=False, stop=True)
            cx.inc("s_t", t.drain())
        blk.tensor(e_t)
        e_st_end = cx.val("s_t")
        e_st0 = e_st_end - 65

        def e_a(s, e_st0=e_st0):
            for ic in range(32):
                cx.wait(s, "s_t", e_st0 + 2 * ic + 1)
                s.activation(egs[:, (ic % 2) * 1024:(ic % 2 + 1) * 1024],
                             psp[:, (ic % 2) * 1024:(ic % 2 + 1) * 1024],
                             AF.Silu, bias=bcol[:, 24 + ic:25 + ic], scale=1.0)
                cx.inc("s_a", s.drain())
        blk.scalar(e_a)

        def e_v(v, sl=sl, e_st_end=e_st_end):
            cx.wait(v, "s_t", e_st_end)
            v.tensor_scalar(edgeb[:, sl * 1024:(sl + 1) * 1024],
                            psE[0:16, :], l1st[:, 0:1], None, ALU.add)
            v.drain()
            v.tensor_reduce(l1st[:, 6 + sl:7 + sl],
                            edgeb[:, sl * 1024:(sl + 1) * 1024], AX.X, ALU.max)
            cx.inc("s_v", v.drain())
        blk.vector(e_v)

        if sl == 0:
            last = cx.val("s_v")

            def c0_t(t, last=last):
                cx.wait(t, "s_v", last)
                t.transpose(psE[0:16, 0:1], qkrow[0:1, 0:16], C["onesf"][0:1, 0:1])
                cx.inc("s_t", t.drain())
            blk.tensor(c0_t)
            c0_tr = cx.val("s_t")

            def c0_v(v, c0_tr=c0_tr):
                cx.wait(v, "s_t", c0_tr)
                v.tensor_copy(l1st[:, 3:4], psE[0:16, 0:1])
                v.drain()
                v.tensor_add(l1st[:, 4:5], l1st[:, 3:4], l1st[:, 6:7])
                v.drain()
                v.tensor_add(l1st[:, 4:5], l1st[:, 4:5], l1st[:, 1:2])
                cx.inc("s_v", v.drain())
            blk.vector(c0_v)
        prev_edge_mark = cx.val("s_v")

    done_v = cx.val("s_v")
    done_s = cx.val("sd_s")
    done_a = cx.val("s_a")
    done_st = (cx.val("st0"), cx.val("st1"))
    es_a.close()
    return dict(es=es_l1, vtm=vtm, edgeb=edgeb, rtil=rtil,
                relsb=relsb, l1st=l1st, VW=VW,
                done_v=done_v, done_s=done_s, done_a=done_a, done_st=done_st)


def phase5_l1b(cx, C, dram, scr, l1, wq_done_g, write_out_dbg=False):
    nc, blk = cx.nc, cx.blk
    stat, ident = C["stat"], C["ident"]
    vtm, edgeb = l1["vtm"], l1["edgeb"]
    rtil, relsb, l1st = l1["rtil"], l1["relsb"], l1["l1st"]
    VW = l1["VW"]

    es = ExitStack()
    kbuf = es.enter_context(nc.sbuf_tensor("kbuf", [128, 2 * T], BF16))
    qsl = es.enter_context(nc.sbuf_tensor("qsl", [128, 8 * 512], BF16))
    gsl = es.enter_context(nc.sbuf_tensor("gsl", [128, 8 * 512], BF16))
    e3 = es.enter_context(nc.sbuf_tensor("e3", [16, 3 * 512], BF16))
    e3p = es.enter_context(nc.sbuf_tensor("e3p", [1, 3072], BF16))
    mbuf = es.enter_context(nc.sbuf_tensor("mbuf", [128, 2 * 1664], BF16))
    EA5 = es.enter_context(nc.sbuf_tensor("EA5", [128, 2 * 512], BF16))
    EB5 = es.enter_context(nc.sbuf_tensor("EB5", [128, 2 * 512], BF16))
    gated = es.enter_context(nc.sbuf_tensor("gated", [128, 8 * 512], BF16))
    pvB = es.enter_context(nc.sbuf_tensor("pvB", [64, 512], F32))
    lr65 = es.enter_context(nc.sbuf_tensor("lr65", [65, 1024], F32))
    ones65 = es.enter_context(nc.sbuf_tensor("ones65", [65, 64], F32))
    shid = es.enter_context(nc.sbuf_tensor("shid", [64, 128], F32))
    rbc = es.enter_context(nc.sbuf_tensor("rbc", [128, 512], F32))
    grbc = es.enter_context(nc.sbuf_tensor("grbc", [128, 512], F32))
    borow = es.enter_context(nc.sbuf_tensor("borow", [1, 1024], BF16))
    wo64 = es.enter_context(nc.sbuf_tensor("wo64", [128, 8 * 1024], BF16))
    x1c = es.enter_context(nc.sbuf_tensor("x1c", [128, 2 * 1024], F32))
    lnx = es.enter_context(nc.sbuf_tensor("lnx", [128, 4 * 1024], F32))
    ln5q = es.enter_context(nc.sbuf_tensor("ln5q", [128, 1024], BF16))
    outc = es.enter_context(nc.sbuf_tensor("outc5", [128, 1024], F32))
    psA = es.enter_context(nc.psum_tensor("psA5", [128, 1024], F32))
    psB = es.enter_context(nc.psum_tensor("psB5", [128, 1024], F32))
    pso = es.enter_context(nc.psum_tensor("pso5", [65, 1024], F32))

    def wo_g(g):
        cx.wait(g, "s_v", l1["done_v"])
        cx.wait(g, "st0", l1["done_st"][0])
        cx.wait(g, "st1", l1["done_st"][1])
        cx.wait(g, "sd_g", wq_done_g)
        cx.inc("sd_g", g.dma_start(out=borow[:, :], in_=dram["bo"][:, :]), 16)
        for e in range(ND):
            cx.inc("sd_g", g.dma_start(
                out=wo64[:, e * 1024:(e + 1) * 1024],
                in_=dram["wo"][e * 128:(e + 1) * 128, :]), 16)
    blk.gpsimd(wo_g)
    wo_ld = cx.val("sd_g")

    def c5_v(v):
        cx.wait(v, "s_v", l1["done_v"])
        cx.wait(v, "st0", l1["done_st"][0])
        cx.wait(v, "st1", l1["done_st"][1])
        v.memset(ones65[64:65, :], 1.0)
        v.memset(shid[:, :], 0.0)
        v.drain()
        v.tensor_copy(shid[:, 64:128], C["identf"][0:64, 0:64])
        cx.inc("s_v", v.drain())
    blk.vector(c5_v)
    c5_done = cx.val("s_v")

    x2_marks = []
    prev_wo_marks = None
    for sl in range(8):
        prev_sl_st = cx.val("s_t")

        def qg_s(s, sl=sl, prev_sl_st=prev_sl_st):
            s.wait_ge(cx.h("s_t"), prev_sl_st)
            if sl == 0:
                cx.wait(s, "sd_s", l1["done_s"])
                cx.wait(s, "st0", l1["done_st"][0])
                cx.wait(s, "st1", l1["done_st"][1])
            for dc in range(8):
                cx.inc("sd_s", s.dma_start(
                    out=qsl[:, dc * 512:(dc + 1) * 512],
                    in_=scr["q_scr"][dc * 128:(dc + 1) * 128, sl * 512:(sl + 1) * 512]), 16)
            for dc in range(8):
                cx.inc("sd_s", s.dma_start(
                    out=gsl[:, dc * 512:(dc + 1) * 512],
                    in_=scr["g_scr"][dc * 128:(dc + 1) * 128, sl * 512:(sl + 1) * 512]), 16)
        blk.sync(qg_s)
        qg_ld = cx.val("sd_s")

        def e3_v(v, sl=sl):
            if sl == 0:
                cx.wait(v, "s_v", c5_done)
            cx.wait(v, "sd_g")
            v.tensor_scalar(e3[:, 0:512], edgeb[:, sl * 512:(sl + 1) * 512],
                            l1st[:, 4:5], None, ALU.subtract)
            v.drain()
            v.tensor_scalar(e3[:, 512:1024], e3[:, 0:512],
                            relsb[:, 0:1], None, ALU.add)
            v.tensor_scalar(e3[:, 1024:1536], e3[:, 0:512],
                            relsb[:, 510:511], None, ALU.add)
            cx.inc("s_v", v.drain())
        blk.vector(e3_v)
        e3_done = cx.val("s_v")

        for p in range(8):
            pair_prev_st = cx.val("s_t")
            sv_epi_prev = cx.val("s_v")

            def me_g(g, p=p, e3_done=e3_done, pair_prev_st=pair_prev_st,
                     first=(sl == 0 and p == 0)):
                cx.wait(g, "s_v", e3_done)
                g.wait_ge(cx.h("s_t"), pair_prev_st)
                if first:
                    cx.wait(g, "st0", l1["done_st"][0])
                    cx.wait(g, "st1", l1["done_st"][1])
                cx.inc("sd_g", g.dma_start(
                    out=mbuf[:, 0:1664],
                    in_=AP(scr["rtil"], (2 * p) * 1792, [[1, 128], [1, 1664]])), 16)
                cx.inc("sd_g", g.dma_start(
                    out=mbuf[:, 1664:3328],
                    in_=AP(scr["rtil"], (2 * p + 1) * 1792, [[1, 128], [1, 1664]])), 16)
                cx.inc("sd_g", g.dma_start(
                    out=e3p[:, :], in_=e3[2 * p:2 * p + 2, :]), 16)
                cx.inc("sd_g", g.dma_start(
                    out=kbuf[:, (p % 2) * T:(p % 2 + 1) * T],
                    in_=scr["k_scr"][p * 128:(p + 1) * 128, :]), 16)
            blk.gpsimd(me_g)
            me_ld = cx.val("sd_g")

            st_j0 = cx.val("s_t")
            sa_j0 = cx.val("s_a")

            def pair_t(t, sl=sl, p=p, me_ld=me_ld, st_j0=st_j0, sa_j0=sa_j0,
                       qg_ld=qg_ld, sv_epi_prev=sv_epi_prev, wo_ld=wo_ld):
                cx.wait(t, "sd_g", me_ld)
                cx.wait(t, "sd_s", qg_ld)
                cx.wait(t, "s_v", sv_epi_prev)
                if sl == 0 and p == 0:
                    cx.wait(t, "sd_g", wo_ld)
                for jc in range(33):
                    if jc < 32:
                        par = jc % 2
                        d0 = sl * 512 - jc * 128
                        nd = -766 < d0 < 382
                        if nd:
                            vA, vB = 0, 1536
                        elif d0 >= 382:
                            vA, vB = 1024, 1536 + 1024
                        else:
                            vA, vB = 512, 1536 + 512
                        if jc >= 2:
                            t.wait_ge(cx.h("s_a"), sa_j0 + jc - 1)
                        t.matmul(psA[:, par * 512:(par + 1) * 512],
                                 C["onesb"][0:1, 0:128], e3p[0:1, vA:vA + 512],
                                 start=True, stop=False)
                        t.matmul(psB[:, par * 512:(par + 1) * 512],
                                 C["onesb"][0:1, 0:128], e3p[0:1, vB:vB + 512],
                                 start=True, stop=False)
                        if nd:
                            ms = d0 + 766
                            t.matmul(psA[:, par * 512:(par + 1) * 512],
                                     C["identJ"][:, :], mbuf[:, ms:ms + 512],
                                     start=False, stop=False)
                            t.matmul(psB[:, par * 512:(par + 1) * 512],
                                     C["identJ"][:, :], mbuf[:, 1664 + ms:1664 + ms + 512],
                                     start=False, stop=False)
                        t.matmul(psA[:, par * 512:(par + 1) * 512],
                                 kbuf[0:64, (p % 2) * T + jc * 128: (p % 2) * T + (jc + 1) * 128],
                                 qsl[0:64, p * 512:(p + 1) * 512],
                                 start=False, stop=True, tile_position=(0, 0))
                        t.matmul(psB[:, par * 512:(par + 1) * 512],
                                 kbuf[64:128, (p % 2) * T + jc * 128: (p % 2) * T + (jc + 1) * 128],
                                 qsl[64:128, p * 512:(p + 1) * 512],
                                 start=False, stop=True, tile_position=(64, 0))
                    if jc >= 1:
                        jp = jc - 1
                        parp = jp % 2
                        t.wait_ge(cx.h("s_a"), sa_j0 + jp + 1)
                        t.matmul(pso[:, 0:512],
                                 vtm[:, jp * VW + (2 * p) * 65: jp * VW + (2 * p + 1) * 65],
                                 EA5[:, parp * 512:(parp + 1) * 512],
                                 start=(jp == 0), stop=(jp == 31))
                        t.matmul(pso[:, 512:1024],
                                 vtm[:, jp * VW + (2 * p + 1) * 65: jp * VW + (2 * p + 2) * 65],
                                 EB5[:, parp * 512:(parp + 1) * 512],
                                 start=(jp == 0), stop=(jp == 31))
                    cx.inc("s_t", t.drain())
            blk.tensor(pair_t)
            pv_all = cx.val("s_t")

            def pair_s(s, st_j0=st_j0):
                for jc in range(32):
                    par = jc % 2
                    cx.wait(s, "s_t", st_j0 + jc + 1)
                    s.activation(EA5[:, par * 512:(par + 1) * 512],
                                 psA[:, par * 512:(par + 1) * 512], AF.Exp,
                                 bias=0.0, scale=1.0)
                    s.activation(EB5[:, par * 512:(par + 1) * 512],
                                 psB[:, par * 512:(par + 1) * 512], AF.Exp,
                                 bias=0.0, scale=1.0)
                    cx.inc("s_a", s.drain())
            blk.scalar(pair_s)

            def ep_v1(v, pv_all=pv_all):
                cx.wait(v, "s_t", pv_all)
                v.tensor_copy(lr65[64:65, 0:1024], pso[64:65, 0:1024])
                v.drain()
                v.reciprocal(lr65[64:65, 0:1024], lr65[64:65, 0:1024])
                v.drain()
                v.tensor_copy(pvB[:, :], pso[0:64, 512:1024])
                cx.inc("s_v", v.drain())
            blk.vector(ep_v1)
            m1 = cx.val("s_v")

            def ep_t1(t, m1=m1):
                cx.wait(t, "s_v", m1)
                t.matmul(psA[0:64, 0:512], ones65[64:65, 0:64],
                         lr65[64:65, 0:512], start=True, stop=True,
                         tile_position=(64, 0))
                t.matmul(psA[64:128, 512:1024], ones65[64:65, 0:64],
                         lr65[64:65, 512:1024], start=True, stop=True,
                         tile_position=(64, 64))
                t.matmul(psB[:, 0:512], shid[:, :], pvB[:, :],
                         start=True, stop=True)
                cx.inc("s_t", t.drain())
            blk.tensor(ep_t1)
            m2 = cx.val("s_t")

            def ep_v2(v, p=p, m2=m2):
                cx.wait(v, "s_t", m2)
                v.tensor_copy(rbc[0:64, :], psA[0:64, 0:512])
                v.tensor_copy(rbc[64:128, :], psA[64:128, 512:1024])
                v.drain()
                v.tensor_tensor(grbc[0:64, :], gsl[0:64, p * 512:(p + 1) * 512],
                                rbc[0:64, :], ALU.mult)
                v.tensor_tensor(grbc[64:128, :], gsl[64:128, p * 512:(p + 1) * 512],
                                rbc[64:128, :], ALU.mult)
                v.drain()
                v.scalar_tensor_tensor(gated[0:64, p * 512:(p + 1) * 512],
                                       pso[0:64, 0:512], 1.0, grbc[0:64, :],
                                       ALU.mult, ALU.mult)
                v.scalar_tensor_tensor(gated[64:128, p * 512:(p + 1) * 512],
                                       psB[64:128, 0:512], 1.0, grbc[64:128, :],
                                       ALU.mult, ALU.mult)
                cx.inc("s_v", v.drain())
            blk.vector(ep_v2)

        pairs_done_v = cx.val("s_v")
        pairs_done_st = cx.val("s_t")

        xz0 = cx.val("sd_s")

        x1_marks = []
        sv_wo0 = cx.val("s_v")

        def x1_s(s, sl=sl, sv_wo0=sv_wo0, pwm=prev_wo_marks):
            for tokc in range(4):
                if tokc >= 2:
                    s.wait_ge(cx.h("s_v"), sv_wo0 + tokc - 1)
                elif pwm is not None:
                    s.wait_ge(cx.h("s_v"), pwm[tokc + 2])
                x1_marks.append(cx.inc("sq%d" % (tokc % 2), s.dma_start(
                    out=x1c[:, (tokc % 2) * 1024:(tokc % 2 + 1) * 1024],
                    in_=scr["x1s"][(sl * 4 + tokc) * 128:(sl * 4 + tokc + 1) * 128, :]), 16))
        blk.sync(x1_s)

        def wo_t(t, sl=sl, pairs_done_v=pairs_done_v, sv_wo0=sv_wo0):
            cx.wait(t, "s_v", pairs_done_v)
            for tokc in range(4):
                ps = psA if tokc % 2 == 0 else psB
                if tokc >= 2:
                    t.wait_ge(cx.h("s_v"), sv_wo0 + tokc - 1)
                for nb in range(2):
                    t.matmul(ps[:, nb * 512:(nb + 1) * 512],
                             C["onesb"][0:1, 0:128],
                             borow[0:1, nb * 512:(nb + 1) * 512],
                             start=True, stop=False)
                for dc in range(8):
                    for nb in range(2):
                        t.matmul(ps[:, nb * 512:(nb + 1) * 512],
                                 gated[:, dc * 512 + tokc * 128: dc * 512 + (tokc + 1) * 128],
                                 wo64[:, dc * 1024 + nb * 512: dc * 1024 + (nb + 1) * 512],
                                 start=False, stop=(dc == 7))
                cx.inc("s_t", t.drain())
        blk.tensor(wo_t)
        wo_st0 = cx.val("s_t") - 4

        def wo_v(v, sl=sl, wo_st0=wo_st0, x1_marks=x1_marks):
            for tokc in range(4):
                ps = psA if tokc % 2 == 0 else psB
                cx.wait(v, "s_t", wo_st0 + tokc + 1)
                cx.wait(v, "sq%d" % (tokc % 2), x1_marks[tokc])
                v.tensor_add(lnx[:, tokc * 1024:(tokc + 1) * 1024], ps[:, :],
                             x1c[:, (tokc % 2) * 1024:(tokc % 2 + 1) * 1024])
                v.drain()
                v.reduce_sum(stat[:, 168 + tokc:169 + tokc],
                             lnx[:, tokc * 1024:(tokc + 1) * 1024], AX.X)
                v.scalar_tensor_tensor(ln5q[:, :],
                                       lnx[:, tokc * 1024:(tokc + 1) * 1024], 1.0,
                                       lnx[:, tokc * 1024:(tokc + 1) * 1024],
                                       ALU.mult, ALU.mult,
                                       accum_out=stat[:, 172 + tokc:173 + tokc])
                cx.inc("s_v", v.drain())
            v.tensor_scalar_mul(stat[:, 168:172], stat[:, 168:172], 1.0 / D)
            v.tensor_scalar_mul(stat[:, 172:176], stat[:, 172:176], 1.0 / D)
            v.drain()
            v.scalar_tensor_tensor(stat[:, 176:180], stat[:, 168:172], 1.0,
                                   stat[:, 168:172], ALU.mult, ALU.mult)
            v.drain()
            v.tensor_sub(stat[:, 172:176], stat[:, 172:176], stat[:, 176:180])
            v.drain()
            v.tensor_scalar_add(stat[:, 172:176], stat[:, 172:176], 1e-5)
            v.drain()
            v.reciprocal(stat[:, 176:180], stat[:, 172:176])
            cx.inc("s_v", v.drain())
        blk.vector(wo_v)
        lnst = cx.val("s_v")

        def ln_s5(s, lnst=lnst):
            cx.wait(s, "s_v", lnst)
            s.activation(stat[:, 172:176], stat[:, 176:180], AF.Sqrt)
            cx.inc("s_a", s.drain())
        blk.scalar(ln_s5)
        rstd5 = cx.val("s_a")

        for tokc in range(4):
            def ap_v(v, sl=sl, tokc=tokc, rstd5=rstd5):
                cx.wait(v, "s_a", rstd5)
                if x2_marks:
                    cx.wait(v, "st0", x2_marks[-1])
                    if write_out_dbg:
                        cx.wait(v, "sd_o")
                v.tensor_scalar(outc[:, :], lnx[:, tokc * 1024:(tokc + 1) * 1024],
                                stat[:, 168 + tokc:169 + tokc],
                                stat[:, 172 + tokc:173 + tokc],
                                ALU.subtract, ALU.mult)
                v.drain()
                v.tensor_reduce(stat[:, sl * 4 + tokc:sl * 4 + tokc + 1],
                                outc[:, :], AX.X, ALU.max,
                                apply_absolute_value=True)
                cx.inc("s_v", v.drain())
            blk.vector(ap_v)
            apd = cx.val("s_v")

            def x2_s(s, sl=sl, tokc=tokc, apd=apd):
                cx.wait(s, "s_v", apd)
                cx.inc("st0", s.dma_start(
                    out=scr["x2s"][(sl * 4 + tokc) * 128:(sl * 4 + tokc + 1) * 128, :],
                    in_=outc[:, :]), 16)
                if write_out_dbg:
                    cx.inc("sd_o", s.dma_start(
                        out=scr["out"][(sl * 4 + tokc) * 128:(sl * 4 + tokc + 1) * 128, :],
                        in_=outc[:, :]), 16)
            blk.sync(x2_s)
            x2_marks.append(cx.val("st0"))
        prev_wo_marks = [sv_wo0 + i for i in range(5)]

    done_v = cx.val("s_v")
    done_s = cx.val("st0")
    es.close()
    return dict(done_v=done_v, done_s=done_s)


def phase6_l1ffn(cx, C, dram, scr, l1, x2, wq_done_g):
    nc, blk = cx.nc, cx.blk
    stat, ident = C["stat"], C["ident"]
    NI = 22

    es = ExitStack()
    row6 = es.enter_context(nc.sbuf_tensor("row6", [1, 16], F32))
    bc6 = es.enter_context(nc.sbuf_tensor("bc6", [128, 4], F32))
    clipsb = es.enter_context(nc.sbuf_tensor("clipsb", [1, 2], F32))
    es_ar1 = ExitStack()
    ps6 = es_ar1.enter_context(nc.psum_tensor("ps6", [128, 512], F32))

    def ar1_v(v, x2=x2):
        cx.wait(v, "s_v", x2["done_v"])
        v.tensor_reduce(stat[:, 32:33], stat[:, 0:32], AX.X, ALU.max)
        cx.inc("s_v", v.drain())
    blk.vector(ar1_v)
    m1 = cx.val("s_v")

    def ar1_t(t, m1=m1):
        cx.wait(t, "s_v", m1)
        t.transpose(ps6[0:1, 0:128], stat[:, 32:33], C["identf"][:, :])
        cx.inc("s_t", t.drain())
    blk.tensor(ar1_t)
    m2 = cx.val("s_t")

    def ar1_v2(v, m2=m2):
        cx.wait(v, "s_t", m2)
        v.tensor_reduce(row6[0:1, 0:1], ps6[0:1, 0:128], AX.X, ALU.max)
        v.drain()
        v.tensor_copy(row6[0:1, 0:16], AP(row6, 0, [[16, 1], [0, 16], [1, 1]]))
        cx.inc("s_v", v.drain())
    blk.vector(ar1_v2)
    m3 = cx.val("s_v")

    def ar1_g(g, m3=m3):
        cx.wait(g, "s_v", m3)
        cx.inc("sd_o2", g.dma_start(out=scr["cc1_in"][:, :], in_=row6[0:1, 0:16]), 16)
        cx.wait(g, "sd_o2")
        g.collective_compute("AllReduce", ALU.max,
                             replica_groups=[list(range(N_CORES))],
                             ins=[scr["cc1_in"].ap().opt()],
                             outs=[scr["cc1_out"].ap().opt()]).then_inc(cx.h("s_g"), 1)
        cx._cnt["s_g"] += 1
        cx.wait(g, "s_g")
        cx.inc("sd_o2", g.dma_start(out=row6[0:1, 0:16], in_=scr["cc1_out"][:, :]), 16)
        cx.inc("sd_o2", g.dma_start(out=clipsb[:, :], in_=dram["clips"][:, :]), 16)
    blk.gpsimd(ar1_g)
    m4 = cx.val("sd_o2")

    def ar1_s(s, m4=m4):
        cx.wait(s, "sd_o2", m4)
        s.activation(clipsb[0:1, 0:1], clipsb[0:1, 0:1], AF.Sigmoid)
        cx.inc("s_a", s.drain())
    blk.scalar(ar1_s)
    m5 = cx.val("s_a")

    def ar1_v3(v, m5=m5):
        cx.wait(v, "s_a", m5)
        v.tensor_tensor(row6[0:1, 1:2], row6[0:1, 0:1], clipsb[0:1, 0:1], ALU.mult)
        v.drain()
        v.tensor_scalar_mul(row6[0:1, 1:2], row6[0:1, 1:2], 1.0 / 127.0)
        v.drain()
        v.reciprocal(row6[0:1, 2:3], row6[0:1, 1:2])
        cx.inc("s_v", v.drain())
    blk.vector(ar1_v3)
    m6 = cx.val("s_v")

    def ar1_t2(t, m6=m6):
        cx.wait(t, "s_v", m6)
        t.matmul(ps6[:, 0:2], C["onesf"][0:1, 0:128], row6[0:1, 1:3],
                 start=True, stop=True)
        cx.inc("s_t", t.drain())
    blk.tensor(ar1_t2)
    m7 = cx.val("s_t")

    def ar1_v4(v, m7=m7):
        cx.wait(v, "s_t", m7)
        v.tensor_copy(bc6[:, 0:2], ps6[:, 0:2])
        cx.inc("s_v", v.drain())
    blk.vector(ar1_v4)
    sx_done = cx.val("s_v")
    es_ar1.close()

    es_a = ExitStack()
    x2c = es_a.enter_context(nc.sbuf_tensor("x2c", [128, 2 * 1024], F32))
    qsc6 = es_a.enter_context(nc.sbuf_tensor("qsc6", [128, 1024], F32))
    xqb6 = es_a.enter_context(nc.sbuf_tensor("xqb6", [128, 8 * 1024], BF16))
    xqf6 = es_a.enter_context(nc.sbuf_tensor("xqf6", [128, 8 * 1024], BF16))
    wgsb = es_a.enter_context(nc.sbuf_tensor("wgsb", [128, 8 * 2816], BF16))
    wusb = es_a.enter_context(nc.sbuf_tensor("wusb", [128, 8 * 2816], BF16))
    sg6 = es_a.enter_context(nc.sbuf_tensor("sg6", [128, 2 * 1024], BF16))
    hc6 = es_a.enter_context(nc.sbuf_tensor("hc6", [128, 2 * 1024], BF16))
    pst6 = es_a.enter_context(nc.psum_tensor("pst6", [128, 2048], BF16))
    psg = es_a.enter_context(nc.psum_tensor("psg", [128, 2048], F32))
    psu = es_a.enter_context(nc.psum_tensor("psu", [128, 1024], F32))

    prev_q_last = None
    prev_gu_last = None
    h_marks_s = []
    for sl in range(4):
        tc0 = sl * 8

        wg_prev_st = cx.val("s_t")

        def w6_g(g, sl=sl, wg_prev_st=wg_prev_st, x2=x2):
            if sl == 0:
                cx.wait(g, "sd_g", wq_done_g)
                cx.wait(g, "s_v", x2["done_v"])
                cx.wait(g, "st0", x2["done_s"])
            else:
                g.wait_ge(cx.h("s_t"), wg_prev_st)
            for e in range(ND):
                cx.inc("sd_g", g.dma_start(
                    out=wgsb[:, e * 2816:(e + 1) * 2816],
                    in_=scr["wgT"][e * 128:(e + 1) * 128, :]), 16)
            for e in range(ND):
                cx.inc("sd_g", g.dma_start(
                    out=wusb[:, e * 2816:(e + 1) * 2816],
                    in_=scr["wuT"][e * 128:(e + 1) * 128, :]), 16)
        blk.gpsimd(w6_g)
        w6_ld = cx.val("sd_g")

        q_marks = []
        for c in range(8):
            def x2_s6(s, sl=sl, tc0=tc0, c=c, x2=x2, prev_q_last=prev_q_last):
                if sl == 0 and c == 0:
                    cx.wait(s, "st0", x2["done_s"])
                if c >= 2:
                    cx.wait(s, "s_v", q_marks[c - 2])
                elif prev_q_last is not None:
                    cx.wait(s, "s_v", prev_q_last)
                cx.inc("sq%d" % (c % 2), s.dma_start(
                    out=x2c[:, (c % 2) * 1024:(c % 2 + 1) * 1024],
                    in_=scr["x2s"][(tc0 + c) * 128:(tc0 + c + 1) * 128, :]), 16)
            blk.sync(x2_s6)
            xld = cx.val("sq%d" % (c % 2))

            def q_v6(v, c=c, xld=xld, sx_done=sx_done):
                if c == 0:
                    cx.wait(v, "s_v", sx_done)
                cx.wait(v, "sq%d" % (c % 2), xld)
                v.tensor_scalar(qsc6[:, :], x2c[:, (c % 2) * 1024:(c % 2 + 1) * 1024],
                                bc6[:, 1:2], MAGIC, ALU.mult, ALU.add)
                v.drain()
                v.tensor_scalar(xqb6[:, c * 1024:(c + 1) * 1024], qsc6[:, :],
                                MAGIC, 127.0, ALU.subtract, ALU.min)
                v.drain()
                v.tensor_scalar_max(xqb6[:, c * 1024:(c + 1) * 1024],
                                    xqb6[:, c * 1024:(c + 1) * 1024], -128.0)
                cx.inc("s_v", v.drain())
            blk.vector(q_v6)
            q_marks.append(cx.val("s_v"))
        prev_q_last = q_marks[-1]
        emit_tmfm_transpose(cx, xqb6, xqf6, ident, pst6, cx.val("s_v"))
        xqf_done = cx.val("s_v")

        h_marks_v = []
        for dc in range(NI):
            def gu_t(t, sl=sl, dc=dc, w6_ld=w6_ld, xqf_done=xqf_done,
                     prev_gu_last=prev_gu_last):
                if dc == 0:
                    cx.wait(t, "s_v", xqf_done)
                    cx.wait(t, "sd_g", w6_ld)
                if dc >= 2:
                    cx.wait(t, "s_v", h_marks_v[dc - 2])
                elif prev_gu_last is not None:
                    cx.wait(t, "s_v", prev_gu_last)
                par = dc % 2
                for nb in range(2):
                    for e in range(ND):
                        t.matmul(psg[:, par * 1024 + nb * 512: par * 1024 + (nb + 1) * 512],
                                 wgsb[:, e * 2816 + dc * 128:(e) * 2816 + (dc + 1) * 128],
                                 xqf6[:, e * 1024 + nb * 512: e * 1024 + (nb + 1) * 512],
                                 start=(e == 0), stop=(e == ND - 1))
                cx.inc("s_t", t.drain())
                if dc >= 1:
                    cx.wait(t, "s_v", h_marks_v[dc - 1])
                for nb in range(2):
                    for e in range(ND):
                        t.matmul(psu[:, nb * 512:(nb + 1) * 512],
                                 wusb[:, e * 2816 + dc * 128:(e) * 2816 + (dc + 1) * 128],
                                 xqf6[:, e * 1024 + nb * 512: e * 1024 + (nb + 1) * 512],
                                 start=(e == 0), stop=(e == ND - 1))
                cx.inc("s_t", t.drain())
            blk.tensor(gu_t)
            g_mm = cx.val("s_t") - 1
            u_mm = cx.val("s_t")

            def gu_a(s, dc=dc, g_mm=g_mm):
                cx.wait(s, "s_t", g_mm)
                par = dc % 2
                s.activation(sg6[:, par * 1024:(par + 1) * 1024],
                             psg[:, par * 1024:(par + 1) * 1024],
                             AF.Silu, bias=0.0, scale=bc6[:, 0:1])
                cx.inc("s_a", s.drain())
            blk.scalar(gu_a)
            sg_done = cx.val("s_a")

            def gu_v(v, sl=sl, dc=dc, u_mm=u_mm, sg_done=sg_done,
                     h_marks_s=h_marks_s):
                cx.wait(v, "s_t", u_mm)
                cx.wait(v, "s_a", sg_done)
                same_par = [m for m in h_marks_s if m[0] == dc % 2]
                if same_par:
                    cx.wait(v, "gx%d" % (dc % 2), same_par[-1][1])
                par = dc % 2
                v.scalar_tensor_tensor(hc6[:, par * 1024:(par + 1) * 1024],
                                       psu[:, :],
                                       bc6[:, 0:1], sg6[:, par * 1024:(par + 1) * 1024],
                                       ALU.mult, ALU.mult)
                v.drain()
                v.tensor_reduce(stat[:, 96 + dc:97 + dc],
                                hc6[:, par * 1024:(par + 1) * 1024], AX.X, ALU.max,
                                apply_absolute_value=True)
                cx.inc("s_v", v.drain())
            blk.vector(gu_v)
            h_marks_v.append(cx.val("s_v"))

            def h_g(g, sl=sl, dc=dc, hv=h_marks_v[-1]):
                cx.wait(g, "s_v", hv)
                par = dc % 2
                cx.inc("gx%d" % par, g.dma_start(
                    out=scr["hs"][dc * 128:(dc + 1) * 128, sl * 1024:(sl + 1) * 1024],
                    in_=hc6[:, par * 1024:(par + 1) * 1024]), 16)
            blk.gpsimd(h_g)
            h_marks_s.append((par_ := dc % 2, cx.val("gx%d" % (dc % 2))))

        prev_gu_last = h_marks_v[-1]

        def hm_v(v, sl=sl, last=h_marks_v[-1]):
            cx.wait(v, "s_v", last)
            v.tensor_reduce(stat[:, 33 + sl:34 + sl], stat[:, 96:96 + NI],
                            AX.X, ALU.max)
            cx.inc("s_v", v.drain())
        blk.vector(hm_v)
    h_all_v = cx.val("s_v")
    h_all_g = (cx.val("gx0"), cx.val("gx1"))
    es_a.close()

    es_ar2 = ExitStack()
    ps6 = es_ar2.enter_context(nc.psum_tensor("ps6b", [128, 512], F32))
    def ar2_v(v, h_all_v=h_all_v):
        cx.wait(v, "s_v", h_all_v)
        v.tensor_reduce(stat[:, 37:38], stat[:, 33:37], AX.X, ALU.max)
        cx.inc("s_v", v.drain())
    blk.vector(ar2_v)
    n1 = cx.val("s_v")

    def ar2_t(t, n1=n1):
        cx.wait(t, "s_v", n1)
        t.transpose(ps6[0:1, 0:128], stat[:, 37:38], C["identf"][:, :])
        cx.inc("s_t", t.drain())
    blk.tensor(ar2_t)
    n2 = cx.val("s_t")

    def ar2_v2(v, n2=n2):
        cx.wait(v, "s_t", n2)
        v.tensor_reduce(row6[0:1, 4:5], ps6[0:1, 0:128], AX.X, ALU.max)
        v.drain()
        v.tensor_copy(row6[0:1, 0:16], AP(row6, 4, [[16, 1], [0, 16], [1, 1]]))
        cx.inc("s_v", v.drain())
    blk.vector(ar2_v2)
    n3 = cx.val("s_v")

    def ar2_g(g, n3=n3):
        cx.wait(g, "s_v", n3)
        cx.inc("sd_o2", g.dma_start(out=scr["cc2_in"][:, :], in_=row6[0:1, 0:16]), 16)
        cx.wait(g, "sd_o2")
        g.collective_compute("AllReduce", ALU.max,
                             replica_groups=[list(range(N_CORES))],
                             ins=[scr["cc2_in"].ap().opt()],
                             outs=[scr["cc2_out"].ap().opt()]).then_inc(cx.h("s_g"), 1)
        cx._cnt["s_g"] += 1
        cx.wait(g, "s_g")
        cx.inc("sd_o2", g.dma_start(out=row6[0:1, 0:16], in_=scr["cc2_out"][:, :]), 16)
    blk.gpsimd(ar2_g)
    n4 = cx.val("sd_o2")

    def ar2_s(s, n4=n4):
        cx.wait(s, "sd_o2", n4)
        s.activation(clipsb[0:1, 1:2], clipsb[0:1, 1:2], AF.Sigmoid)
        cx.inc("s_a", s.drain())
    blk.scalar(ar2_s)
    n5 = cx.val("s_a")

    def ar2_v3(v, n5=n5):
        cx.wait(v, "s_a", n5)
        v.tensor_tensor(row6[0:1, 5:6], row6[0:1, 0:1], clipsb[0:1, 1:2], ALU.mult)
        v.drain()
        v.tensor_scalar_mul(row6[0:1, 5:6], row6[0:1, 5:6], 1.0 / 127.0)
        v.drain()
        v.reciprocal(row6[0:1, 6:7], row6[0:1, 5:6])
        cx.inc("s_v", v.drain())
    blk.vector(ar2_v3)
    n6 = cx.val("s_v")

    def ar2_t2(t, n6=n6):
        cx.wait(t, "s_v", n6)
        t.matmul(ps6[:, 0:2], C["onesf"][0:1, 0:128], row6[0:1, 5:7],
                 start=True, stop=True)
        cx.inc("s_t", t.drain())
    blk.tensor(ar2_t2)
    n7 = cx.val("s_t")

    def ar2_v4(v, n7=n7):
        cx.wait(v, "s_t", n7)
        v.tensor_copy(bc6[:, 2:4], ps6[:, 0:2])
        cx.inc("s_v", v.drain())
    blk.vector(ar2_v4)
    sh_done = cx.val("s_v")
    es_ar2.close()

    es_b = ExitStack()
    hbt = es_b.enter_context(nc.sbuf_tensor("hbt", [128, 2 * 1024], BF16))
    qsc6b = es_b.enter_context(nc.sbuf_tensor("qsc6b", [128, 1024], F32))
    hq = es_b.enter_context(nc.sbuf_tensor("hq", [128, NI * 1024], BF16))
    wdsb = es_b.enter_context(nc.sbuf_tensor("wdsb", [128, NI * 1024], BF16))
    x2c2 = es_b.enter_context(nc.sbuf_tensor("x2c2", [128, 2 * 1024], F32))
    lnx6 = es_b.enter_context(nc.sbuf_tensor("lnx6", [128, 8 * 1024], F32))
    ln6q = es_b.enter_context(nc.sbuf_tensor("ln6q", [128, 1024], BF16))
    outc6 = es_b.enter_context(nc.sbuf_tensor("outc6", [128, 1024], F32))
    psz = es_b.enter_context(nc.psum_tensor("psz", [128, 2048], F32))

    prev_dn_st = None
    prev_ep = []
    prev_hq_marks = None
    out_marks = []
    for sl in range(4):
        tc0 = sl * 8
        wd_prev = cx.val("s_t")

        def wd_g(g, sl=sl, wd_prev=wd_prev, h_all_g=h_all_g):
            if sl == 0:
                g.wait_ge(cx.h("gx0"), h_all_g[0])
                g.wait_ge(cx.h("gx1"), h_all_g[1])
            else:
                g.wait_ge(cx.h("s_t"), wd_prev)
            for ic in range(NI):
                cx.inc("sd_g", g.dma_start(
                    out=wdsb[:, ic * 1024:(ic + 1) * 1024],
                    in_=scr["wdT"][ic * 128:(ic + 1) * 128, :]), 16)
        blk.gpsimd(wd_g)
        wd_ld = cx.val("sd_g")

        hq_marks = []
        for ic in range(NI):
            def h_s6(s, sl=sl, ic=ic, x2=x2, phm=prev_hq_marks,
                     pep=prev_ep[:]):
                if ic >= 2:
                    cx.wait(s, "s_v", hq_marks[ic - 2])
                elif sl == 0:
                    cx.wait(s, "s_v", sh_done)
                    g0, g1 = h_all_g
                    s.wait_ge(cx.h("gx0"), g0)
                    s.wait_ge(cx.h("gx1"), g1)
                else:
                    cx.wait(s, "s_v", max(phm[ic - 2], pep[ic - 2]))
                cx.inc("sq%d" % (ic % 2), s.dma_start(
                    out=hbt[:, (ic % 2) * 1024:(ic % 2 + 1) * 1024],
                    in_=scr["hs"][ic * 128:(ic + 1) * 128, sl * 1024:(sl + 1) * 1024]), 16)
            blk.sync(h_s6)
            hld = cx.val("sq%d" % (ic % 2))

            def hq_v(v, ic=ic, hld=hld, sh_done=sh_done, prev_dn_st=prev_dn_st):
                if ic == 0:
                    cx.wait(v, "s_v", sh_done)
                    if prev_dn_st is not None:
                        cx.wait(v, "s_t", prev_dn_st)
                cx.wait(v, "sq%d" % (ic % 2), hld)
                v.tensor_scalar(qsc6b[:, :], hbt[:, (ic % 2) * 1024:(ic % 2 + 1) * 1024],
                                bc6[:, 3:4], MAGIC, ALU.mult, ALU.add)
                v.drain()
                v.tensor_scalar(hq[:, ic * 1024:(ic + 1) * 1024], qsc6b[:, :],
                                MAGIC, 127.0, ALU.subtract, ALU.min)
                v.drain()
                v.tensor_scalar_max(hq[:, ic * 1024:(ic + 1) * 1024],
                                    hq[:, ic * 1024:(ic + 1) * 1024], -128.0)
                cx.inc("s_v", v.drain())
            blk.vector(hq_v)
            hq_marks.append(cx.val("s_v"))
        hq_done = cx.val("s_v")
        prev_hq_marks = hq_marks

        ep_marks = []
        for tokc in range(8):
            def x2_s6b(s, sl=sl, tokc=tokc, x2=x2, prev_ep=prev_ep,
                       hq_done=hq_done):
                if tokc >= 2:
                    cx.wait(s, "s_v", ep_marks[tokc - 2])
                else:
                    m = hq_done
                    if prev_ep:
                        m = max(m, prev_ep[tokc - 2])
                    cx.wait(s, "s_v", m)
                cx.inc("sq%d" % (tokc % 2), s.dma_start(
                    out=x2c2[:, (tokc % 2) * 1024:(tokc % 2 + 1) * 1024],
                    in_=scr["x2s"][(sl * 8 + tokc) * 128:(sl * 8 + tokc + 1) * 128, :]), 16)
            blk.sync(x2_s6b)
            x2ld = cx.val("sq%d" % (tokc % 2))

            def dn_t(t, sl=sl, tokc=tokc, hq_done=hq_done, wd_ld=wd_ld,
                     x2ld=x2ld):
                if tokc == 0:
                    cx.wait(t, "s_v", hq_done)
                    cx.wait(t, "sd_g", wd_ld)
                par = tokc % 2
                if tokc >= 2:
                    t.wait_ge(cx.h("s_v"), ep_marks[tokc - 2])
                for ic in range(NI):
                    for nb in range(2):
                        t.matmul(psz[:, par * 1024 + nb * 512: par * 1024 + (nb + 1) * 512],
                                 hq[:, ic * 1024 + tokc * 128: ic * 1024 + (tokc + 1) * 128],
                                 wdsb[:, ic * 1024 + nb * 512: ic * 1024 + (nb + 1) * 512],
                                 start=(ic == 0), stop=(ic == NI - 1))
                cx.inc("s_t", t.drain())
            blk.tensor(dn_t)
            dn_mm = cx.val("s_t")

            def dn_v(v, sl=sl, tokc=tokc, dn_mm=dn_mm, x2ld=x2ld):
                cx.wait(v, "s_t", dn_mm)
                cx.wait(v, "sq%d" % (tokc % 2), x2ld)
                par = tokc % 2
                v.scalar_tensor_tensor(lnx6[:, tokc * 1024:(tokc + 1) * 1024],
                                       psz[:, par * 1024:(par + 1) * 1024],
                                       bc6[:, 2:3],
                                       x2c2[:, par * 1024:(par + 1) * 1024],
                                       ALU.mult, ALU.add)
                v.drain()
                v.reduce_sum(stat[:, 104 + tokc:105 + tokc],
                             lnx6[:, tokc * 1024:(tokc + 1) * 1024], AX.X)
                v.scalar_tensor_tensor(ln6q[:, :],
                                       lnx6[:, tokc * 1024:(tokc + 1) * 1024], 1.0,
                                       lnx6[:, tokc * 1024:(tokc + 1) * 1024],
                                       ALU.mult, ALU.mult,
                                       accum_out=stat[:, 112 + tokc:113 + tokc])
                cx.inc("s_v", v.drain())
            blk.vector(dn_v)
            ep_marks.append(cx.val("s_v"))

        prev_dn_st = cx.val("s_t")
        prev_ep = ep_marks

        def ln6_v(v, last=ep_marks[-1]):
            cx.wait(v, "s_v", last)
            v.tensor_scalar_mul(stat[:, 104:112], stat[:, 104:112], 1.0 / D)
            v.tensor_scalar_mul(stat[:, 112:120], stat[:, 112:120], 1.0 / D)
            v.drain()
            v.scalar_tensor_tensor(stat[:, 120:128], stat[:, 104:112], 1.0,
                                   stat[:, 104:112], ALU.mult, ALU.mult)
            v.drain()
            v.tensor_sub(stat[:, 112:120], stat[:, 112:120], stat[:, 120:128])
            v.drain()
            v.tensor_scalar_add(stat[:, 112:120], stat[:, 112:120], 1e-5)
            v.drain()
            v.reciprocal(stat[:, 120:128], stat[:, 112:120])
            cx.inc("s_v", v.drain())
        blk.vector(ln6_v)
        l6 = cx.val("s_v")

        def ln6_s(s, l6=l6):
            cx.wait(s, "s_v", l6)
            s.activation(stat[:, 112:120], stat[:, 120:128], AF.Sqrt)
            cx.inc("s_a", s.drain())
        blk.scalar(ln6_s)
        r6 = cx.val("s_a")

        for tokc in range(8):
            def fo_v(v, sl=sl, tokc=tokc, r6=r6):
                cx.wait(v, "s_a", r6)
                if out_marks:
                    cx.wait(v, "sd_o", out_marks[-1])
                v.tensor_scalar(outc6[:, :], lnx6[:, tokc * 1024:(tokc + 1) * 1024],
                                stat[:, 104 + tokc:105 + tokc],
                                stat[:, 112 + tokc:113 + tokc],
                                ALU.subtract, ALU.mult)
                cx.inc("s_v", v.drain())
            blk.vector(fo_v)
            fod = cx.val("s_v")

            def fo_s(s, sl=sl, tokc=tokc, fod=fod):
                cx.wait(s, "s_v", fod)
                cx.inc("sd_o", s.dma_start(
                    out=scr["out"][(sl * 8 + tokc) * 128:(sl * 8 + tokc + 1) * 128, :],
                    in_=outc6[:, :]), 16)
            blk.sync(fo_s)
            out_marks.append(cx.val("sd_o"))

    es_b.close()
    es.close()

    def fin_s(s):
        cx.wait(s, "sd_o")
        cx.wait(s, "sd_s")
        cx.wait(s, "sd_g")
    blk.sync(fin_s)


# version 05
# speedup vs baseline: 1.1577x; 1.1577x over previous
from contextlib import ExitStack

import numpy as np
import concourse.bass as bass
import concourse.mybir as mybir
from concourse.bass_utils import run_bass_kernel_spmd

F32 = mybir.dt.float32
BF16 = mybir.dt.bfloat16
I32 = mybir.dt.int32
AF = mybir.ActivationFunctionType
ALU = mybir.AluOpType
AX = mybir.AxisListType
AP = bass.AP

T = 4096
D = 1024
NT = T // 128
ND = D // 128
MAGIC = float(3 * 2 ** 22)
N_CORES = 8
DBG = False
UPTO = 99
SUB2A = 99


def emit_dump(cx, dram, name, src_ap, shape, marker):
    if not DBG:
        return
    t = cx.nc.dram_tensor("dbg_" + name, list(shape), F32, kind="ExternalOutput")

    def d_g(g):
        sem, val = marker
        cx.wait(g, sem, val)
        cx.inc("sd_o2", g.dma_start(out=t[:, :], in_=src_ap), 16)
    cx.blk.gpsimd(d_g)


class Ctx:

    def __init__(self, nc, blk):
        self.nc = nc
        self.blk = blk
        self._sems = {}
        self._cnt = {}
        self._names = {}

    def sem(self, name):
        h = self.nc.alloc_semaphore(name=name)
        self._sems[name] = h
        self._cnt[name] = 0

    def h(self, name):
        return self._sems[name]

    def inc(self, name, instr, n=1):
        instr.then_inc(self._sems[name], n)
        self._cnt[name] += n
        return self._cnt[name]

    def val(self, name):
        return self._cnt[name]

    def wait(self, eng, name, v=None):
        eng.wait_ge(self._sems[name], self._cnt[name] if v is None else v)

    def uname(self, base):
        i = self._names.get(base, 0)
        self._names[base] = i + 1
        return f"{base}{i}"



def emit_tmfm_transpose(cx, src, dst, ident, pst, src_ready_sv, nchunks=8):
    ngrp = nchunks * 2

    def tr_t(t):
        cx.wait(t, "s_v", src_ready_sv)
        for grp in range(ngrp):
            if grp >= 2:
                t.wait_ge(cx.h("s_v"), src_ready_sv + grp - 1)
            for j in range(4):
                idx = grp * 4 + j
                tc, dc = idx // 8, idx % 8
                slot = (grp % 2) * 8 + j
                t.transpose(pst[:, slot * 128:(slot + 1) * 128],
                            src[:, tc * 1024 + dc * 128: tc * 1024 + (dc + 1) * 128],
                            ident[:, :])
            cx.inc("s_t", t.drain())
    cx.blk.tensor(tr_t)
    t0 = cx.val("s_t") - ngrp

    def tr_v(v):
        for grp in range(ngrp):
            cx.wait(v, "s_t", t0 + grp + 1)
            for j in range(4):
                idx = grp * 4 + j
                tc, dc = idx // 8, idx % 8
                slot = (grp % 2) * 8 + j
                v.tensor_copy(dst[:, dc * 1024 + tc * 128: dc * 1024 + (tc + 1) * 128],
                              pst[:, slot * 128:(slot + 1) * 128])
            cx.inc("s_v", v.drain())
    cx.blk.vector(tr_v)


def emit_ln(cx, xs, stat, wait_marker, eps=1e-5):
    nc = cx.nc
    sem, val = wait_marker
    with nc.sbuf_tensor(cx.uname("lnsq"), [128, 1024], BF16) as lnsq:
        def ln_v(v):
            cx.wait(v, sem, val)
            if DBG:
                cx.wait(v, "sd_o2")
            for tc in range(NT):
                xc = xs[:, tc * 1024:(tc + 1) * 1024]
                v.reduce_sum(stat[:, 104 + tc:105 + tc], xc, AX.X)
                v.scalar_tensor_tensor(lnsq[:, :], xc, 1.0, xc, ALU.mult, ALU.mult,
                                       accum_out=stat[:, 136 + tc:137 + tc])
                v.drain()
            v.tensor_scalar_mul(stat[:, 104:136], stat[:, 104:136], 1.0 / D)
            v.tensor_scalar_mul(stat[:, 136:168], stat[:, 136:168], 1.0 / D)
            v.drain()
            v.scalar_tensor_tensor(stat[:, 72:104], stat[:, 104:136], 1.0,
                                   stat[:, 104:136], ALU.mult, ALU.mult)
            v.drain()
            v.tensor_sub(stat[:, 136:168], stat[:, 136:168], stat[:, 72:104])
            v.drain()
            v.tensor_scalar_add(stat[:, 136:168], stat[:, 136:168], eps)
            v.drain()
            v.reciprocal(stat[:, 72:104], stat[:, 136:168])
            cx.inc("s_v", v.drain())
        cx.blk.vector(ln_v)
        sready = cx.val("s_v")

        def ln_s(s):
            cx.wait(s, "s_v", sready)
            s.activation(stat[:, 136:168], stat[:, 72:104], AF.Sqrt)
            cx.inc("s_a", s.drain())
        cx.blk.scalar(ln_s)
        rstd_ready = cx.val("s_a")

        def ln_v2(v):
            cx.wait(v, "s_a", rstd_ready)
            for tc in range(NT):
                v.tensor_scalar(xs[:, tc * 1024:(tc + 1) * 1024],
                                xs[:, tc * 1024:(tc + 1) * 1024],
                                stat[:, 104 + tc:105 + tc],
                                stat[:, 136 + tc:137 + tc],
                                ALU.subtract, ALU.mult)
            cx.inc("s_v", v.drain())
        cx.blk.vector(ln_v2)



def phase0_setup(cx, C, dram):
    nc = cx.nc
    es = ExitStack()
    iot = es.enter_context(nc.sbuf_tensor("iot", [128, 128], F32))
    brow = es.enter_context(nc.sbuf_tensor("brow", [1, D], F32))
    onesf = es.enter_context(nc.sbuf_tensor("onesf", [1, 128], F32))
    psb = es.enter_context(nc.psum_tensor("psb", [128, 1024], F32))

    def p0_g(g):
        g.iota(iot[:, :], [[1, 128]], channel_multiplier=-1,
               allow_small_or_imprecise_dtypes=True)
        g.drain()
        cx.inc("s_g", g.drain())
        cx.inc("sd_g", g.dma_start(
            out=AP(C["wqk"], 0, [[C["wqk"].shape[1], 128], [128, ND], [1, 128]]),
            in_=AP(dram["wqk0"], 0, [[128, 128], [128 * 128, ND], [1, 128]])), 16)
        cx.inc("sd_g", g.dma_start(
            out=AP(C["wv"], 0, [[C["wv"].shape[1], 128], [64, ND], [1, 64]]),
            in_=AP(dram["wv0"], 0, [[64, 128], [128 * 64, ND], [1, 64]])), 16)
        cx.inc("sd_g", g.dma_start(out=brow[:, :], in_=dram["bias0"][:, :]), 16)
        cx.inc("sd_g", g.dma_start(out=C["scal"][0:1, 0:1], in_=dram["ws0"][:, :]), 16)
    cx.blk.gpsimd(p0_g)
    C["w0ld"] = cx.val("sd_g")
    ld_done = cx.val("sd_g")

    def p0_v(v):
        cx.wait(v, "s_g", 1)
        v.tensor_single_scalar(C["ident"][:, :], iot[:, :], 0.0, ALU.is_equal)
        v.tensor_single_scalar(C["identf"][:, :], iot[:, :], 0.0, ALU.is_equal)
        v.memset(C["onesb"][:, :], 1.0)
        v.memset(onesf[:, :], 1.0)
        v.memset(C["onesf"][:, :], 1.0)
        cx.inc("s_v", v.drain())
    cx.blk.vector(p0_v)
    C["const_done"] = cx.val("s_v")

    def p0_g2(g):
        cx.wait(g, "s_v", C["const_done"])
        g.iota(iot[:, :], [[1, 128]], channel_multiplier=1,
               allow_small_or_imprecise_dtypes=True)
        g.drain()
        cx.inc("s_g", g.drain())
    cx.blk.gpsimd(p0_g2)

    def p0_v2(v):
        cx.wait(v, "s_g", 2)
        v.tensor_single_scalar(C["identJ"][:, :], iot[:, :], 127.0, ALU.is_equal)
        cx.inc("s_v", v.drain())
    cx.blk.vector(p0_v2)
    C["const_done"] = cx.val("s_v")

    def p0_t(t):
        cx.wait(t, "s_v", C["const_done"])
        cx.wait(t, "sd_g", ld_done)
        for nb in range(2):
            t.matmul(psb[:, nb * 512:(nb + 1) * 512], onesf[:, :],
                     brow[:, nb * 512:(nb + 1) * 512], start=True, stop=True)
        cx.inc("s_t", t.drain())
    cx.blk.tensor(p0_t)
    bias_mm = cx.val("s_t")

    def p0_v2(v):
        cx.wait(v, "s_t", bias_mm)
        v.tensor_copy(C["b0row"][:, :], psb[:, :])
        cx.inc("s_v", v.drain())
    cx.blk.vector(p0_v2)

    def p0_t2(t):
        cx.wait(t, "s_v", cx.val("s_v"))
        t.matmul(psb[:, 0:1], onesf[:, :], C["scal"][0:1, 0:1], start=True, stop=True)
        cx.inc("s_t", t.drain())
    cx.blk.tensor(p0_t2)
    ws_mm = cx.val("s_t")

    def p0_v3(v):
        cx.wait(v, "s_t", ws_mm)
        v.tensor_copy(C["scal"][:, 0:1], psb[:, 0:1])
        cx.inc("s_v", v.drain())
    cx.blk.vector(p0_v3)
    C["bcast_done"] = cx.val("s_v")
    C["const_done"] = cx.val("s_v")
    return es


def phase2_l0_attn(cx, C, dram, xs):
    nc, blk = cx.nc, cx.blk
    stat, ident, identf = C["stat"], C["ident"], C["identf"]

    es_p = ExitStack()
    q2 = es_p.enter_context(nc.sbuf_tensor("q2", [128, T], BF16))
    k2 = es_p.enter_context(nc.sbuf_tensor("k2", [128, T], BF16))
    vtm = es_p.enter_context(nc.sbuf_tensor("vtm", [128, NT * 64], BF16))
    outtm = es_p.enter_context(nc.sbuf_tensor("outtm", [128, NT * 64], F32))

    es_a = ExitStack()
    xb = es_a.enter_context(nc.sbuf_tensor("xb", [128, 8 * 1024], BF16))
    xfm = es_a.enter_context(nc.sbuf_tensor("xfm", [128, 8 * 1024], BF16))
    vfm = es_a.enter_context(nc.sbuf_tensor("vfm", [64, T], BF16))
    pst = es_a.enter_context(nc.psum_tensor("pst", [128, 2048], BF16))
    psqk = es_a.enter_context(nc.psum_tensor("psqk", [128, 1024], F32))
    psv = es_a.enter_context(nc.psum_tensor("psv", [64, 1024], F32))

    for sl in range(4 if SUB2A >= 3 else 1):
        tc0 = sl * 8

        def cast_v(v, sl=sl, tc0=tc0):
            if sl == 0:
                cx.wait(v, "sd_s", C["xs_loaded"])
                cx.wait(v, "s_v", C["const_done"])
            for c in range(8):
                v.tensor_copy(xb[:, c * 1024:(c + 1) * 1024],
                              xs[:, (tc0 + c) * 1024:(tc0 + c + 1) * 1024])
            cx.inc("s_v", v.drain())
        blk.vector(cast_v)
        if SUB2A <= 0:
            break
        emit_tmfm_transpose(cx, xb, xfm, ident, pst, cx.val("s_v"))
        xfm_ready = cx.val("s_v")
        if SUB2A <= 1:
            break

        def proj_t(t, sl=sl, xfm_ready=xfm_ready):
            cx.wait(t, "s_v", xfm_ready)
            if sl == 0:
                cx.wait(t, "sd_g", C["w0ld"])
            for nb in range(2):
                for e in range(ND):
                    t.matmul(psqk[:, nb * 512:(nb + 1) * 512],
                             C["wqk"][:, e * 128:(e + 1) * 128],
                             xfm[:, e * 1024 + nb * 512: e * 1024 + (nb + 1) * 512],
                             start=(e == 0), stop=(e == ND - 1))
            cx.inc("s_t", t.drain())
            for nb in range(2):
                for e in range(ND):
                    t.matmul(psv[:, nb * 512:(nb + 1) * 512],
                             C["wv"][:, e * 64:(e + 1) * 64],
                             xfm[:, e * 1024 + nb * 512: e * 1024 + (nb + 1) * 512],
                             start=(e == 0), stop=(e == ND - 1))
            cx.inc("s_t", t.drain())
        blk.tensor(proj_t)
        qk_ready = cx.val("s_t") - 1
        v_ready = cx.val("s_t")

        def evac_v(v, sl=sl, qk_ready=qk_ready, v_ready=v_ready):
            cols = slice(sl * 1024, (sl + 1) * 1024)
            cx.wait(v, "s_t", qk_ready)
            v.tensor_copy(q2[0:64, cols], psqk[0:64, :])
            v.tensor_copy(q2[64:128, cols], psqk[0:64, :])
            v.tensor_copy(k2[0:64, cols], psqk[64:128, :])
            v.tensor_copy(k2[64:128, cols], psqk[64:128, :])
            cx.wait(v, "s_t", v_ready)
            v.tensor_copy(vfm[:, cols], psv[:, :])
            cx.inc("s_v", v.drain())
        blk.vector(evac_v)
    qkv_done = cx.val("s_v")
    if SUB2A <= 3:
        es_a.close(); es_p.close()
        return cx.val("s_v")
    if UPTO <= 2:
        es_a.close(); es_p.close()
        return cx.val("s_v")
    emit_dump(cx, dram, "q2", q2[:, :], [128, T], ("s_v", qkv_done))
    emit_dump(cx, dram, "k2", k2[:, :], [128, T], ("s_v", qkv_done))

    def vT_t(t):
        cx.wait(t, "s_v", qkv_done)
        for grp in range(8):
            if grp >= 2:
                t.wait_ge(cx.h("s_v"), qkv_done + grp - 1)
            for j in range(4):
                tc = grp * 4 + j
                t.transpose(pst[:, (grp % 2) * 1024 + j * 128:(grp % 2) * 1024 + j * 128 + 64],
                            vfm[:, tc * 128:(tc + 1) * 128], ident[0:64, 0:64])
            cx.inc("s_t", t.drain())
    blk.tensor(vT_t)
    vt0 = cx.val("s_t") - 8

    def vT_v(v):
        for grp in range(8):
            cx.wait(v, "s_t", vt0 + grp + 1)
            for j in range(4):
                tc = grp * 4 + j
                v.tensor_copy(vtm[:, tc * 64:(tc + 1) * 64],
                              pst[:, (grp % 2) * 1024 + j * 128:(grp % 2) * 1024 + j * 128 + 64])
            cx.inc("s_v", v.drain())
    blk.vector(vT_v)
    vtm_done = cx.val("s_v")
    emit_dump(cx, dram, "vtm", vtm[:, :], [128, NT * 64], ("s_v", vtm_done))
    es_a.close()

    es_b = ExitStack()
    EA = es_b.enter_context(nc.sbuf_tensor("EA", [128, T], BF16))
    EB = es_b.enter_context(nc.sbuf_tensor("EB", [128, T], BF16))
    vsc = es_b.enter_context(nc.sbuf_tensor("vsc", [128, 128], BF16))
    acc = es_b.enter_context(nc.sbuf_tensor("acc", [64, T], F32))

    es_ps = ExitStack()
    psA = es_ps.enter_context(nc.psum_tensor("psA", [128, 1024], F32))
    psB = es_ps.enter_context(nc.psum_tensor("psB", [128, 1024], F32))
    pso = es_ps.enter_context(nc.psum_tensor("pso", [64, 2048], F32))

    for kp in range(16):
        ca, cb = 2 * kp, 2 * kp + 1

        def qkmm(t, isp, ca=ca, cb=cb):
            for nb in range(2):
                t.matmul(psA[:, nb * 512:(nb + 1) * 512],
                         k2[0:64, ca * 128:(ca + 1) * 128],
                         q2[0:64, isp * 1024 + nb * 512: isp * 1024 + (nb + 1) * 512],
                         start=True, stop=True, tile_position=(0, 0))
                t.matmul(psB[:, nb * 512:(nb + 1) * 512],
                         k2[64:128, cb * 128:(cb + 1) * 128],
                         q2[64:128, isp * 1024 + nb * 512: isp * 1024 + (nb + 1) * 512],
                         start=True, stop=True, tile_position=(64, 0))

        last_e = cx.val("s_a")
        svA = cx.val("s_v")
        tA0 = cx.val("s_t")

        def pA_t(t, kp=kp, qkmm=qkmm, last_e=last_e, svA=svA):
            if kp == 0:
                cx.wait(t, "s_v", vtm_done)
            else:
                cx.wait(t, "s_a", last_e)
            for isp in range(4):
                if isp >= 1:
                    t.wait_ge(cx.h("s_v"), svA + isp)
                qkmm(t, isp)
                cx.inc("s_t", t.drain())
        blk.tensor(pA_t)

        def pA_v(v, kp=kp, tA0=tA0):
            for isp in range(4):
                cx.wait(v, "s_t", tA0 + isp + 1)
                v.tensor_reduce(stat[:, isp:isp + 1], psA[:, :], AX.X, ALU.max)
                v.tensor_reduce(stat[:, 4 + isp:5 + isp], psB[:, :], AX.X, ALU.max)
                cx.inc("s_v", v.drain())
            v.tensor_reduce(stat[:, 8:9], stat[:, 0:4], AX.X, ALU.max, negate=True)
            v.tensor_reduce(stat[:, 9:10], stat[:, 4:8], AX.X, ALU.max, negate=True)
            cx.inc("s_v", v.drain())
        blk.vector(pA_v)
        mA_done = cx.val("s_v")

        tB0 = cx.val("s_t")
        saB = cx.val("s_a")

        def pB_t(t, kp=kp, qkmm=qkmm, mA_done=mA_done, saB=saB):
            cx.wait(t, "s_v", mA_done)
            for isp in range(4):
                if isp >= 1:
                    t.wait_ge(cx.h("s_a"), saB + isp)
                qkmm(t, isp)
                cx.inc("s_t", t.drain())
        blk.tensor(pB_t)

        def pB_s(s, kp=kp, tB0=tB0):
            for isp in range(4):
                cx.wait(s, "s_t", tB0 + isp + 1)
                s.activation(EA[:, isp * 1024:(isp + 1) * 1024], psA[:, :],
                             AF.Exp, bias=stat[:, 8:9], scale=1.0,
                             accum_out=stat[:, 10 + isp:11 + isp])
                s.activation(EB[:, isp * 1024:(isp + 1) * 1024], psB[:, :],
                             AF.Exp, bias=stat[:, 9:10], scale=1.0,
                             accum_out=stat[:, 14 + isp:15 + isp])
                cx.inc("s_a", s.drain())
        blk.scalar(pB_s)
        e_done = cx.val("s_a")

        def lrec_v(v, kp=kp, ca=ca, e_done=e_done):
            cx.wait(v, "s_a", e_done)
            v.reduce_sum(stat[:, 18:19], stat[:, 10:14], AX.X)
            v.reduce_sum(stat[:, 19:20], stat[:, 14:18], AX.X)
            v.drain()
            v.reciprocal(stat[:, 20:21], stat[:, 18:19])
            v.reciprocal(stat[:, 21:22], stat[:, 19:20])
            v.drain()
            v.tensor_scalar(vsc[:, 0:64], vtm[:, ca * 64:(ca + 1) * 64],
                            stat[:, 20:21], None, ALU.mult)
            v.tensor_scalar(vsc[:, 64:128], vtm[:, (ca + 1) * 64:(ca + 2) * 64],
                            stat[:, 21:22], None, ALU.mult)
            cx.inc("s_v", v.drain())
        blk.vector(lrec_v)
        vsc_ready = cx.val("s_v")

        def pv_t(t, kp=kp, vsc_ready=vsc_ready):
            cx.wait(t, "s_v", vsc_ready)
            for ih in range(2):
                if ih == 1:
                    t.wait_ge(cx.h("s_v"), vsc_ready + 1)
                for nb in range(4):
                    t.matmul(pso[:, nb * 512:(nb + 1) * 512], vsc[:, 0:64],
                             EA[:, ih * 2048 + nb * 512: ih * 2048 + (nb + 1) * 512],
                             start=True, stop=False)
                    t.matmul(pso[:, nb * 512:(nb + 1) * 512], vsc[:, 64:128],
                             EB[:, ih * 2048 + nb * 512: ih * 2048 + (nb + 1) * 512],
                             start=False, stop=True)
                cx.inc("s_t", t.drain())
        blk.tensor(pv_t)
        pvh0 = cx.val("s_t") - 2

        def pvacc_v(v, kp=kp, pvh0=pvh0):
            for ih in range(2):
                cx.wait(v, "s_t", pvh0 + ih + 1)
                if kp == 0:
                    v.tensor_copy(acc[:, ih * 2048:(ih + 1) * 2048], pso[:, :])
                else:
                    v.tensor_add(acc[:, ih * 2048:(ih + 1) * 2048],
                                 acc[:, ih * 2048:(ih + 1) * 2048], pso[:, :])
                cx.inc("s_v", v.drain())
        blk.vector(pvacc_v)
    acc_done = cx.val("s_v")
    if UPTO <= 3:
        es_ps.close(); es_b.close(); es_p.close()
        return cx.val("s_v")
    emit_dump(cx, dram, "acc", acc[:, :], [64, T], ("s_v", acc_done))
    emit_dump(cx, dram, "EA15", EA[:, :], [128, T], ("s_v", acc_done))
    emit_dump(cx, dram, "stat22", stat[:, 0:22], [128, 22], ("s_v", acc_done))
    es_ps.close()

    with nc.psum_tensor("pst2", [128, 1024], F32) as pst2:
        def oT_t(t):
            cx.wait(t, "s_v", acc_done)
            for grp in range(8):
                if grp >= 2:
                    t.wait_ge(cx.h("s_v"), acc_done + grp - 1)
                for j in range(4):
                    tc = grp * 4 + j
                    t.transpose(pst2[:, (grp % 2) * 512 + j * 128:(grp % 2) * 512 + j * 128 + 64],
                                acc[:, tc * 128:(tc + 1) * 128], identf[0:64, 0:64])
                cx.inc("s_t", t.drain())
        blk.tensor(oT_t)
        ot0 = cx.val("s_t") - 8

        def oT_v(v):
            for grp in range(8):
                cx.wait(v, "s_t", ot0 + grp + 1)
                for j in range(4):
                    tc = grp * 4 + j
                    v.tensor_copy(outtm[:, tc * 64:(tc + 1) * 64],
                                  pst2[:, (grp % 2) * 512 + j * 128:(grp % 2) * 512 + j * 128 + 64])
                cx.inc("s_v", v.drain())
        blk.vector(oT_v)
    outtm_done = cx.val("s_v")
    emit_dump(cx, dram, "outtm", outtm[:, :], [128, NT * 64], ("s_v", outtm_done))
    es_b.close()

    def resid_v(v):
        cx.wait(v, "s_v", outtm_done)
        for tc in range(NT):
            v.tensor_add(xs[:, tc * 1024:(tc + 1) * 1024],
                         xs[:, tc * 1024:(tc + 1) * 1024],
                         AP(outtm, tc * 64, [[NT * 64, 128], [0, 16], [1, 64]]))
        cx.inc("s_v", v.drain())
    blk.vector(resid_v)
    emit_ln(cx, xs, stat, ("s_v", cx.val("s_v")))
    emit_dump(cx, dram, "x1", xs[:, :], [128, NT * 1024], ("s_v", cx.val("s_v")))
    es_p.close()
    return cx.val("s_v")


def phase3_l0_ffn(cx, C, dram, xs, ln0_done, safe_prefetch_sv):
    nc, blk = cx.nc, cx.blk
    stat = C["stat"]

    es = ExitStack()
    xqb = es.enter_context(nc.sbuf_tensor("xqb", [128, 8 * 1024], BF16))
    xqf = es.enter_context(nc.sbuf_tensor("xqf", [128, 8 * 1024], BF16))
    w0full = es.enter_context(nc.sbuf_tensor("w0full", [128, ND * 1024], BF16))
    ffo = es.enter_context(nc.sbuf_tensor("ffo", [128, 1024], F32))
    qsc = es.enter_context(nc.sbuf_tensor("qsc", [128, 1024], F32))
    pst3 = es.enter_context(nc.psum_tensor("pst3", [128, 2048], BF16))
    psf = es.enter_context(nc.psum_tensor("psf", [128, 2048], F32))

    def w0_g(g):
        cx.wait(g, "s_v", safe_prefetch_sv)
        if DBG:
            cx.wait(g, "sd_o2")
        for e in range(ND):
            cx.inc("sd_g", g.dma_start(
                out=w0full[:, e * 1024:(e + 1) * 1024],
                in_=dram["wint0"][e * 128:(e + 1) * 128, :]), 16)
    blk.gpsimd(w0_g)
    w0_loaded = cx.val("sd_g")

    for sl in range(4):
        tc0 = sl * 8

        def quant_v(v, sl=sl, tc0=tc0):
            if sl == 0:
                cx.wait(v, "s_v", ln0_done)
                cx.wait(v, "s_v", C["bcast_done"])
            if DBG:
                cx.wait(v, "sd_o2")
            for c in range(8):
                xc = xs[:, (tc0 + c) * 1024:(tc0 + c + 1) * 1024]
                v.scalar_tensor_tensor(xqb[:, c * 1024:(c + 1) * 1024], xc, 1.0, xc,
                                       ALU.mult, ALU.mult,
                                       accum_out=stat[:, 32 + c:33 + c])
                v.tensor_reduce(stat[:, 40 + c:41 + c], xc, AX.X, ALU.max,
                                apply_absolute_value=True)
            v.drain()
            v.tensor_scalar(stat[:, 48:56], stat[:, 32:40], 1.0 / D, 1e-6,
                            ALU.mult, ALU.add)
            v.drain()
            v.reciprocal(stat[:, 56:64], stat[:, 48:56])
            cx.inc("s_v", v.drain())
        blk.vector(quant_v)
        st_ready = cx.val("s_v")

        def rr_s(s, st_ready=st_ready):
            cx.wait(s, "s_v", st_ready)
            s.activation(stat[:, 64:72], stat[:, 56:64], AF.Sqrt)
            cx.inc("s_a", s.drain())
        blk.scalar(rr_s)
        rr_ready = cx.val("s_a")

        def quant2_v(v, sl=sl, tc0=tc0, rr_ready=rr_ready):
            cx.wait(v, "s_a", rr_ready)
            v.tensor_tensor(stat[:, 72:80], stat[:, 40:48], stat[:, 64:72], ALU.mult)
            v.drain()
            v.tensor_scalar_max(stat[:, 72:80], stat[:, 72:80], 1e-5)
            v.drain()
            v.reciprocal(stat[:, 80:88], stat[:, 72:80])
            v.drain()
            v.tensor_scalar_mul(stat[:, 80:88], stat[:, 80:88], 127.0)
            v.drain()
            v.tensor_tensor(stat[:, 88:96], stat[:, 64:72], stat[:, 80:88], ALU.mult)
            v.drain()
            for c in range(8):
                xc = xs[:, (tc0 + c) * 1024:(tc0 + c + 1) * 1024]
                v.tensor_scalar(qsc[:, :], xc,
                                stat[:, 88 + c:89 + c], MAGIC, ALU.mult, ALU.add)
                v.drain()
                v.tensor_scalar(xqb[:, c * 1024:(c + 1) * 1024], qsc[:, :],
                                MAGIC, 127.0, ALU.subtract, ALU.min)
                v.drain()
                v.tensor_scalar_max(xqb[:, c * 1024:(c + 1) * 1024],
                                    xqb[:, c * 1024:(c + 1) * 1024], -128.0)
                v.drain()
            v.reciprocal(stat[:, 96:104], stat[:, 80:88])
            v.drain()
            v.tensor_scalar(stat[:, 96:104], stat[:, 96:104],
                            C["scal"][:, 0:1], None, ALU.mult)
            cx.inc("s_v", v.drain())
        blk.vector(quant2_v)
        if sl == 0:
            emit_dump(cx, dram, "xqb0", xqb[:, :], [128, 8 * 1024], ("s_v", cx.val("s_v")))
            emit_dump(cx, dram, "qstat0", stat[:, 32:104], [128, 72], ("s_v", cx.val("s_v")))
        emit_tmfm_transpose(cx, xqb, xqf, C["ident"], pst3, cx.val("s_v"))
        xqf_ready = cx.val("s_v")

        for c in range(8):
            tc = tc0 + c
            half = c % 2
            prev_resid = cx.val("s_v")

            def mm_t(t, sl=sl, c=c, half=half, xqf_ready=xqf_ready,
                     prev_resid=prev_resid):
                if c == 0:
                    cx.wait(t, "s_v", xqf_ready)
                    if sl == 0:
                        cx.wait(t, "sd_g", w0_loaded)
                if c >= 2:
                    cx.wait(t, "s_v", prev_resid)
                for nb in range(2):
                    for e in range(ND):
                        t.matmul(psf[:, half * 1024 + nb * 512: half * 1024 + (nb + 1) * 512],
                                 xqf[:, e * 1024 + c * 128: e * 1024 + (c + 1) * 128],
                                 w0full[:, e * 1024 + nb * 512: e * 1024 + (nb + 1) * 512],
                                 start=(e == 0), stop=(e == ND - 1))
                cx.inc("s_t", t.drain())
            blk.tensor(mm_t)
            mm_done = cx.val("s_t")

            def epi_v(v, c=c, tc=tc, half=half, mm_done=mm_done):
                cx.wait(v, "s_t", mm_done)
                v.scalar_tensor_tensor(ffo[:, :], psf[:, half * 1024:(half + 1) * 1024],
                                       stat[:, 96 + c:97 + c], C["b0row"][:, :],
                                       ALU.mult, ALU.add)
                v.drain()
                v.tensor_add(xs[:, tc * 1024:(tc + 1) * 1024],
                             xs[:, tc * 1024:(tc + 1) * 1024], ffo[:, :])
                cx.inc("s_v", v.drain())
            blk.vector(epi_v)

    emit_ln(cx, xs, stat, ("s_v", cx.val("s_v")))
    es.close()
    return cx.val("s_v")



def build_program():
    nc = bass.Bass("TRN2", target_bir_lowering=False)

    def DI(name, shape, dt=F32):
        return nc.dram_tensor(name, list(shape), dt, kind="ExternalInput")

    dram = dict(
        x0=DI("x0", (T, D)),
        wqk0=DI("wqk0", (D, 128)),
        wv0=DI("wv0", (D, 64)),
        wint0=DI("wint0", (D, D), I32),
        ws0=DI("ws0", (1, 1)),
        bias0=DI("bias0", (1, D)),
        wqkv=DI("wqkv", (D, 3072)),
        bqkv=DI("bqkv", (1, 3072)),
        wg1=DI("wg1", (D, D)),
        bg1=DI("bg1", (1, D)),
        relpos_t=DI("relpos_t", (16, 512)),
        w1=DI("w1", (D, 4096)),
        b1=DI("b1", (1, 4096)),
        w2=DI("w2", (4096, 16)),
        b2=DI("b2", (1, 16)),
        wo=DI("wo", (D, D)),
        bo=DI("bo", (1, D)),
        clips=DI("clips", (1, 2)),
        wgn=DI("wgn", (2816, D)),
        wun=DI("wun", (2816, D)),
        wdn=DI("wdn", (D, 2816)),
    )
    out = nc.dram_tensor("out", [T, D], F32, kind="ExternalOutput")
    x1s = nc.dram_tensor("x1s", [T, D], F32)
    x2s = nc.dram_tensor("x2s", [T, D], F32)
    hs_scr = nc.dram_tensor("hs_scr", [2816, T], BF16)
    wgq = nc.dram_tensor("wgq", [2816, D], BF16)
    wuq = nc.dram_tensor("wuq", [2816, D], BF16)
    wdq = nc.dram_tensor("wdq", [D, 2816], BF16)
    wgT = nc.dram_tensor("wgT", [D, 2816], BF16)
    wuT = nc.dram_tensor("wuT", [D, 2816], BF16)
    wdT = nc.dram_tensor("wdT", [2816, D], BF16)
    cc1_in = nc.dram_tensor("cc1_in", [1, 16], F32)
    cc1_out = nc.dram_tensor("cc1_out", [1, 16], F32)
    cc2_in = nc.dram_tensor("cc2_in", [1, 16], F32)
    cc2_out = nc.dram_tensor("cc2_out", [1, 16], F32)
    q_scr = nc.dram_tensor("q_scr", [D, T], BF16)
    g_scr = nc.dram_tensor("g_scr", [D, T], BF16)
    k_scr = nc.dram_tensor("k_scr", [D, T], BF16)
    rtil_scr = nc.dram_tensor("rtil_scr", [16, 1792], BF16)
    scr = dict(x1s=x1s, x2s=x2s, hs=hs_scr, wgq=wgq, wuq=wuq, wdq=wdq,
               wgT=wgT, wuT=wuT, wdT=wdT, cc1_in=cc1_in, cc1_out=cc1_out,
               cc2_in=cc2_in, cc2_out=cc2_out, out=out,
               q_scr=q_scr, g_scr=g_scr, k_scr=k_scr, rtil=rtil_scr)

    es_top = ExitStack()
    with es_top:
        blk = es_top.enter_context(nc.Block())
        cx = Ctx(nc, blk)
        for s in ["sd_g", "sd_s", "sd_o", "sd_o2", "s_t", "s_v", "s_a", "s_g",
                  "sq0", "sq1", "st0", "st1", "gx0", "gx1", "gw0", "gw1"]:
            cx.sem(s)

        sb = lambda n, sh, dt: es_top.enter_context(nc.sbuf_tensor(n, sh, dt))
        C = dict(
            ident=sb("ident", [128, 128], BF16),
            identJ=sb("identJ", [128, 128], BF16),
            identf=sb("identf", [128, 128], F32),
            onesb=sb("onesb", [1, 128], BF16),
            onesf=sb("onesf2", [1, 128], F32),
            scal=sb("scal", [128, 16], F32),
            stat=sb("stat", [128, 192], F32),
        )
        es_l0 = ExitStack()
        C["b0row"] = es_l0.enter_context(nc.sbuf_tensor("b0row", [128, D], F32))
        C["wqk"] = es_l0.enter_context(nc.sbuf_tensor("wqk", [128, ND * 128], BF16))
        C["wv"] = es_l0.enter_context(nc.sbuf_tensor("wv", [128, ND * 64], BF16))

        es0 = phase0_setup(cx, C, dram)
        es0.close()

        es_xs = ExitStack()
        xs = es_xs.enter_context(nc.sbuf_tensor("xs", [128, NT * 1024], F32))

        def p1_s(s):
            cx.wait(s, "s_v", C["const_done"])
            cx.wait(s, "s_v", C["bcast_done"])
            for i in range(8):
                c0 = i * 4
                cx.inc("sd_s", s.dma_start(
                    out=xs[:, c0 * 1024:(c0 + 4) * 1024],
                    in_=AP(dram["x0"], c0 * 128 * D, [[D, 128], [128 * D, 4], [1, D]])), 16)
        blk.sync(p1_s)
        C["xs_loaded"] = cx.val("sd_s")

        if UPTO >= 2:
            ln0_done = phase2_l0_attn(cx, C, dram, xs)
        else:
            ln0_done = cx.val("s_v")
        if UPTO >= 5:
            ln1_done = phase3_l0_ffn(cx, C, dram, xs, ln0_done,
                                     safe_prefetch_sv=ln0_done - 2)
        else:
            ln1_done = ln0_done

        def pout_s(s):
            cx.wait(s, "s_v", ln1_done)
            for i in range(8):
                c0 = i * 4
                cx.inc("sd_o", s.dma_start(
                    out=AP(x1s, c0 * 128 * D, [[D, 128], [128 * D, 4], [1, D]]),
                    in_=xs[:, c0 * 1024:(c0 + 4) * 1024]), 16)
                if UPTO < 6:
                    cx.inc("sd_o", s.dma_start(
                        out=AP(out, c0 * 128 * D, [[D, 128], [128 * D, 4], [1, D]]),
                        in_=xs[:, c0 * 1024:(c0 + 4) * 1024]), 16)
            cx.wait(s, "sd_o")
        blk.sync(pout_s)
        x1s_done = cx.val("sd_o")

        if DBG:
            def dbg_fin(g):
                cx.wait(g, "sd_o2")
            blk.gpsimd(dbg_fin)

        es_xs.close()
        es_l0.close()

        if UPTO >= 6:
            wq_done = phase_wprep(cx, C, dram, scr)
            l1 = phase4_l1a(cx, C, dram, scr, x1s_done)
            if UPTO >= 7:
                x2 = phase5_l1b(cx, C, dram, scr, l1, wq_done,
                                write_out_dbg=(UPTO == 7))
                if UPTO >= 8:
                    l1["es"].close()
                    phase6_l1ffn(cx, C, dram, scr, l1, x2, wq_done)
                else:
                    l1["es"].close()

                    def fin_o(s):
                        cx.wait(s, "sd_o")
                        cx.wait(s, "sd_s")
                        cx.wait(s, "sd_g")
                    blk.sync(fin_o)
            else:
                l1["es"].close()

                def fin_o(s):
                    cx.wait(s, "sd_o")
                    cx.wait(s, "sd_s")
                    cx.wait(s, "sd_g")
                blk.sync(fin_o)
    return nc


def kernel(L0_attn_x, L0_attn_Wq, L0_attn_Wk, L0_attn_Wv, L0_ffn_w_int8,
           L0_ffn_weight_scale, L0_ffn_bias, L1_attn_Wqkv, L1_attn_bqkv,
           L1_attn_Wg, L1_attn_bg, L1_attn_relpos_emb, L1_attn_W1, L1_attn_b1,
           L1_attn_W2, L1_attn_b2, L1_attn_Wo, L1_attn_bo, L1_ffn_clip_x,
           L1_ffn_clip_down, L1_ffn_w_gate, L1_ffn_w_up, L1_ffn_w_down):
    a = lambda x: np.ascontiguousarray(np.asarray(x))
    x = a(L0_attn_x)

    shared = dict(
        wqk0=a(np.concatenate([np.asarray(L0_attn_Wq).T, np.asarray(L0_attn_Wk).T], axis=1)),
        wv0=a(np.asarray(L0_attn_Wv).T),
        wint0=a(np.asarray(L0_ffn_w_int8).T),
        ws0=a(np.asarray(L0_ffn_weight_scale, dtype=np.float32).reshape(1, 1)),
        bias0=a(np.asarray(L0_ffn_bias).reshape(1, D)),
        wqkv=a(L1_attn_Wqkv),
        bqkv=a(np.asarray(L1_attn_bqkv).reshape(1, 3072)),
        wg1=a(L1_attn_Wg),
        bg1=a(np.asarray(L1_attn_bg).reshape(1, D)),
        relpos_t=a(np.asarray(L1_attn_relpos_emb).T),
        w1=a(L1_attn_W1),
        b1=a(np.asarray(L1_attn_b1).reshape(1, 4096)),
        w2=a(L1_attn_W2),
        b2=a(np.asarray(L1_attn_b2).reshape(1, 16)),
        wo=a(L1_attn_Wo),
        bo=a(np.asarray(L1_attn_bo).reshape(1, D)),
        clips=a(np.concatenate([np.asarray(L1_ffn_clip_x).reshape(1),
                                np.asarray(L1_ffn_clip_down).reshape(1)]).reshape(1, 2)),
        wgn=a(L1_ffn_w_gate),
        wun=a(L1_ffn_w_up),
        wdn=a(L1_ffn_w_down),
    )
    nc = build_program()
    in_maps = [dict(x0=a(x[i]), **shared) for i in range(N_CORES)]
    res = run_bass_kernel_spmd(nc, in_maps, core_ids=list(range(N_CORES)))
    return np.stack([np.asarray(r["out"]) for r in res.results], axis=0)




def phase_wprep(cx, C, dram, scr):
    nc, blk = cx.nc, cx.blk
    es = ExitStack()
    wnat = es.enter_context(nc.sbuf_tensor("wnat", [128, 2 * 2816], F32))
    wtmp = es.enter_context(nc.sbuf_tensor("wtmp", [128, 2816], F32))
    wdeq = es.enter_context(nc.sbuf_tensor("wdeq", [128, 2816], BF16))
    wst = es.enter_context(nc.sbuf_tensor("wst", [128, 4], F32))
    tbuf = es.enter_context(nc.sbuf_tensor("tbuf", [128, 2816], BF16))

    specs = [(dram["wgn"], scr["wgq"], scr["wgT"], 2816, D),
             (dram["wun"], scr["wuq"], scr["wuT"], 2816, D),
             (dram["wdn"], scr["wdq"], scr["wdT"], D, 2816)]

    q_marks = []
    st_marks = []
    ci = 0
    for nat, qn, qt, R, Dc in specs:
        for ch in range(R // 128):
            buf = ci % 2

            def ld_s(s, nat=nat, ch=ch, buf=buf, Dc=Dc, ci=ci):
                if ci >= 2:
                    s.wait_ge(cx.h("s_v"), q_marks[ci - 2])
                cx.inc("sq%d" % buf, s.dma_start(
                    out=wnat[:, buf * 2816:buf * 2816 + Dc],
                    in_=nat[ch * 128:(ch + 1) * 128, :]), 16)
            blk.sync(ld_s)
            ld_done = cx.val("sq%d" % buf)

            def qv(v, buf=buf, Dc=Dc, ld_done=ld_done, ci=ci):
                cx.wait(v, "sq%d" % buf, ld_done)
                if ci >= 1:
                    cx.wait(v, "sd_g", st_marks[ci - 1])
                w = wnat[:, buf * 2816:buf * 2816 + Dc]
                v.tensor_reduce(wst[:, 0:1], w, AX.X, ALU.max,
                                apply_absolute_value=True)
                v.drain()
                v.reciprocal(wst[:, 1:2], wst[:, 0:1])
                v.drain()
                v.tensor_scalar_mul(wst[:, 1:2], wst[:, 1:2], 127.0)
                v.tensor_scalar_mul(wst[:, 2:3], wst[:, 0:1], 1.0 / 127.0)
                v.drain()
                v.tensor_scalar(wtmp[:, 0:Dc], w, wst[:, 1:2], MAGIC,
                                ALU.mult, ALU.add)
                v.drain()
                v.tensor_scalar(wtmp[:, 0:Dc], wtmp[:, 0:Dc], MAGIC, 127.0,
                                ALU.subtract, ALU.min)
                v.drain()
                v.tensor_scalar(wdeq[:, 0:Dc], wtmp[:, 0:Dc], -128.0,
                                wst[:, 2:3], ALU.max, ALU.mult)
                cx.inc("s_v", v.drain())
            blk.vector(qv)
            q_marks.append(cx.val("s_v"))

            def st_g(g, qn=qn, ch=ch, Dc=Dc, qd=q_marks[-1]):
                cx.wait(g, "s_v", qd)
                cx.inc("sd_g", g.dma_start(
                    out=qn[ch * 128:(ch + 1) * 128, :], in_=wdeq[:, 0:Dc]), 16)
            blk.gpsimd(st_g)
            st_marks.append(cx.val("sd_g"))
            ci += 1
    qn_all = cx.val("sd_g")

    tb_marks = []
    for nat, qn, qt, R, Dc in specs:
        for j in range(Dc // 128):
            def trs(s, qn=qn, j=j, R=R, first=(len(tb_marks) == 0)):
                if first:
                    cx.wait(s, "sd_g", qn_all)
                else:
                    cx.wait(s, "sd_g", tb_marks[-1])
                cx.inc("sd_s", s.dma_start_transpose(
                    tbuf[:, 0:R], qn[:, j * 128:(j + 1) * 128]), 16)
            blk.sync(trs)
            tr_done = cx.val("sd_s")

            def tst(g, qt=qt, j=j, R=R, tr_done=tr_done):
                cx.wait(g, "sd_s", tr_done)
                cx.inc("sd_g", g.dma_start(
                    out=qt[j * 128:(j + 1) * 128, :], in_=tbuf[:, 0:R]), 16)
            blk.gpsimd(tst)
            tb_marks.append(cx.val("sd_g"))
    wq_done_g = cx.val("sd_g")

    def wq_fin(g):
        cx.wait(g, "sd_g", wq_done_g)
    blk.gpsimd(wq_fin)
    es.close()
    return wq_done_g


def phase4_l1a(cx, C, dram, scr, x1s_done):
    nc, blk = cx.nc, cx.blk
    stat, ident = C["stat"], C["ident"]
    VW = 1040

    es_l1 = ExitStack()
    vtm = es_l1.enter_context(nc.sbuf_tensor("vtm65", [128, NT * VW], BF16))
    bias3 = es_l1.enter_context(nc.sbuf_tensor("bias3", [128, 48], F32))
    rtil = es_l1.enter_context(nc.sbuf_tensor("rtil", [16, 1792], BF16))
    relsb = es_l1.enter_context(nc.sbuf_tensor("relsb", [16, 512], F32))
    l1st = es_l1.enter_context(nc.sbuf_tensor("l1st", [16, 16], F32))
    qkrow = es_l1.enter_context(nc.sbuf_tensor("qkrow", [1, 64], F32))
    bcol = es_l1.enter_context(nc.sbuf_tensor("bcol", [128, 64], F32))
    kpre = es_l1.enter_context(nc.sbuf_tensor("kpre", [128, 8 * 128], BF16))

    es_a = ExitStack()
    browb = es_a.enter_context(nc.sbuf_tensor("browb", [1, 2048], BF16))
    brow = es_a.enter_context(nc.sbuf_tensor("brow2", [1, 1024], F32))
    wbuf = es_a.enter_context(nc.sbuf_tensor("wbuf", [128, 2 * 4096], BF16))
    xbt = es_a.enter_context(nc.sbuf_tensor("xbt", [128, 2 * 1024], BF16))
    xfm = es_a.enter_context(nc.sbuf_tensor("xfm4", [128, 8 * 1024], BF16))
    st2 = es_a.enter_context(nc.sbuf_tensor("st2", [128, 2 * 1024], BF16))
    pst = es_a.enter_context(nc.psum_tensor("pst4", [128, 2048], BF16))
    psp = es_a.enter_context(nc.psum_tensor("psp", [128, 2048], F32))
    psE = es_a.enter_context(nc.psum_tensor("psE", [128, 1024], F32))

    def s4_g(g):
        cx.wait(g, "sd_o", x1s_done)
        cx.inc("sd_g", g.dma_start(out=browb[:, 0:1024],
                                   in_=dram["bqkv"][:, 2048:3072]), 16)
        cx.inc("sd_g", g.dma_start(out=relsb[:, :], in_=dram["relpos_t"][:, :]), 16)
        cx.inc("sd_g", g.dma_start(
            out=AP(l1st, 0, [[16, 16], [1, 1]]), in_=dram["b2"][:, :]), 16)
        cx.inc("sd_g", g.dma_start(out=browb[:, 1024:2048], in_=dram["bo"][:, :]), 16)
    blk.gpsimd(s4_g)
    s4_ld = cx.val("sd_g")

    def s4_v(v):
        cx.wait(v, "sd_g", s4_ld)
        v.tensor_reduce(l1st[:, 1:2], relsb[:, :], AX.X, ALU.max)
        v.tensor_scalar(rtil[:, 0:638], AP(relsb, 0, [[512, 16], [0, 638]]),
                        1.0, None, ALU.mult)
        v.tensor_copy(rtil[:, 638:1149], relsb[:, 0:511])
        v.tensor_scalar(rtil[:, 1149:1792], AP(relsb, 510, [[512, 16], [0, 643]]),
                        1.0, None, ALU.mult)
        for tc in range(NT):
            v.memset(AP(vtm, tc * VW + 64, [[vtm.shape[1], 128], [65, 16], [1, 1]]), 1.0)
        cx.inc("s_v", v.drain())
    blk.vector(s4_v)
    s4_prep = cx.val("s_v")

    def rt_g(g):
        cx.wait(g, "s_v", s4_prep)
        cx.inc("sd_g", g.dma_start(out=scr["rtil"][:, :], in_=rtil[:, :]), 16)
    blk.gpsimd(rt_g)

    rowsrc = [(dram["bqkv"], 0, 0), (dram["bqkv"], 1024, 8), (dram["bg1"], 0, 16),
              (dram["w1"], -1, 24), (dram["w1"], -1, 28), (dram["w1"], -1, 32),
              (dram["w1"], -1, 36)]
    rowsrc = [(dram["bqkv"], 0, 0), (dram["bqkv"], 1024, 8), (dram["bg1"], 0, 16)]
    prev_tr = None
    sv_br0 = cx.val("s_v")
    for gi, (ten, c0, b0) in enumerate(rowsrc):
        def br_g(g, ten=ten, c0=c0, prev_tr=prev_tr):
            if prev_tr is not None:
                g.wait_ge(cx.h("s_t"), prev_tr)
            else:
                cx.wait(g, "s_v", s4_prep)
            cx.inc("sd_g", g.dma_start(out=brow[:, :], in_=ten[0:1, c0:c0 + 1024]), 16)
        blk.gpsimd(br_g)
        br_ld = cx.val("sd_g")

        def br_t(t, br_ld=br_ld, gi=gi):
            cx.wait(t, "sd_g", br_ld)
            for i in range(8):
                n = gi * 8 + i
                if n >= 2:
                    t.wait_ge(cx.h("s_v"), sv_br0 + n - 1)
                t.transpose(psp[:, (i % 2) * 1024:(i % 2) * 1024 + 1],
                            brow[0:1, i * 128:(i + 1) * 128], C["onesf"][0:1, 0:1])
                cx.inc("s_t", t.drain())
        blk.tensor(br_t)
        tr0 = cx.val("s_t") - 8
        prev_tr = cx.val("s_t")

        def br_v(v, b0=b0, tr0=tr0):
            for i in range(8):
                cx.wait(v, "s_t", tr0 + i + 1)
                v.tensor_copy(bcol[:, b0 + i:b0 + i + 1],
                              psp[:, (i % 2) * 1024:(i % 2) * 1024 + 1])
                cx.inc("s_v", v.drain())
        blk.vector(br_v)

    def s4_v2(v):
        v.tensor_scalar_mul(bcol[:, 0:8], bcol[:, 0:8], 0.125)
        cx.inc("s_v", v.drain())
    blk.vector(s4_v2)
    bcol_done = cx.val("s_v")

    prev_g_store_marks = None
    prev_ring_st = None
    prev_g_done_a = None
    for sl in range(4):
        tc0 = sl * 8

        st_base = cx.val("s_t")
        sv_base = cx.val("s_v")

        xld_marks = []

        def x_g(g, tc0=tc0, st_base=st_base):
            for c in range(8):
                if c >= 2:
                    g.wait_ge(cx.h("s_t"), st_base + (c - 1) * 2)
                xld_marks.append(cx.inc("gx%d" % (c % 2), g.dma_start(
                    out=xbt[:, (c % 2) * 1024:(c % 2 + 1) * 1024],
                    in_=scr["x1s"][(tc0 + c) * 128:(tc0 + c + 1) * 128, :]), 16))
        blk.gpsimd(x_g)

        def x_t(t, sl=sl, sv_base=sv_base, xld_marks=xld_marks):
            if sl == 0:
                cx.wait(t, "s_v", bcol_done)
            for c in range(8):
                t.wait_ge(cx.h("gx%d" % (c % 2)), xld_marks[c])
                for grp in range(2):
                    gi = c * 2 + grp
                    if gi >= 2:
                        t.wait_ge(cx.h("s_v"), sv_base + gi - 1)
                    for j in range(4):
                        dc = grp * 4 + j
                        slot = (gi % 2) * 8 + j
                        t.transpose(pst[:, slot * 128:(slot + 1) * 128],
                                    xbt[:, (c % 2) * 1024 + dc * 128:(c % 2) * 1024 + (dc + 1) * 128],
                                    ident[:, :])
                    cx.inc("s_t", t.drain())
        blk.tensor(x_t)

        def x_v(v, st_base=st_base):
            for gi in range(16):
                c, grp = gi // 2, gi % 2
                cx.wait(v, "s_t", st_base + gi + 1)
                for j in range(4):
                    dc = grp * 4 + j
                    slot = (gi % 2) * 8 + j
                    v.tensor_copy(xfm[:, dc * 1024 + c * 128: dc * 1024 + (c + 1) * 128],
                                  pst[:, slot * 128:(slot + 1) * 128])
                cx.inc("s_v", v.drain())
        blk.vector(x_v)
        xfm_ready = cx.val("s_v")

        wseq = [(dram["wqkv"], 1024), (dram["wqkv"], 1536),
                (dram["wqkv"], 2048), (dram["wqkv"], 2560),
                (dram["wqkv"], 0), (dram["wqkv"], 512),
                (dram["wg1"], 0), (dram["wg1"], 512)]
        incs_per_half = [4, 4, 8, 8, 9, 9, 4, 4] if sl == 0 else [4, 4, 8, 8, 4, 4, 4, 4]
        if sl == 0:
            incs_per_half = [4, 4, 8, 8, 4 + 4 * 3, 4 + 4 * 3, 4, 4]
        proj_st0 = cx.val("s_t")
        half_marks = [proj_st0 + sum(incs_per_half[:i + 1]) for i in range(8)]

        wld_marks = []

        def w_g(g, wseq=wseq, half_marks=half_marks, prs=prev_ring_st):
            for i, (ten, c0) in enumerate(wseq):
                if i >= 2:
                    g.wait_ge(cx.h("s_t"), half_marks[i - 2])
                elif prs is not None:
                    g.wait_ge(cx.h("s_t"), prs)
                wld_marks.append(cx.inc("gw%d" % (i % 2), g.dma_start(
                    out=AP(wbuf, (i % 2) * 4096, [[wbuf.shape[1], 128], [512, 8], [1, 512]]),
                    in_=AP(ten, c0, [[ten.shape[1], 128], [128 * ten.shape[1], 8], [1, 512]])), 16))
        blk.gpsimd(w_g)

        sa_k = cx.val("s_a")

        def k_t(t, xfm_ready=xfm_ready, sa_k=sa_k, wld_marks=wld_marks,
                pgd=prev_g_done_a):
            cx.wait(t, "s_v", xfm_ready)
            if pgd is not None:
                t.wait_ge(cx.h("s_a"), pgd)
            for dc in range(8):
                half = dc // 4
                if dc % 4 == 0:
                    t.wait_ge(cx.h("gw%d" % (half % 2)), wld_marks[half])
                if dc >= 2:
                    t.wait_ge(cx.h("s_a"), sa_k + dc - 1)
                for nb in range(2):
                    for e in range(ND):
                        t.matmul(psp[:, (dc % 2) * 1024 + nb * 512:(dc % 2) * 1024 + (nb + 1) * 512],
                                 wbuf[:, half * 4096 + e * 512 + (dc % 4) * 128: half * 4096 + e * 512 + (dc % 4) * 128 + 128],
                                 xfm[:, e * 1024 + nb * 512: e * 1024 + (nb + 1) * 512],
                                 start=(e == 0), stop=(e == ND - 1))
                cx.inc("s_t", t.drain())
        blk.tensor(k_t)
        kst0 = cx.val("s_t") - 8

        k_store_marks = []
        k_marks_a = []
        for dc in range(8):
            def k_a(s, sl=sl, dc=dc, kst0=kst0, pgs=prev_g_store_marks):
                cx.wait(s, "s_t", kst0 + dc + 1)
                if dc >= 2:
                    cx.wait(s, "st%d" % (dc % 2), k_store_marks[dc - 2])
                elif pgs is not None:
                    cx.wait(s, "st%d" % (dc % 2), pgs[6 + dc])
                s.activation(st2[:, (dc % 2) * 1024:(dc % 2 + 1) * 1024],
                             psp[:, (dc % 2) * 1024:(dc % 2 + 1) * 1024],
                             AF.Identity, bias=bcol[:, 8 + dc:9 + dc], scale=1.0)
                if sl == 0:
                    s.drain()
                    s.copy(kpre[:, dc * 128:(dc + 1) * 128],
                           st2[:, (dc % 2) * 1024:(dc % 2) * 1024 + 128])
                cx.inc("s_a", s.drain())
            blk.scalar(k_a)
            k_marks_a.append(cx.val("s_a"))

            def k_s(s, sl=sl, dc=dc, ka=k_marks_a[-1]):
                cx.wait(s, "s_a", ka)
                cx.inc("st%d" % (dc % 2), s.dma_start(
                    out=scr["k_scr"][dc * 128:(dc + 1) * 128, sl * 1024:(sl + 1) * 1024],
                    in_=st2[:, (dc % 2) * 1024:(dc % 2 + 1) * 1024]), 16)
            blk.sync(k_s)
            k_store_marks.append(cx.val("st%d" % (dc % 2)))
        k_done_a = cx.val("s_a")

        sv_v = cx.val("s_v")

        def v_t(t, sv_v=sv_v, k_done_a=k_done_a, wld_marks=wld_marks):
            cx.wait(t, "s_a", k_done_a)
            for vdb in range(2):
                t.wait_ge(cx.h("gw%d" % ((vdb + 2) % 2)), wld_marks[vdb + 2])
                for tc in range(8):
                    i = vdb * 8 + tc
                    if i >= 2:
                        t.wait_ge(cx.h("s_v"), sv_v + i - 1)
                    t.matmul(psp[:, (i % 2) * 1024:(i % 2) * 1024 + 512],
                             C["onesb"][0:1, 0:128],
                             browb[0:1, vdb * 512:(vdb + 1) * 512],
                             start=True, stop=False)
                    for e in range(ND):
                        t.matmul(psp[:, (i % 2) * 1024:(i % 2) * 1024 + 512],
                                 xfm[:, e * 1024 + tc * 128: e * 1024 + (tc + 1) * 128],
                                 wbuf[:, (vdb % 2) * 4096 + e * 512: (vdb % 2) * 4096 + (e + 1) * 512],
                                 start=False, stop=(e == ND - 1))
                    cx.inc("s_t", t.drain())
        blk.tensor(v_t)
        vst0 = cx.val("s_t") - 16

        def v_v(v, tc0=tc0, vst0=vst0):
            for vdb in range(2):
                for tc in range(8):
                    i = vdb * 8 + tc
                    cx.wait(v, "s_t", vst0 + i + 1)
                    v.tensor_copy(
                        AP(vtm, (tc0 + tc) * VW + vdb * 8 * 65,
                           [[vtm.shape[1], 128], [65, 8], [1, 64]]),
                        AP(psp, (i % 2) * 1024, [[psp.shape[1], 128], [64, 8], [1, 64]]))
                    cx.inc("s_v", v.drain())
        blk.vector(v_v)
        v_done_v = cx.val("s_v")

        q_marks_a = []
        q_store_marks = []
        pre_marks = []
        first_q = True
        for dc in range(8):
            half4 = 4 + dc // 4

            def q_t(t, dc=dc, half4=half4, first_q=first_q,
                    v_done_v=v_done_v, wld_marks=wld_marks):
                if first_q:
                    cx.wait(t, "s_v", v_done_v)
                if dc % 4 == 0:
                    t.wait_ge(cx.h("gw%d" % (half4 % 2)), wld_marks[half4])
                if dc >= 2:
                    cx.wait(t, "s_a", q_marks_a[dc - 2])
                for nb in range(2):
                    for e in range(ND):
                        t.matmul(psp[:, (dc % 2) * 1024 + nb * 512:(dc % 2) * 1024 + (nb + 1) * 512],
                                 wbuf[:, (half4 % 2) * 4096 + e * 512 + (dc % 4) * 128: (half4 % 2) * 4096 + e * 512 + (dc % 4) * 128 + 128],
                                 xfm[:, e * 1024 + nb * 512: e * 1024 + (nb + 1) * 512],
                                 start=(e == 0), stop=(e == ND - 1))
                cx.inc("s_t", t.drain())
            blk.tensor(q_t)
            first_q = False
            q_mm = cx.val("s_t")

            def q_a(s, sl=sl, dc=dc, q_mm=q_mm):
                cx.wait(s, "s_t", q_mm)
                if dc < 2:
                    cx.wait(s, "st%d" % (dc % 2), k_store_marks[6 + dc])
                else:
                    cx.wait(s, "st%d" % (dc % 2), q_store_marks[dc - 2])
                    if sl == 0:
                        cx.wait(s, "s_t", pre_marks[dc - 2])
                s.activation(st2[:, (dc % 2) * 1024:(dc % 2 + 1) * 1024],
                             psp[:, (dc % 2) * 1024:(dc % 2 + 1) * 1024],
                             AF.Identity, bias=bcol[:, dc:dc + 1], scale=0.125)
                cx.inc("s_a", s.drain())
            blk.scalar(q_a)
            q_marks_a.append(cx.val("s_a"))

            def q_s(s, sl=sl, dc=dc, qa=q_marks_a[-1]):
                cx.wait(s, "s_a", qa)
                cx.inc("st%d" % (dc % 2), s.dma_start(
                    out=scr["q_scr"][dc * 128:(dc + 1) * 128, sl * 1024:(sl + 1) * 1024],
                    in_=st2[:, (dc % 2) * 1024:(dc % 2 + 1) * 1024]), 16)
            blk.sync(q_s)
            q_store_marks.append(cx.val("st%d" % (dc % 2)))

            if sl == 0:
                def pre_t(t, dc=dc, qa=q_marks_a[-1]):
                    cx.wait(t, "s_a", qa)
                    t.matmul(psE[:, 0:512], kpre[0:64, dc * 128:(dc + 1) * 128],
                             st2[0:64, (dc % 2) * 1024:(dc % 2) * 1024 + 512],
                             start=True, stop=True, tile_position=(0, 0))
                    t.matmul(psE[:, 512:1024], kpre[64:128, dc * 128:(dc + 1) * 128],
                             st2[64:128, (dc % 2) * 1024:(dc % 2) * 1024 + 512],
                             start=True, stop=True, tile_position=(64, 0))
                    cx.inc("s_t", t.drain())
                blk.tensor(pre_t)
                pre_mm = cx.val("s_t")

                def pre_v(v, pre_mm=pre_mm):
                    cx.wait(v, "s_t", pre_mm)
                    v.tensor_reduce(stat[:, 90:91], psE[:, 0:512], AX.X, ALU.max)
                    v.tensor_reduce(stat[:, 91:92], psE[:, 512:1024], AX.X, ALU.max)
                    cx.inc("s_v", v.drain())
                blk.vector(pre_v)
                pre_red = cx.val("s_v")

                def pre_t2(t, pre_red=pre_red):
                    cx.wait(t, "s_v", pre_red)
                    t.transpose(psE[0:1, 0:128], stat[:, 90:91], C["identf"][:, :])
                    t.transpose(psE[0:1, 512:640], stat[:, 91:92], C["identf"][:, :])
                    cx.inc("s_t", t.drain())
                blk.tensor(pre_t2)
                pre_tr = cx.val("s_t")

                def pre_v2(v, dc=dc, pre_tr=pre_tr):
                    cx.wait(v, "s_t", pre_tr)
                    v.tensor_reduce(qkrow[0:1, 2 * dc:2 * dc + 1], psE[0:1, 0:128],
                                    AX.X, ALU.max)
                    v.tensor_reduce(qkrow[0:1, 2 * dc + 1:2 * dc + 2],
                                    psE[0:1, 512:640], AX.X, ALU.max)
                    cx.inc("s_v", v.drain())
                blk.vector(pre_v2)

                def pre_fin(t, fin=cx.val("s_v")):
                    cx.wait(t, "s_v", fin)
                    cx.inc("s_t", t.drain())
                blk.tensor(pre_fin)
                pre_marks.append(cx.val("s_t"))
            else:
                pre_marks.append(cx.val("s_t"))
        q_done_a = cx.val("s_a")

        g_marks_a = []
        g_store_marks = []
        for dc in range(8):
            half6 = 6 + dc // 4

            def g_t(t, dc=dc, half6=half6, q_done_a=q_done_a,
                    wld_marks=wld_marks):
                if dc == 0:
                    cx.wait(t, "s_a", q_done_a)
                if dc % 4 == 0:
                    t.wait_ge(cx.h("gw%d" % (half6 % 2)), wld_marks[half6])
                if dc >= 2:
                    cx.wait(t, "s_a", g_marks_a[dc - 2])
                for nb in range(2):
                    for e in range(ND):
                        t.matmul(psp[:, (dc % 2) * 1024 + nb * 512:(dc % 2) * 1024 + (nb + 1) * 512],
                                 wbuf[:, (half6 % 2) * 4096 + e * 512 + (dc % 4) * 128: (half6 % 2) * 4096 + e * 512 + (dc % 4) * 128 + 128],
                                 xfm[:, e * 1024 + nb * 512: e * 1024 + (nb + 1) * 512],
                                 start=(e == 0), stop=(e == ND - 1))
                cx.inc("s_t", t.drain())
            blk.tensor(g_t)
            g_mm = cx.val("s_t")

            def g_a(s, g_mm=g_mm, dc=dc):
                cx.wait(s, "s_t", g_mm)
                if dc < 2:
                    cx.wait(s, "st%d" % (dc % 2), q_store_marks[6 + dc])
                else:
                    cx.wait(s, "st%d" % (dc % 2), g_store_marks[dc - 2])
                s.activation(st2[:, (dc % 2) * 1024:(dc % 2 + 1) * 1024],
                             psp[:, (dc % 2) * 1024:(dc % 2 + 1) * 1024],
                             AF.Sigmoid, bias=bcol[:, 16 + dc:17 + dc], scale=1.0)
                cx.inc("s_a", s.drain())
            blk.scalar(g_a)
            g_marks_a.append(cx.val("s_a"))

            def g_s(s, sl=sl, dc=dc, ga=g_marks_a[-1]):
                cx.wait(s, "s_a", ga)
                cx.inc("st%d" % (dc % 2), s.dma_start(
                    out=scr["g_scr"][dc * 128:(dc + 1) * 128, sl * 1024:(sl + 1) * 1024],
                    in_=st2[:, (dc % 2) * 1024:(dc % 2 + 1) * 1024]), 16)
            blk.sync(g_s)
            g_store_marks.append(cx.val("st%d" % (dc % 2)))
        g_done_a = cx.val("s_a")
        prev_g_store_marks = g_store_marks
        prev_g_done_a = g_done_a

        ring_done_st = cx.val("s_t")

        if sl == 0:
            last = cx.val("s_v")

            def c0_t(t, last=last):
                cx.wait(t, "s_v", last)
                t.transpose(psE[0:16, 0:1], qkrow[0:1, 0:16], C["onesf"][0:1, 0:1])
                cx.inc("s_t", t.drain())
            blk.tensor(c0_t)
            c0_tr = cx.val("s_t")

            def c0_v(v, c0_tr=c0_tr):
                cx.wait(v, "s_t", c0_tr)
                v.tensor_copy(l1st[:, 3:4], psE[0:16, 0:1])
                v.drain()
                v.tensor_add(l1st[:, 4:5], l1st[:, 3:4], l1st[:, 1:2])
                v.drain()
                v.tensor_scalar_mul(l1st[:, 5:6], l1st[:, 4:5], -1.0)
                v.tensor_sub(l1st[:, 6:7], relsb[:, 0:1], l1st[:, 4:5])
                v.tensor_sub(l1st[:, 7:8], relsb[:, 510:511], l1st[:, 4:5])
                cx.inc("s_v", v.drain())
            blk.vector(c0_v)
            b3m = cx.val("s_v")

            def b3_t(t, b3m=b3m):
                cx.wait(t, "s_v", b3m)
                for vv in range(3):
                    t.transpose(psE[0:1, 512 + vv * 16:512 + vv * 16 + 16],
                                l1st[:, 5 + vv:6 + vv], C["identf"][0:16, 0:16])
                cx.inc("s_t", t.drain())
            blk.tensor(b3_t)
            b3t = cx.val("s_t")

            def b3_v(v, b3t=b3t):
                cx.wait(v, "s_t", b3t)
                v.tensor_copy(qkrow[0:1, 16:64], psE[0:1, 512:560])
                cx.inc("s_v", v.drain())
            blk.vector(b3_v)
            b3v = cx.val("s_v")

            def b3_t2(t, b3v=b3v):
                cx.wait(t, "s_v", b3v)
                t.matmul(psE[:, 0:48], C["onesf"][0:1, 0:128], qkrow[0:1, 16:64],
                         start=True, stop=True)
                cx.inc("s_t", t.drain())
            blk.tensor(b3_t2)
            b3t2 = cx.val("s_t")

            def b3_v2(v, b3t2=b3t2):
                cx.wait(v, "s_t", b3t2)
                v.tensor_copy(bias3[:, :], psE[:, 0:48])
                cx.inc("s_v", v.drain())
            blk.vector(b3_v2)

        prev_ring_st = ring_done_st

    done_v = cx.val("s_v")
    done_s = cx.val("sd_s")
    done_a = cx.val("s_a")
    done_st = (cx.val("st0"), cx.val("st1"))
    es_a.close()
    return dict(es=es_l1, vtm=vtm, bias3=bias3, rtil=rtil,
                relsb=relsb, l1st=l1st, VW=VW,
                done_v=done_v, done_s=done_s, done_a=done_a, done_st=done_st)


def phase5_l1b(cx, C, dram, scr, l1, wq_done_g, write_out_dbg=False):
    nc, blk = cx.nc, cx.blk
    stat, ident = C["stat"], C["ident"]
    vtm, bias3 = l1["vtm"], l1["bias3"]
    rtil, relsb, l1st = l1["rtil"], l1["relsb"], l1["l1st"]
    VW = l1["VW"]

    es = ExitStack()
    kbuf = es.enter_context(nc.sbuf_tensor("kbuf", [128, 2 * T], BF16))
    qsl = es.enter_context(nc.sbuf_tensor("qsl", [128, 8 * 512], BF16))
    gsl = es.enter_context(nc.sbuf_tensor("gsl", [128, 8 * 512], BF16))
    mbuf = es.enter_context(nc.sbuf_tensor("mbuf", [128, 2 * 1664], BF16))
    EA5 = es.enter_context(nc.sbuf_tensor("EA5", [128, 2 * 512], BF16))
    EB5 = es.enter_context(nc.sbuf_tensor("EB5", [128, 2 * 512], BF16))
    gated = es.enter_context(nc.sbuf_tensor("gated", [128, 8 * 512], BF16))
    pvB = es.enter_context(nc.sbuf_tensor("pvB", [64, 512], F32))
    lr65 = es.enter_context(nc.sbuf_tensor("lr65", [65, 1024], F32))
    ones65 = es.enter_context(nc.sbuf_tensor("ones65", [65, 64], F32))
    shid = es.enter_context(nc.sbuf_tensor("shid", [64, 128], F32))
    rbc = es.enter_context(nc.sbuf_tensor("rbc", [128, 512], F32))
    grbc = es.enter_context(nc.sbuf_tensor("grbc", [128, 512], F32))
    borow = es.enter_context(nc.sbuf_tensor("borow", [1, 1024], BF16))
    wo64 = es.enter_context(nc.sbuf_tensor("wo64", [128, 8 * 1024], BF16))
    x1c = es.enter_context(nc.sbuf_tensor("x1c", [128, 2 * 1024], F32))
    lnx = es.enter_context(nc.sbuf_tensor("lnx", [128, 4 * 1024], F32))
    ln5q = es.enter_context(nc.sbuf_tensor("ln5q", [128, 1024], BF16))
    outc = es.enter_context(nc.sbuf_tensor("outc5", [128, 1024], F32))
    psA = es.enter_context(nc.psum_tensor("psA5", [128, 1024], F32))
    psB = es.enter_context(nc.psum_tensor("psB5", [128, 1024], F32))
    pso = es.enter_context(nc.psum_tensor("pso5", [65, 1024], F32))

    def wo_g(g):
        cx.wait(g, "s_v", l1["done_v"])
        cx.wait(g, "st0", l1["done_st"][0])
        cx.wait(g, "st1", l1["done_st"][1])
        cx.wait(g, "sd_g", wq_done_g)
        cx.inc("sd_g", g.dma_start(out=borow[:, :], in_=dram["bo"][:, :]), 16)
        for e in range(ND):
            cx.inc("sd_g", g.dma_start(
                out=wo64[:, e * 1024:(e + 1) * 1024],
                in_=dram["wo"][e * 128:(e + 1) * 128, :]), 16)
    blk.gpsimd(wo_g)
    wo_ld = cx.val("sd_g")

    def c5_v(v):
        cx.wait(v, "s_v", l1["done_v"])
        cx.wait(v, "st0", l1["done_st"][0])
        cx.wait(v, "st1", l1["done_st"][1])
        v.memset(ones65[64:65, :], 1.0)
        v.memset(shid[:, :], 0.0)
        v.drain()
        v.tensor_copy(shid[:, 64:128], C["identf"][0:64, 0:64])
        cx.inc("s_v", v.drain())
    blk.vector(c5_v)
    c5_done = cx.val("s_v")

    x2_marks = []
    prev_wo_marks = None
    pair_pp_st = 0
    for sl in range(8):
        prev_sl_st = cx.val("s_t")

        def qg_s(s, sl=sl, prev_sl_st=prev_sl_st):
            s.wait_ge(cx.h("s_t"), prev_sl_st)
            if sl == 0:
                cx.wait(s, "sd_s", l1["done_s"])
                cx.wait(s, "st0", l1["done_st"][0])
                cx.wait(s, "st1", l1["done_st"][1])
            for dc in range(8):
                cx.inc("sd_s", s.dma_start(
                    out=qsl[:, dc * 512:(dc + 1) * 512],
                    in_=scr["q_scr"][dc * 128:(dc + 1) * 128, sl * 512:(sl + 1) * 512]), 16)
            for dc in range(8):
                cx.inc("sd_s", s.dma_start(
                    out=gsl[:, dc * 512:(dc + 1) * 512],
                    in_=scr["g_scr"][dc * 128:(dc + 1) * 128, sl * 512:(sl + 1) * 512]), 16)
        blk.sync(qg_s)
        qg_ld = cx.val("sd_s")

        for p in range(8):
            pair_prev_st = cx.val("s_t")
            sv_epi_prev = cx.val("s_v")

            def me_g(g, sl=sl, p=p, pair_prev_st=pair_prev_st,
                     pp_st=pair_pp_st, first=(sl == 0 and p == 0)):
                if first:
                    cx.wait(g, "st0", l1["done_st"][0])
                    cx.wait(g, "st1", l1["done_st"][1])
                    cx.wait(g, "s_v", c5_done)
                g.wait_ge(cx.h("s_t"), pp_st)
                cx.inc("gw%d" % (p % 2), g.dma_start(
                    out=kbuf[:, (p % 2) * T:(p % 2 + 1) * T],
                    in_=scr["k_scr"][p * 128:(p + 1) * 128, :]), 16)
                g.wait_ge(cx.h("s_t"), pair_prev_st)
                cx.inc("sd_g", g.dma_start(
                    out=mbuf[:, 0:1664],
                    in_=AP(scr["rtil"], (2 * p) * 1792, [[1, 128], [1, 1664]])), 16)
                cx.inc("sd_g", g.dma_start(
                    out=mbuf[:, 1664:3328],
                    in_=AP(scr["rtil"], (2 * p + 1) * 1792, [[1, 128], [1, 1664]])), 16)
            blk.gpsimd(me_g)
            k_ld = cx.val("gw%d" % (p % 2))
            me_ld = cx.val("sd_g")

            st_j0 = cx.val("s_t")
            sa_j0 = cx.val("s_a")

            def pair_t(t, sl=sl, p=p, me_ld=me_ld, k_ld=k_ld, st_j0=st_j0,
                       sa_j0=sa_j0, qg_ld=qg_ld, sv_epi_prev=sv_epi_prev,
                       wo_ld=wo_ld):
                cx.wait(t, "sd_g", me_ld)
                t.wait_ge(cx.h("gw%d" % (p % 2)), k_ld)
                cx.wait(t, "sd_s", qg_ld)
                cx.wait(t, "s_v", sv_epi_prev)
                for jc in range(33):
                    if jc < 32:
                        par = jc % 2
                        d0 = sl * 512 - jc * 128
                        nd = -766 < d0 < 382
                        if jc >= 2:
                            t.wait_ge(cx.h("s_a"), sa_j0 + jc - 1)
                        if nd:
                            ms = d0 + 766
                            t.matmul(psA[:, par * 512:(par + 1) * 512],
                                     C["identJ"][:, :], mbuf[:, ms:ms + 512],
                                     start=True, stop=False)
                            t.matmul(psB[:, par * 512:(par + 1) * 512],
                                     C["identJ"][:, :], mbuf[:, 1664 + ms:1664 + ms + 512],
                                     start=True, stop=False)
                        t.matmul(psA[:, par * 512:(par + 1) * 512],
                                 kbuf[0:64, (p % 2) * T + jc * 128: (p % 2) * T + (jc + 1) * 128],
                                 qsl[0:64, p * 512:(p + 1) * 512],
                                 start=not nd, stop=True, tile_position=(0, 0))
                        t.matmul(psB[:, par * 512:(par + 1) * 512],
                                 kbuf[64:128, (p % 2) * T + jc * 128: (p % 2) * T + (jc + 1) * 128],
                                 qsl[64:128, p * 512:(p + 1) * 512],
                                 start=not nd, stop=True, tile_position=(64, 0))
                    if jc >= 1:
                        jp = jc - 1
                        parp = jp % 2
                        t.wait_ge(cx.h("s_a"), sa_j0 + jp + 1)
                        t.matmul(pso[:, 0:512],
                                 vtm[:, jp * VW + (2 * p) * 65: jp * VW + (2 * p + 1) * 65],
                                 EA5[:, parp * 512:(parp + 1) * 512],
                                 start=(jp == 0), stop=(jp == 31))
                        t.matmul(pso[:, 512:1024],
                                 vtm[:, jp * VW + (2 * p + 1) * 65: jp * VW + (2 * p + 2) * 65],
                                 EB5[:, parp * 512:(parp + 1) * 512],
                                 start=(jp == 0), stop=(jp == 31))
                    cx.inc("s_t", t.drain())
            blk.tensor(pair_t)
            pv_all = cx.val("s_t")

            def pair_s(s, sl=sl, p=p, st_j0=st_j0):
                for jc in range(32):
                    par = jc % 2
                    d0 = sl * 512 - jc * 128
                    if -766 < d0 < 382:
                        vv = 0
                    elif d0 >= 382:
                        vv = 2
                    else:
                        vv = 1
                    cx.wait(s, "s_t", st_j0 + jc + 1)
                    s.activation(EA5[:, par * 512:(par + 1) * 512],
                                 psA[:, par * 512:(par + 1) * 512], AF.Exp,
                                 bias=bias3[:, vv * 16 + 2 * p:vv * 16 + 2 * p + 1],
                                 scale=1.0)
                    s.activation(EB5[:, par * 512:(par + 1) * 512],
                                 psB[:, par * 512:(par + 1) * 512], AF.Exp,
                                 bias=bias3[:, vv * 16 + 2 * p + 1:vv * 16 + 2 * p + 2],
                                 scale=1.0)
                    cx.inc("s_a", s.drain())
            blk.scalar(pair_s)

            def ep_v1(v, pv_all=pv_all):
                cx.wait(v, "s_t", pv_all)
                v.tensor_copy(lr65[64:65, 0:1024], pso[64:65, 0:1024])
                v.drain()
                v.reciprocal(lr65[64:65, 0:1024], lr65[64:65, 0:1024])
                v.drain()
                v.tensor_copy(pvB[:, :], pso[0:64, 512:1024])
                cx.inc("s_v", v.drain())
            blk.vector(ep_v1)
            m1 = cx.val("s_v")

            def ep_t1(t, m1=m1):
                cx.wait(t, "s_v", m1)
                t.matmul(psA[0:64, 0:512], ones65[64:65, 0:64],
                         lr65[64:65, 0:512], start=True, stop=True,
                         tile_position=(64, 0))
                t.matmul(psA[64:128, 512:1024], ones65[64:65, 0:64],
                         lr65[64:65, 512:1024], start=True, stop=True,
                         tile_position=(64, 64))
                t.matmul(psB[:, 0:512], shid[:, :], pvB[:, :],
                         start=True, stop=True)
                cx.inc("s_t", t.drain())
            blk.tensor(ep_t1)
            m2 = cx.val("s_t")

            def ep_v2(v, p=p, m2=m2):
                cx.wait(v, "s_t", m2)
                v.tensor_copy(rbc[0:64, :], psA[0:64, 0:512])
                v.tensor_copy(rbc[64:128, :], psA[64:128, 512:1024])
                v.drain()
                v.tensor_tensor(grbc[0:64, :], gsl[0:64, p * 512:(p + 1) * 512],
                                rbc[0:64, :], ALU.mult)
                v.tensor_tensor(grbc[64:128, :], gsl[64:128, p * 512:(p + 1) * 512],
                                rbc[64:128, :], ALU.mult)
                v.drain()
                v.scalar_tensor_tensor(gated[0:64, p * 512:(p + 1) * 512],
                                       pso[0:64, 0:512], 1.0, grbc[0:64, :],
                                       ALU.mult, ALU.mult)
                v.scalar_tensor_tensor(gated[64:128, p * 512:(p + 1) * 512],
                                       psB[64:128, 0:512], 1.0, grbc[64:128, :],
                                       ALU.mult, ALU.mult)
                cx.inc("s_v", v.drain())
            blk.vector(ep_v2)
            pair_pp_st = pair_prev_st

        pairs_done_v = cx.val("s_v")
        pairs_done_st = cx.val("s_t")

        xz0 = cx.val("sd_s")

        x1_marks = []
        sv_wo0 = cx.val("s_v")

        def x1_s(s, sl=sl, sv_wo0=sv_wo0, pwm=prev_wo_marks):
            for tokc in range(4):
                if tokc >= 2:
                    s.wait_ge(cx.h("s_v"), sv_wo0 + tokc - 1)
                elif pwm is not None:
                    s.wait_ge(cx.h("s_v"), pwm[tokc + 2])
                x1_marks.append(cx.inc("sq%d" % (tokc % 2), s.dma_start(
                    out=x1c[:, (tokc % 2) * 1024:(tokc % 2 + 1) * 1024],
                    in_=scr["x1s"][(sl * 4 + tokc) * 128:(sl * 4 + tokc + 1) * 128, :]), 16))
        blk.sync(x1_s)

        def wo_t(t, sl=sl, pairs_done_v=pairs_done_v, sv_wo0=sv_wo0):
            cx.wait(t, "s_v", pairs_done_v)
            for tokc in range(4):
                ps = psA if tokc % 2 == 0 else psB
                if tokc >= 2:
                    t.wait_ge(cx.h("s_v"), sv_wo0 + tokc - 1)
                for nb in range(2):
                    t.matmul(ps[:, nb * 512:(nb + 1) * 512],
                             C["onesb"][0:1, 0:128],
                             borow[0:1, nb * 512:(nb + 1) * 512],
                             start=True, stop=False)
                for dc in range(8):
                    for nb in range(2):
                        t.matmul(ps[:, nb * 512:(nb + 1) * 512],
                                 gated[:, dc * 512 + tokc * 128: dc * 512 + (tokc + 1) * 128],
                                 wo64[:, dc * 1024 + nb * 512: dc * 1024 + (nb + 1) * 512],
                                 start=False, stop=(dc == 7))
                cx.inc("s_t", t.drain())
        blk.tensor(wo_t)
        wo_st0 = cx.val("s_t") - 4

        def wo_v(v, sl=sl, wo_st0=wo_st0, x1_marks=x1_marks):
            for tokc in range(4):
                ps = psA if tokc % 2 == 0 else psB
                cx.wait(v, "s_t", wo_st0 + tokc + 1)
                cx.wait(v, "sq%d" % (tokc % 2), x1_marks[tokc])
                v.tensor_add(lnx[:, tokc * 1024:(tokc + 1) * 1024], ps[:, :],
                             x1c[:, (tokc % 2) * 1024:(tokc % 2 + 1) * 1024])
                v.drain()
                v.reduce_sum(stat[:, 168 + tokc:169 + tokc],
                             lnx[:, tokc * 1024:(tokc + 1) * 1024], AX.X)
                v.scalar_tensor_tensor(ln5q[:, :],
                                       lnx[:, tokc * 1024:(tokc + 1) * 1024], 1.0,
                                       lnx[:, tokc * 1024:(tokc + 1) * 1024],
                                       ALU.mult, ALU.mult,
                                       accum_out=stat[:, 172 + tokc:173 + tokc])
                cx.inc("s_v", v.drain())
            v.tensor_scalar_mul(stat[:, 168:172], stat[:, 168:172], 1.0 / D)
            v.tensor_scalar_mul(stat[:, 172:176], stat[:, 172:176], 1.0 / D)
            v.drain()
            v.scalar_tensor_tensor(stat[:, 176:180], stat[:, 168:172], 1.0,
                                   stat[:, 168:172], ALU.mult, ALU.mult)
            v.drain()
            v.tensor_sub(stat[:, 172:176], stat[:, 172:176], stat[:, 176:180])
            v.drain()
            v.tensor_scalar_add(stat[:, 172:176], stat[:, 172:176], 1e-5)
            v.drain()
            v.reciprocal(stat[:, 176:180], stat[:, 172:176])
            cx.inc("s_v", v.drain())
        blk.vector(wo_v)
        lnst = cx.val("s_v")

        def ln_s5(s, lnst=lnst):
            cx.wait(s, "s_v", lnst)
            s.activation(stat[:, 172:176], stat[:, 176:180], AF.Sqrt)
            cx.inc("s_a", s.drain())
        blk.scalar(ln_s5)
        rstd5 = cx.val("s_a")

        for tokc in range(4):
            def ap_v(v, sl=sl, tokc=tokc, rstd5=rstd5):
                cx.wait(v, "s_a", rstd5)
                if x2_marks:
                    cx.wait(v, "st0", x2_marks[-1])
                    if write_out_dbg:
                        cx.wait(v, "sd_o")
                v.tensor_scalar(outc[:, :], lnx[:, tokc * 1024:(tokc + 1) * 1024],
                                stat[:, 168 + tokc:169 + tokc],
                                stat[:, 172 + tokc:173 + tokc],
                                ALU.subtract, ALU.mult)
                v.drain()
                v.tensor_reduce(stat[:, sl * 4 + tokc:sl * 4 + tokc + 1],
                                outc[:, :], AX.X, ALU.max,
                                apply_absolute_value=True)
                cx.inc("s_v", v.drain())
            blk.vector(ap_v)
            apd = cx.val("s_v")

            def x2_s(s, sl=sl, tokc=tokc, apd=apd):
                cx.wait(s, "s_v", apd)
                cx.inc("st0", s.dma_start(
                    out=scr["x2s"][(sl * 4 + tokc) * 128:(sl * 4 + tokc + 1) * 128, :],
                    in_=outc[:, :]), 16)
                if write_out_dbg:
                    cx.inc("sd_o", s.dma_start(
                        out=scr["out"][(sl * 4 + tokc) * 128:(sl * 4 + tokc + 1) * 128, :],
                        in_=outc[:, :]), 16)
            blk.sync(x2_s)
            x2_marks.append(cx.val("st0"))
        prev_wo_marks = [sv_wo0 + i for i in range(5)]

    done_v = cx.val("s_v")
    done_s = cx.val("st0")
    es.close()
    return dict(done_v=done_v, done_s=done_s)


def phase6_l1ffn(cx, C, dram, scr, l1, x2, wq_done_g):
    nc, blk = cx.nc, cx.blk
    stat, ident = C["stat"], C["ident"]
    NI = 22

    es = ExitStack()
    row6 = es.enter_context(nc.sbuf_tensor("row6", [1, 16], F32))
    bc6 = es.enter_context(nc.sbuf_tensor("bc6", [128, 4], F32))
    clipsb = es.enter_context(nc.sbuf_tensor("clipsb", [1, 2], F32))
    es_ar1 = ExitStack()
    ps6 = es_ar1.enter_context(nc.psum_tensor("ps6", [128, 512], F32))

    def ar1_v(v, x2=x2):
        cx.wait(v, "s_v", x2["done_v"])
        v.tensor_reduce(stat[:, 32:33], stat[:, 0:32], AX.X, ALU.max)
        cx.inc("s_v", v.drain())
    blk.vector(ar1_v)
    m1 = cx.val("s_v")

    def ar1_t(t, m1=m1):
        cx.wait(t, "s_v", m1)
        t.transpose(ps6[0:1, 0:128], stat[:, 32:33], C["identf"][:, :])
        cx.inc("s_t", t.drain())
    blk.tensor(ar1_t)
    m2 = cx.val("s_t")

    def ar1_v2(v, m2=m2):
        cx.wait(v, "s_t", m2)
        v.tensor_reduce(row6[0:1, 0:1], ps6[0:1, 0:128], AX.X, ALU.max)
        v.drain()
        v.tensor_copy(row6[0:1, 0:16], AP(row6, 0, [[16, 1], [0, 16], [1, 1]]))
        cx.inc("s_v", v.drain())
    blk.vector(ar1_v2)
    m3 = cx.val("s_v")

    def ar1_g(g, m3=m3):
        cx.wait(g, "s_v", m3)
        cx.inc("sd_o2", g.dma_start(out=scr["cc1_in"][:, :], in_=row6[0:1, 0:16]), 16)
        cx.wait(g, "sd_o2")
        g.collective_compute("AllReduce", ALU.max,
                             replica_groups=[list(range(N_CORES))],
                             ins=[scr["cc1_in"].ap().opt()],
                             outs=[scr["cc1_out"].ap().opt()]).then_inc(cx.h("s_g"), 1)
        cx._cnt["s_g"] += 1
        cx.wait(g, "s_g")
        cx.inc("sd_o2", g.dma_start(out=row6[0:1, 0:16], in_=scr["cc1_out"][:, :]), 16)
        cx.inc("sd_o2", g.dma_start(out=clipsb[:, :], in_=dram["clips"][:, :]), 16)
    blk.gpsimd(ar1_g)
    m4 = cx.val("sd_o2")

    def ar1_s(s, m4=m4):
        cx.wait(s, "sd_o2", m4)
        s.activation(clipsb[0:1, 0:1], clipsb[0:1, 0:1], AF.Sigmoid)
        cx.inc("s_a", s.drain())
    blk.scalar(ar1_s)
    m5 = cx.val("s_a")

    def ar1_v3(v, m5=m5):
        cx.wait(v, "s_a", m5)
        v.tensor_tensor(row6[0:1, 1:2], row6[0:1, 0:1], clipsb[0:1, 0:1], ALU.mult)
        v.drain()
        v.tensor_scalar_mul(row6[0:1, 1:2], row6[0:1, 1:2], 1.0 / 127.0)
        v.drain()
        v.reciprocal(row6[0:1, 2:3], row6[0:1, 1:2])
        cx.inc("s_v", v.drain())
    blk.vector(ar1_v3)
    m6 = cx.val("s_v")

    def ar1_t2(t, m6=m6):
        cx.wait(t, "s_v", m6)
        t.matmul(ps6[:, 0:2], C["onesf"][0:1, 0:128], row6[0:1, 1:3],
                 start=True, stop=True)
        cx.inc("s_t", t.drain())
    blk.tensor(ar1_t2)
    m7 = cx.val("s_t")

    def ar1_v4(v, m7=m7):
        cx.wait(v, "s_t", m7)
        v.tensor_copy(bc6[:, 0:2], ps6[:, 0:2])
        cx.inc("s_v", v.drain())
    blk.vector(ar1_v4)
    sx_done = cx.val("s_v")
    es_ar1.close()

    es_a = ExitStack()
    x2c = es_a.enter_context(nc.sbuf_tensor("x2c", [128, 2 * 1024], F32))
    qsc6 = es_a.enter_context(nc.sbuf_tensor("qsc6", [128, 1024], F32))
    xqb6 = es_a.enter_context(nc.sbuf_tensor("xqb6", [128, 8 * 1024], BF16))
    xqf6 = es_a.enter_context(nc.sbuf_tensor("xqf6", [128, 8 * 1024], BF16))
    wgsb = es_a.enter_context(nc.sbuf_tensor("wgsb", [128, 8 * 2816], BF16))
    wusb = es_a.enter_context(nc.sbuf_tensor("wusb", [128, 8 * 2816], BF16))
    sg6 = es_a.enter_context(nc.sbuf_tensor("sg6", [128, 2 * 1024], BF16))
    hc6 = es_a.enter_context(nc.sbuf_tensor("hc6", [128, 2 * 1024], BF16))
    pst6 = es_a.enter_context(nc.psum_tensor("pst6", [128, 2048], BF16))
    psg = es_a.enter_context(nc.psum_tensor("psg", [128, 2048], F32))
    psu = es_a.enter_context(nc.psum_tensor("psu", [128, 1024], F32))

    prev_q_last = None
    prev_gu_last = None
    h_marks_s = []
    for sl in range(4):
        tc0 = sl * 8

        wg_prev_st = cx.val("s_t")

        def w6_g(g, sl=sl, wg_prev_st=wg_prev_st, x2=x2):
            if sl == 0:
                cx.wait(g, "sd_g", wq_done_g)
                cx.wait(g, "s_v", x2["done_v"])
                cx.wait(g, "st0", x2["done_s"])
            else:
                g.wait_ge(cx.h("s_t"), wg_prev_st)
            for e in range(ND):
                cx.inc("sd_g", g.dma_start(
                    out=wgsb[:, e * 2816:(e + 1) * 2816],
                    in_=scr["wgT"][e * 128:(e + 1) * 128, :]), 16)
            for e in range(ND):
                cx.inc("sd_g", g.dma_start(
                    out=wusb[:, e * 2816:(e + 1) * 2816],
                    in_=scr["wuT"][e * 128:(e + 1) * 128, :]), 16)
        blk.gpsimd(w6_g)
        w6_ld = cx.val("sd_g")

        q_marks = []
        for c in range(8):
            def x2_s6(s, sl=sl, tc0=tc0, c=c, x2=x2, prev_q_last=prev_q_last):
                if sl == 0 and c == 0:
                    cx.wait(s, "st0", x2["done_s"])
                if c >= 2:
                    cx.wait(s, "s_v", q_marks[c - 2])
                elif prev_q_last is not None:
                    cx.wait(s, "s_v", prev_q_last)
                cx.inc("sq%d" % (c % 2), s.dma_start(
                    out=x2c[:, (c % 2) * 1024:(c % 2 + 1) * 1024],
                    in_=scr["x2s"][(tc0 + c) * 128:(tc0 + c + 1) * 128, :]), 16)
            blk.sync(x2_s6)
            xld = cx.val("sq%d" % (c % 2))

            def q_v6(v, c=c, xld=xld, sx_done=sx_done):
                if c == 0:
                    cx.wait(v, "s_v", sx_done)
                cx.wait(v, "sq%d" % (c % 2), xld)
                v.tensor_scalar(qsc6[:, :], x2c[:, (c % 2) * 1024:(c % 2 + 1) * 1024],
                                bc6[:, 1:2], MAGIC, ALU.mult, ALU.add)
                v.drain()
                v.tensor_scalar(xqb6[:, c * 1024:(c + 1) * 1024], qsc6[:, :],
                                MAGIC, 127.0, ALU.subtract, ALU.min)
                v.drain()
                v.tensor_scalar_max(xqb6[:, c * 1024:(c + 1) * 1024],
                                    xqb6[:, c * 1024:(c + 1) * 1024], -128.0)
                cx.inc("s_v", v.drain())
            blk.vector(q_v6)
            q_marks.append(cx.val("s_v"))
        prev_q_last = q_marks[-1]
        emit_tmfm_transpose(cx, xqb6, xqf6, ident, pst6, cx.val("s_v"))
        xqf_done = cx.val("s_v")

        h_marks_v = []
        for dc in range(NI):
            def gu_t(t, sl=sl, dc=dc, w6_ld=w6_ld, xqf_done=xqf_done,
                     prev_gu_last=prev_gu_last):
                if dc == 0:
                    cx.wait(t, "s_v", xqf_done)
                    cx.wait(t, "sd_g", w6_ld)
                if dc >= 2:
                    cx.wait(t, "s_v", h_marks_v[dc - 2])
                elif prev_gu_last is not None:
                    cx.wait(t, "s_v", prev_gu_last)
                par = dc % 2
                for nb in range(2):
                    for e in range(ND):
                        t.matmul(psg[:, par * 1024 + nb * 512: par * 1024 + (nb + 1) * 512],
                                 wgsb[:, e * 2816 + dc * 128:(e) * 2816 + (dc + 1) * 128],
                                 xqf6[:, e * 1024 + nb * 512: e * 1024 + (nb + 1) * 512],
                                 start=(e == 0), stop=(e == ND - 1))
                cx.inc("s_t", t.drain())
                if dc >= 1:
                    cx.wait(t, "s_v", h_marks_v[dc - 1])
                for nb in range(2):
                    for e in range(ND):
                        t.matmul(psu[:, nb * 512:(nb + 1) * 512],
                                 wusb[:, e * 2816 + dc * 128:(e) * 2816 + (dc + 1) * 128],
                                 xqf6[:, e * 1024 + nb * 512: e * 1024 + (nb + 1) * 512],
                                 start=(e == 0), stop=(e == ND - 1))
                cx.inc("s_t", t.drain())
            blk.tensor(gu_t)
            g_mm = cx.val("s_t") - 1
            u_mm = cx.val("s_t")

            def gu_a(s, dc=dc, g_mm=g_mm):
                cx.wait(s, "s_t", g_mm)
                par = dc % 2
                s.activation(sg6[:, par * 1024:(par + 1) * 1024],
                             psg[:, par * 1024:(par + 1) * 1024],
                             AF.Silu, bias=0.0, scale=bc6[:, 0:1])
                cx.inc("s_a", s.drain())
            blk.scalar(gu_a)
            sg_done = cx.val("s_a")

            def gu_v(v, sl=sl, dc=dc, u_mm=u_mm, sg_done=sg_done,
                     h_marks_s=h_marks_s):
                cx.wait(v, "s_t", u_mm)
                cx.wait(v, "s_a", sg_done)
                same_par = [m for m in h_marks_s if m[0] == dc % 2]
                if same_par:
                    cx.wait(v, "gx%d" % (dc % 2), same_par[-1][1])
                par = dc % 2
                v.scalar_tensor_tensor(hc6[:, par * 1024:(par + 1) * 1024],
                                       psu[:, :],
                                       bc6[:, 0:1], sg6[:, par * 1024:(par + 1) * 1024],
                                       ALU.mult, ALU.mult)
                v.drain()
                v.tensor_reduce(stat[:, 96 + dc:97 + dc],
                                hc6[:, par * 1024:(par + 1) * 1024], AX.X, ALU.max,
                                apply_absolute_value=True)
                cx.inc("s_v", v.drain())
            blk.vector(gu_v)
            h_marks_v.append(cx.val("s_v"))

            def h_g(g, sl=sl, dc=dc, hv=h_marks_v[-1]):
                cx.wait(g, "s_v", hv)
                par = dc % 2
                cx.inc("gx%d" % par, g.dma_start(
                    out=scr["hs"][dc * 128:(dc + 1) * 128, sl * 1024:(sl + 1) * 1024],
                    in_=hc6[:, par * 1024:(par + 1) * 1024]), 16)
            blk.gpsimd(h_g)
            h_marks_s.append((par_ := dc % 2, cx.val("gx%d" % (dc % 2))))

        prev_gu_last = h_marks_v[-1]

        def hm_v(v, sl=sl, last=h_marks_v[-1]):
            cx.wait(v, "s_v", last)
            v.tensor_reduce(stat[:, 33 + sl:34 + sl], stat[:, 96:96 + NI],
                            AX.X, ALU.max)
            cx.inc("s_v", v.drain())
        blk.vector(hm_v)
    h_all_v = cx.val("s_v")
    h_all_g = (cx.val("gx0"), cx.val("gx1"))
    es_a.close()

    es_ar2 = ExitStack()
    ps6 = es_ar2.enter_context(nc.psum_tensor("ps6b", [128, 512], F32))
    def ar2_v(v, h_all_v=h_all_v):
        cx.wait(v, "s_v", h_all_v)
        v.tensor_reduce(stat[:, 37:38], stat[:, 33:37], AX.X, ALU.max)
        cx.inc("s_v", v.drain())
    blk.vector(ar2_v)
    n1 = cx.val("s_v")

    def ar2_t(t, n1=n1):
        cx.wait(t, "s_v", n1)
        t.transpose(ps6[0:1, 0:128], stat[:, 37:38], C["identf"][:, :])
        cx.inc("s_t", t.drain())
    blk.tensor(ar2_t)
    n2 = cx.val("s_t")

    def ar2_v2(v, n2=n2):
        cx.wait(v, "s_t", n2)
        v.tensor_reduce(row6[0:1, 4:5], ps6[0:1, 0:128], AX.X, ALU.max)
        v.drain()
        v.tensor_copy(row6[0:1, 0:16], AP(row6, 4, [[16, 1], [0, 16], [1, 1]]))
        cx.inc("s_v", v.drain())
    blk.vector(ar2_v2)
    n3 = cx.val("s_v")

    def ar2_g(g, n3=n3):
        cx.wait(g, "s_v", n3)
        cx.inc("sd_o2", g.dma_start(out=scr["cc2_in"][:, :], in_=row6[0:1, 0:16]), 16)
        cx.wait(g, "sd_o2")
        g.collective_compute("AllReduce", ALU.max,
                             replica_groups=[list(range(N_CORES))],
                             ins=[scr["cc2_in"].ap().opt()],
                             outs=[scr["cc2_out"].ap().opt()]).then_inc(cx.h("s_g"), 1)
        cx._cnt["s_g"] += 1
        cx.wait(g, "s_g")
        cx.inc("sd_o2", g.dma_start(out=row6[0:1, 0:16], in_=scr["cc2_out"][:, :]), 16)
    blk.gpsimd(ar2_g)
    n4 = cx.val("sd_o2")

    def ar2_s(s, n4=n4):
        cx.wait(s, "sd_o2", n4)
        s.activation(clipsb[0:1, 1:2], clipsb[0:1, 1:2], AF.Sigmoid)
        cx.inc("s_a", s.drain())
    blk.scalar(ar2_s)
    n5 = cx.val("s_a")

    def ar2_v3(v, n5=n5):
        cx.wait(v, "s_a", n5)
        v.tensor_tensor(row6[0:1, 5:6], row6[0:1, 0:1], clipsb[0:1, 1:2], ALU.mult)
        v.drain()
        v.tensor_scalar_mul(row6[0:1, 5:6], row6[0:1, 5:6], 1.0 / 127.0)
        v.drain()
        v.reciprocal(row6[0:1, 6:7], row6[0:1, 5:6])
        cx.inc("s_v", v.drain())
    blk.vector(ar2_v3)
    n6 = cx.val("s_v")

    def ar2_t2(t, n6=n6):
        cx.wait(t, "s_v", n6)
        t.matmul(ps6[:, 0:2], C["onesf"][0:1, 0:128], row6[0:1, 5:7],
                 start=True, stop=True)
        cx.inc("s_t", t.drain())
    blk.tensor(ar2_t2)
    n7 = cx.val("s_t")

    def ar2_v4(v, n7=n7):
        cx.wait(v, "s_t", n7)
        v.tensor_copy(bc6[:, 2:4], ps6[:, 0:2])
        cx.inc("s_v", v.drain())
    blk.vector(ar2_v4)
    sh_done = cx.val("s_v")
    es_ar2.close()

    es_b = ExitStack()
    hbt = es_b.enter_context(nc.sbuf_tensor("hbt", [128, 2 * 1024], BF16))
    qsc6b = es_b.enter_context(nc.sbuf_tensor("qsc6b", [128, 1024], F32))
    hq = es_b.enter_context(nc.sbuf_tensor("hq", [128, NI * 1024], BF16))
    wdsb = es_b.enter_context(nc.sbuf_tensor("wdsb", [128, NI * 1024], BF16))
    x2c2 = es_b.enter_context(nc.sbuf_tensor("x2c2", [128, 2 * 1024], F32))
    lnx6 = es_b.enter_context(nc.sbuf_tensor("lnx6", [128, 8 * 1024], F32))
    ln6q = es_b.enter_context(nc.sbuf_tensor("ln6q", [128, 1024], BF16))
    outc6 = es_b.enter_context(nc.sbuf_tensor("outc6", [128, 1024], F32))
    psz = es_b.enter_context(nc.psum_tensor("psz", [128, 2048], F32))

    prev_dn_st = None
    prev_ep = []
    prev_hq_marks = None
    out_marks = []
    for sl in range(4):
        tc0 = sl * 8
        wd_prev = cx.val("s_t")

        def wd_g(g, sl=sl, wd_prev=wd_prev, h_all_g=h_all_g):
            if sl == 0:
                g.wait_ge(cx.h("gx0"), h_all_g[0])
                g.wait_ge(cx.h("gx1"), h_all_g[1])
            else:
                g.wait_ge(cx.h("s_t"), wd_prev)
            for ic in range(NI):
                cx.inc("sd_g", g.dma_start(
                    out=wdsb[:, ic * 1024:(ic + 1) * 1024],
                    in_=scr["wdT"][ic * 128:(ic + 1) * 128, :]), 16)
        blk.gpsimd(wd_g)
        wd_ld = cx.val("sd_g")

        hq_marks = []
        for ic in range(NI):
            def h_s6(s, sl=sl, ic=ic, x2=x2, phm=prev_hq_marks,
                     pep=prev_ep[:]):
                if ic >= 2:
                    cx.wait(s, "s_v", hq_marks[ic - 2])
                elif sl == 0:
                    cx.wait(s, "s_v", sh_done)
                    g0, g1 = h_all_g
                    s.wait_ge(cx.h("gx0"), g0)
                    s.wait_ge(cx.h("gx1"), g1)
                else:
                    cx.wait(s, "s_v", max(phm[ic - 2], pep[ic - 2]))
                cx.inc("sq%d" % (ic % 2), s.dma_start(
                    out=hbt[:, (ic % 2) * 1024:(ic % 2 + 1) * 1024],
                    in_=scr["hs"][ic * 128:(ic + 1) * 128, sl * 1024:(sl + 1) * 1024]), 16)
            blk.sync(h_s6)
            hld = cx.val("sq%d" % (ic % 2))

            def hq_v(v, ic=ic, hld=hld, sh_done=sh_done, prev_dn_st=prev_dn_st):
                if ic == 0:
                    cx.wait(v, "s_v", sh_done)
                    if prev_dn_st is not None:
                        cx.wait(v, "s_t", prev_dn_st)
                cx.wait(v, "sq%d" % (ic % 2), hld)
                v.tensor_scalar(qsc6b[:, :], hbt[:, (ic % 2) * 1024:(ic % 2 + 1) * 1024],
                                bc6[:, 3:4], MAGIC, ALU.mult, ALU.add)
                v.drain()
                v.tensor_scalar(hq[:, ic * 1024:(ic + 1) * 1024], qsc6b[:, :],
                                MAGIC, 127.0, ALU.subtract, ALU.min)
                v.drain()
                v.tensor_scalar_max(hq[:, ic * 1024:(ic + 1) * 1024],
                                    hq[:, ic * 1024:(ic + 1) * 1024], -128.0)
                cx.inc("s_v", v.drain())
            blk.vector(hq_v)
            hq_marks.append(cx.val("s_v"))
        hq_done = cx.val("s_v")
        prev_hq_marks = hq_marks

        ep_marks = []
        for tokc in range(8):
            def x2_s6b(s, sl=sl, tokc=tokc, x2=x2, prev_ep=prev_ep,
                       hq_done=hq_done):
                if tokc >= 2:
                    cx.wait(s, "s_v", ep_marks[tokc - 2])
                else:
                    m = hq_done
                    if prev_ep:
                        m = max(m, prev_ep[tokc - 2])
                    cx.wait(s, "s_v", m)
                cx.inc("sq%d" % (tokc % 2), s.dma_start(
                    out=x2c2[:, (tokc % 2) * 1024:(tokc % 2 + 1) * 1024],
                    in_=scr["x2s"][(sl * 8 + tokc) * 128:(sl * 8 + tokc + 1) * 128, :]), 16)
            blk.sync(x2_s6b)
            x2ld = cx.val("sq%d" % (tokc % 2))

            def dn_t(t, sl=sl, tokc=tokc, hq_done=hq_done, wd_ld=wd_ld,
                     x2ld=x2ld):
                if tokc == 0:
                    cx.wait(t, "s_v", hq_done)
                    cx.wait(t, "sd_g", wd_ld)
                par = tokc % 2
                if tokc >= 2:
                    t.wait_ge(cx.h("s_v"), ep_marks[tokc - 2])
                for ic in range(NI):
                    for nb in range(2):
                        t.matmul(psz[:, par * 1024 + nb * 512: par * 1024 + (nb + 1) * 512],
                                 hq[:, ic * 1024 + tokc * 128: ic * 1024 + (tokc + 1) * 128],
                                 wdsb[:, ic * 1024 + nb * 512: ic * 1024 + (nb + 1) * 512],
                                 start=(ic == 0), stop=(ic == NI - 1))
                cx.inc("s_t", t.drain())
            blk.tensor(dn_t)
            dn_mm = cx.val("s_t")

            def dn_v(v, sl=sl, tokc=tokc, dn_mm=dn_mm, x2ld=x2ld):
                cx.wait(v, "s_t", dn_mm)
                cx.wait(v, "sq%d" % (tokc % 2), x2ld)
                par = tokc % 2
                v.scalar_tensor_tensor(lnx6[:, tokc * 1024:(tokc + 1) * 1024],
                                       psz[:, par * 1024:(par + 1) * 1024],
                                       bc6[:, 2:3],
                                       x2c2[:, par * 1024:(par + 1) * 1024],
                                       ALU.mult, ALU.add)
                v.drain()
                v.reduce_sum(stat[:, 104 + tokc:105 + tokc],
                             lnx6[:, tokc * 1024:(tokc + 1) * 1024], AX.X)
                v.scalar_tensor_tensor(ln6q[:, :],
                                       lnx6[:, tokc * 1024:(tokc + 1) * 1024], 1.0,
                                       lnx6[:, tokc * 1024:(tokc + 1) * 1024],
                                       ALU.mult, ALU.mult,
                                       accum_out=stat[:, 112 + tokc:113 + tokc])
                cx.inc("s_v", v.drain())
            blk.vector(dn_v)
            ep_marks.append(cx.val("s_v"))

        prev_dn_st = cx.val("s_t")
        prev_ep = ep_marks

        def ln6_v(v, last=ep_marks[-1]):
            cx.wait(v, "s_v", last)
            v.tensor_scalar_mul(stat[:, 104:112], stat[:, 104:112], 1.0 / D)
            v.tensor_scalar_mul(stat[:, 112:120], stat[:, 112:120], 1.0 / D)
            v.drain()
            v.scalar_tensor_tensor(stat[:, 120:128], stat[:, 104:112], 1.0,
                                   stat[:, 104:112], ALU.mult, ALU.mult)
            v.drain()
            v.tensor_sub(stat[:, 112:120], stat[:, 112:120], stat[:, 120:128])
            v.drain()
            v.tensor_scalar_add(stat[:, 112:120], stat[:, 112:120], 1e-5)
            v.drain()
            v.reciprocal(stat[:, 120:128], stat[:, 112:120])
            cx.inc("s_v", v.drain())
        blk.vector(ln6_v)
        l6 = cx.val("s_v")

        def ln6_s(s, l6=l6):
            cx.wait(s, "s_v", l6)
            s.activation(stat[:, 112:120], stat[:, 120:128], AF.Sqrt)
            cx.inc("s_a", s.drain())
        blk.scalar(ln6_s)
        r6 = cx.val("s_a")

        for tokc in range(8):
            def fo_v(v, sl=sl, tokc=tokc, r6=r6):
                cx.wait(v, "s_a", r6)
                if out_marks:
                    cx.wait(v, "sd_o", out_marks[-1])
                v.tensor_scalar(outc6[:, :], lnx6[:, tokc * 1024:(tokc + 1) * 1024],
                                stat[:, 104 + tokc:105 + tokc],
                                stat[:, 112 + tokc:113 + tokc],
                                ALU.subtract, ALU.mult)
                cx.inc("s_v", v.drain())
            blk.vector(fo_v)
            fod = cx.val("s_v")

            def fo_s(s, sl=sl, tokc=tokc, fod=fod):
                cx.wait(s, "s_v", fod)
                cx.inc("sd_o", s.dma_start(
                    out=scr["out"][(sl * 8 + tokc) * 128:(sl * 8 + tokc + 1) * 128, :],
                    in_=outc6[:, :]), 16)
            blk.sync(fo_s)
            out_marks.append(cx.val("sd_o"))

    es_b.close()
    es.close()

    def fin_s(s):
        cx.wait(s, "sd_o")
        cx.wait(s, "sd_s")
        cx.wait(s, "sd_g")
    blk.sync(fin_s)


# version 06
# speedup vs baseline: 1.2102x; 1.0454x over previous
from contextlib import ExitStack

import numpy as np
import concourse.bass as bass
import concourse.mybir as mybir
from concourse.bass_utils import run_bass_kernel_spmd

F32 = mybir.dt.float32
BF16 = mybir.dt.bfloat16
I32 = mybir.dt.int32
AF = mybir.ActivationFunctionType
ALU = mybir.AluOpType
AX = mybir.AxisListType
AP = bass.AP

T = 4096
D = 1024
NT = T // 128
ND = D // 128
MAGIC = float(3 * 2 ** 22)
N_CORES = 8
DBG = False
UPTO = 99
SUB2A = 99


def emit_dump(cx, dram, name, src_ap, shape, marker):
    if not DBG:
        return
    t = cx.nc.dram_tensor("dbg_" + name, list(shape), F32, kind="ExternalOutput")

    def d_g(g):
        sem, val = marker
        cx.wait(g, sem, val)
        cx.inc("sd_o2", g.dma_start(out=t[:, :], in_=src_ap), 16)
    cx.blk.gpsimd(d_g)


class Ctx:

    def __init__(self, nc, blk):
        self.nc = nc
        self.blk = blk
        self._sems = {}
        self._cnt = {}
        self._names = {}

    def sem(self, name):
        h = self.nc.alloc_semaphore(name=name)
        self._sems[name] = h
        self._cnt[name] = 0

    def h(self, name):
        return self._sems[name]

    def inc(self, name, instr, n=1):
        instr.then_inc(self._sems[name], n)
        self._cnt[name] += n
        return self._cnt[name]

    def val(self, name):
        return self._cnt[name]

    def wait(self, eng, name, v=None):
        eng.wait_ge(self._sems[name], self._cnt[name] if v is None else v)

    def uname(self, base):
        i = self._names.get(base, 0)
        self._names[base] = i + 1
        return f"{base}{i}"



def emit_tmfm_transpose(cx, src, dst, ident, pst, src_ready_sv, nchunks=8):
    ngrp = nchunks * 2

    def tr_t(t):
        cx.wait(t, "s_v", src_ready_sv)
        for grp in range(ngrp):
            if grp >= 2:
                t.wait_ge(cx.h("s_v"), src_ready_sv + grp - 1)
            for j in range(4):
                idx = grp * 4 + j
                tc, dc = idx // 8, idx % 8
                slot = (grp % 2) * 8 + j
                t.transpose(pst[:, slot * 128:(slot + 1) * 128],
                            src[:, tc * 1024 + dc * 128: tc * 1024 + (dc + 1) * 128],
                            ident[:, :])
            cx.inc("s_t", t.drain())
    cx.blk.tensor(tr_t)
    t0 = cx.val("s_t") - ngrp

    def tr_v(v):
        for grp in range(ngrp):
            cx.wait(v, "s_t", t0 + grp + 1)
            for j in range(4):
                idx = grp * 4 + j
                tc, dc = idx // 8, idx % 8
                slot = (grp % 2) * 8 + j
                v.tensor_copy(dst[:, dc * 1024 + tc * 128: dc * 1024 + (tc + 1) * 128],
                              pst[:, slot * 128:(slot + 1) * 128])
            cx.inc("s_v", v.drain())
    cx.blk.vector(tr_v)


def emit_ln(cx, xs, stat, wait_marker, eps=1e-5):
    nc = cx.nc
    sem, val = wait_marker
    with nc.sbuf_tensor(cx.uname("lnsq"), [128, 1024], BF16) as lnsq:
        def ln_v(v):
            cx.wait(v, sem, val)
            if DBG:
                cx.wait(v, "sd_o2")
            for tc in range(NT):
                xc = xs[:, tc * 1024:(tc + 1) * 1024]
                v.reduce_sum(stat[:, 104 + tc:105 + tc], xc, AX.X)
                v.scalar_tensor_tensor(lnsq[:, :], xc, 1.0, xc, ALU.mult, ALU.mult,
                                       accum_out=stat[:, 136 + tc:137 + tc])
                v.drain()
            v.tensor_scalar_mul(stat[:, 104:136], stat[:, 104:136], 1.0 / D)
            v.tensor_scalar_mul(stat[:, 136:168], stat[:, 136:168], 1.0 / D)
            v.drain()
            v.scalar_tensor_tensor(stat[:, 72:104], stat[:, 104:136], 1.0,
                                   stat[:, 104:136], ALU.mult, ALU.mult)
            v.drain()
            v.tensor_sub(stat[:, 136:168], stat[:, 136:168], stat[:, 72:104])
            v.drain()
            v.tensor_scalar_add(stat[:, 136:168], stat[:, 136:168], eps)
            v.drain()
            v.reciprocal(stat[:, 72:104], stat[:, 136:168])
            cx.inc("s_v", v.drain())
        cx.blk.vector(ln_v)
        sready = cx.val("s_v")

        def ln_s(s):
            cx.wait(s, "s_v", sready)
            s.activation(stat[:, 136:168], stat[:, 72:104], AF.Sqrt)
            cx.inc("s_a", s.drain())
        cx.blk.scalar(ln_s)
        rstd_ready = cx.val("s_a")

        def ln_v2(v):
            cx.wait(v, "s_a", rstd_ready)
            for tc in range(NT):
                v.tensor_scalar(xs[:, tc * 1024:(tc + 1) * 1024],
                                xs[:, tc * 1024:(tc + 1) * 1024],
                                stat[:, 104 + tc:105 + tc],
                                stat[:, 136 + tc:137 + tc],
                                ALU.subtract, ALU.mult)
            cx.inc("s_v", v.drain())
        cx.blk.vector(ln_v2)



def phase0_setup(cx, C, dram):
    nc = cx.nc
    es = ExitStack()
    iot = es.enter_context(nc.sbuf_tensor("iot", [128, 128], F32))
    brow = es.enter_context(nc.sbuf_tensor("brow", [1, D], F32))
    onesf = es.enter_context(nc.sbuf_tensor("onesf", [1, 128], F32))
    psb = es.enter_context(nc.psum_tensor("psb", [128, 1024], F32))

    def p0_g(g):
        g.iota(iot[:, :], [[1, 128]], channel_multiplier=-1,
               allow_small_or_imprecise_dtypes=True)
        g.drain()
        cx.inc("s_g", g.drain())
        cx.inc("sd_g", g.dma_start(
            out=AP(C["wqk"], 0, [[C["wqk"].shape[1], 128], [128, ND], [1, 128]]),
            in_=AP(dram["wqk0"], 0, [[128, 128], [128 * 128, ND], [1, 128]])), 16)
        cx.inc("sd_g", g.dma_start(
            out=AP(C["wv"], 0, [[C["wv"].shape[1], 128], [64, ND], [1, 64]]),
            in_=AP(dram["wv0"], 0, [[64, 128], [128 * 64, ND], [1, 64]])), 16)
        cx.inc("sd_g", g.dma_start(out=brow[:, :], in_=dram["bias0"][:, :]), 16)
        cx.inc("sd_g", g.dma_start(out=C["scal"][0:1, 0:1], in_=dram["ws0"][:, :]), 16)
    cx.blk.gpsimd(p0_g)
    C["w0ld"] = cx.val("sd_g")
    ld_done = cx.val("sd_g")

    def p0_v(v):
        cx.wait(v, "s_g", 1)
        v.tensor_single_scalar(C["ident"][:, :], iot[:, :], 0.0, ALU.is_equal)
        v.tensor_single_scalar(C["identf"][:, :], iot[:, :], 0.0, ALU.is_equal)
        v.memset(C["onesb"][:, :], 1.0)
        v.memset(onesf[:, :], 1.0)
        v.memset(C["onesf"][:, :], 1.0)
        cx.inc("s_v", v.drain())
    cx.blk.vector(p0_v)
    C["const_done"] = cx.val("s_v")

    def p0_g2(g):
        cx.wait(g, "s_v", C["const_done"])
        g.iota(iot[:, :], [[1, 128]], channel_multiplier=1,
               allow_small_or_imprecise_dtypes=True)
        g.drain()
        cx.inc("s_g", g.drain())
    cx.blk.gpsimd(p0_g2)

    def p0_v2(v):
        cx.wait(v, "s_g", 2)
        v.tensor_single_scalar(C["identJ"][:, :], iot[:, :], 127.0, ALU.is_equal)
        cx.inc("s_v", v.drain())
    cx.blk.vector(p0_v2)
    C["const_done"] = cx.val("s_v")

    def p0_t(t):
        cx.wait(t, "s_v", C["const_done"])
        cx.wait(t, "sd_g", ld_done)
        for nb in range(2):
            t.matmul(psb[:, nb * 512:(nb + 1) * 512], onesf[:, :],
                     brow[:, nb * 512:(nb + 1) * 512], start=True, stop=True)
        cx.inc("s_t", t.drain())
    cx.blk.tensor(p0_t)
    bias_mm = cx.val("s_t")

    def p0_v2(v):
        cx.wait(v, "s_t", bias_mm)
        v.tensor_copy(C["b0row"][:, :], psb[:, :])
        cx.inc("s_v", v.drain())
    cx.blk.vector(p0_v2)

    def p0_t2(t):
        cx.wait(t, "s_v", cx.val("s_v"))
        t.matmul(psb[:, 0:1], onesf[:, :], C["scal"][0:1, 0:1], start=True, stop=True)
        cx.inc("s_t", t.drain())
    cx.blk.tensor(p0_t2)
    ws_mm = cx.val("s_t")

    def p0_v3(v):
        cx.wait(v, "s_t", ws_mm)
        v.tensor_copy(C["scal"][:, 0:1], psb[:, 0:1])
        cx.inc("s_v", v.drain())
    cx.blk.vector(p0_v3)
    C["bcast_done"] = cx.val("s_v")
    C["const_done"] = cx.val("s_v")
    return es


def phase2_l0_attn(cx, C, dram, xs):
    nc, blk = cx.nc, cx.blk
    stat, ident, identf = C["stat"], C["ident"], C["identf"]

    es_p = ExitStack()
    q2 = es_p.enter_context(nc.sbuf_tensor("q2", [128, T], BF16))
    k2 = es_p.enter_context(nc.sbuf_tensor("k2", [128, T], BF16))
    vtm = es_p.enter_context(nc.sbuf_tensor("vtm", [128, NT * 64], BF16))
    outtm = es_p.enter_context(nc.sbuf_tensor("outtm", [128, NT * 64], F32))

    es_a = ExitStack()
    xb = es_a.enter_context(nc.sbuf_tensor("xb", [128, 8 * 1024], BF16))
    xfm = es_a.enter_context(nc.sbuf_tensor("xfm", [128, 8 * 1024], BF16))
    vfm = es_a.enter_context(nc.sbuf_tensor("vfm", [64, T], BF16))
    pst = es_a.enter_context(nc.psum_tensor("pst", [128, 2048], BF16))
    psqk = es_a.enter_context(nc.psum_tensor("psqk", [128, 1024], F32))
    psv = es_a.enter_context(nc.psum_tensor("psv", [64, 1024], F32))

    for sl in range(4 if SUB2A >= 3 else 1):
        tc0 = sl * 8

        def cast_v(v, sl=sl, tc0=tc0):
            if sl == 0:
                cx.wait(v, "sd_s", C["xs_loaded"])
                cx.wait(v, "s_v", C["const_done"])
            for c in range(8):
                v.tensor_copy(xb[:, c * 1024:(c + 1) * 1024],
                              xs[:, (tc0 + c) * 1024:(tc0 + c + 1) * 1024])
            cx.inc("s_v", v.drain())
        blk.vector(cast_v)
        if SUB2A <= 0:
            break
        emit_tmfm_transpose(cx, xb, xfm, ident, pst, cx.val("s_v"))
        xfm_ready = cx.val("s_v")
        if SUB2A <= 1:
            break

        def proj_t(t, sl=sl, xfm_ready=xfm_ready):
            cx.wait(t, "s_v", xfm_ready)
            if sl == 0:
                cx.wait(t, "sd_g", C["w0ld"])
            for nb in range(2):
                for e in range(ND):
                    t.matmul(psqk[:, nb * 512:(nb + 1) * 512],
                             C["wqk"][:, e * 128:(e + 1) * 128],
                             xfm[:, e * 1024 + nb * 512: e * 1024 + (nb + 1) * 512],
                             start=(e == 0), stop=(e == ND - 1))
            cx.inc("s_t", t.drain())
            for nb in range(2):
                for e in range(ND):
                    t.matmul(psv[:, nb * 512:(nb + 1) * 512],
                             C["wv"][:, e * 64:(e + 1) * 64],
                             xfm[:, e * 1024 + nb * 512: e * 1024 + (nb + 1) * 512],
                             start=(e == 0), stop=(e == ND - 1))
            cx.inc("s_t", t.drain())
        blk.tensor(proj_t)
        qk_ready = cx.val("s_t") - 1
        v_ready = cx.val("s_t")

        def evac_v(v, sl=sl, qk_ready=qk_ready, v_ready=v_ready):
            cols = slice(sl * 1024, (sl + 1) * 1024)
            cx.wait(v, "s_t", qk_ready)
            v.tensor_copy(q2[0:64, cols], psqk[0:64, :])
            v.tensor_copy(q2[64:128, cols], psqk[0:64, :])
            v.tensor_copy(k2[0:64, cols], psqk[64:128, :])
            v.tensor_copy(k2[64:128, cols], psqk[64:128, :])
            cx.wait(v, "s_t", v_ready)
            v.tensor_copy(vfm[:, cols], psv[:, :])
            cx.inc("s_v", v.drain())
        blk.vector(evac_v)
    qkv_done = cx.val("s_v")
    if SUB2A <= 3:
        es_a.close(); es_p.close()
        return cx.val("s_v")
    if UPTO <= 2:
        es_a.close(); es_p.close()
        return cx.val("s_v")
    emit_dump(cx, dram, "q2", q2[:, :], [128, T], ("s_v", qkv_done))
    emit_dump(cx, dram, "k2", k2[:, :], [128, T], ("s_v", qkv_done))

    def vT_t(t):
        cx.wait(t, "s_v", qkv_done)
        for grp in range(8):
            if grp >= 2:
                t.wait_ge(cx.h("s_v"), qkv_done + grp - 1)
            for j in range(4):
                tc = grp * 4 + j
                t.transpose(pst[:, (grp % 2) * 1024 + j * 128:(grp % 2) * 1024 + j * 128 + 64],
                            vfm[:, tc * 128:(tc + 1) * 128], ident[0:64, 0:64])
            cx.inc("s_t", t.drain())
    blk.tensor(vT_t)
    vt0 = cx.val("s_t") - 8

    def vT_v(v):
        for grp in range(8):
            cx.wait(v, "s_t", vt0 + grp + 1)
            for j in range(4):
                tc = grp * 4 + j
                v.tensor_copy(vtm[:, tc * 64:(tc + 1) * 64],
                              pst[:, (grp % 2) * 1024 + j * 128:(grp % 2) * 1024 + j * 128 + 64])
            cx.inc("s_v", v.drain())
    blk.vector(vT_v)
    vtm_done = cx.val("s_v")
    emit_dump(cx, dram, "vtm", vtm[:, :], [128, NT * 64], ("s_v", vtm_done))
    es_a.close()

    es_b = ExitStack()
    EA = es_b.enter_context(nc.sbuf_tensor("EA", [128, T], BF16))
    EB = es_b.enter_context(nc.sbuf_tensor("EB", [128, T], BF16))
    vsc = es_b.enter_context(nc.sbuf_tensor("vsc", [128, 128], BF16))
    acc = es_b.enter_context(nc.sbuf_tensor("acc", [64, T], F32))

    es_ps = ExitStack()
    psA = es_ps.enter_context(nc.psum_tensor("psA", [128, 1024], F32))
    psB = es_ps.enter_context(nc.psum_tensor("psB", [128, 1024], F32))
    pso = es_ps.enter_context(nc.psum_tensor("pso", [64, 2048], F32))

    for kp in range(16):
        ca, cb = 2 * kp, 2 * kp + 1

        def qkmm(t, isp, ca=ca, cb=cb):
            for nb in range(2):
                t.matmul(psA[:, nb * 512:(nb + 1) * 512],
                         k2[0:64, ca * 128:(ca + 1) * 128],
                         q2[0:64, isp * 1024 + nb * 512: isp * 1024 + (nb + 1) * 512],
                         start=True, stop=True, tile_position=(0, 0))
                t.matmul(psB[:, nb * 512:(nb + 1) * 512],
                         k2[64:128, cb * 128:(cb + 1) * 128],
                         q2[64:128, isp * 1024 + nb * 512: isp * 1024 + (nb + 1) * 512],
                         start=True, stop=True, tile_position=(64, 0))

        last_e = cx.val("s_a")
        svA = cx.val("s_v")
        tA0 = cx.val("s_t")

        def pA_t(t, kp=kp, qkmm=qkmm, last_e=last_e, svA=svA):
            if kp == 0:
                cx.wait(t, "s_v", vtm_done)
            else:
                cx.wait(t, "s_a", last_e)
            for isp in range(4):
                if isp >= 1:
                    t.wait_ge(cx.h("s_v"), svA + isp)
                qkmm(t, isp)
                cx.inc("s_t", t.drain())
        blk.tensor(pA_t)

        def pA_v(v, kp=kp, tA0=tA0):
            for isp in range(4):
                cx.wait(v, "s_t", tA0 + isp + 1)
                v.tensor_reduce(stat[:, isp:isp + 1], psA[:, :], AX.X, ALU.max)
                v.tensor_reduce(stat[:, 4 + isp:5 + isp], psB[:, :], AX.X, ALU.max)
                cx.inc("s_v", v.drain())
            v.tensor_reduce(stat[:, 8:9], stat[:, 0:4], AX.X, ALU.max, negate=True)
            v.tensor_reduce(stat[:, 9:10], stat[:, 4:8], AX.X, ALU.max, negate=True)
            cx.inc("s_v", v.drain())
        blk.vector(pA_v)
        mA_done = cx.val("s_v")

        tB0 = cx.val("s_t")
        saB = cx.val("s_a")

        def pB_t(t, kp=kp, qkmm=qkmm, mA_done=mA_done, saB=saB):
            cx.wait(t, "s_v", mA_done)
            for isp in range(4):
                if isp >= 1:
                    t.wait_ge(cx.h("s_a"), saB + isp)
                qkmm(t, isp)
                cx.inc("s_t", t.drain())
        blk.tensor(pB_t)

        def pB_s(s, kp=kp, tB0=tB0):
            for isp in range(4):
                cx.wait(s, "s_t", tB0 + isp + 1)
                s.activation(EA[:, isp * 1024:(isp + 1) * 1024], psA[:, :],
                             AF.Exp, bias=stat[:, 8:9], scale=1.0,
                             accum_out=stat[:, 10 + isp:11 + isp])
                s.activation(EB[:, isp * 1024:(isp + 1) * 1024], psB[:, :],
                             AF.Exp, bias=stat[:, 9:10], scale=1.0,
                             accum_out=stat[:, 14 + isp:15 + isp])
                cx.inc("s_a", s.drain())
        blk.scalar(pB_s)
        e_done = cx.val("s_a")

        def lrec_v(v, kp=kp, ca=ca, e_done=e_done):
            cx.wait(v, "s_a", e_done)
            v.reduce_sum(stat[:, 18:19], stat[:, 10:14], AX.X)
            v.reduce_sum(stat[:, 19:20], stat[:, 14:18], AX.X)
            v.drain()
            v.reciprocal(stat[:, 20:21], stat[:, 18:19])
            v.reciprocal(stat[:, 21:22], stat[:, 19:20])
            v.drain()
            v.tensor_scalar(vsc[:, 0:64], vtm[:, ca * 64:(ca + 1) * 64],
                            stat[:, 20:21], None, ALU.mult)
            v.tensor_scalar(vsc[:, 64:128], vtm[:, (ca + 1) * 64:(ca + 2) * 64],
                            stat[:, 21:22], None, ALU.mult)
            cx.inc("s_v", v.drain())
        blk.vector(lrec_v)
        vsc_ready = cx.val("s_v")

        def pv_t(t, kp=kp, vsc_ready=vsc_ready):
            cx.wait(t, "s_v", vsc_ready)
            for ih in range(2):
                if ih == 1:
                    t.wait_ge(cx.h("s_v"), vsc_ready + 1)
                for nb in range(4):
                    t.matmul(pso[:, nb * 512:(nb + 1) * 512], vsc[:, 0:64],
                             EA[:, ih * 2048 + nb * 512: ih * 2048 + (nb + 1) * 512],
                             start=True, stop=False)
                    t.matmul(pso[:, nb * 512:(nb + 1) * 512], vsc[:, 64:128],
                             EB[:, ih * 2048 + nb * 512: ih * 2048 + (nb + 1) * 512],
                             start=False, stop=True)
                cx.inc("s_t", t.drain())
        blk.tensor(pv_t)
        pvh0 = cx.val("s_t") - 2

        def pvacc_v(v, kp=kp, pvh0=pvh0):
            for ih in range(2):
                cx.wait(v, "s_t", pvh0 + ih + 1)
                if kp == 0:
                    v.tensor_copy(acc[:, ih * 2048:(ih + 1) * 2048], pso[:, :])
                else:
                    v.tensor_add(acc[:, ih * 2048:(ih + 1) * 2048],
                                 acc[:, ih * 2048:(ih + 1) * 2048], pso[:, :])
                cx.inc("s_v", v.drain())
        blk.vector(pvacc_v)
    acc_done = cx.val("s_v")
    if UPTO <= 3:
        es_ps.close(); es_b.close(); es_p.close()
        return cx.val("s_v")
    emit_dump(cx, dram, "acc", acc[:, :], [64, T], ("s_v", acc_done))
    emit_dump(cx, dram, "EA15", EA[:, :], [128, T], ("s_v", acc_done))
    emit_dump(cx, dram, "stat22", stat[:, 0:22], [128, 22], ("s_v", acc_done))
    es_ps.close()

    with nc.psum_tensor("pst2", [128, 1024], F32) as pst2:
        def oT_t(t):
            cx.wait(t, "s_v", acc_done)
            for grp in range(8):
                if grp >= 2:
                    t.wait_ge(cx.h("s_v"), acc_done + grp - 1)
                for j in range(4):
                    tc = grp * 4 + j
                    t.transpose(pst2[:, (grp % 2) * 512 + j * 128:(grp % 2) * 512 + j * 128 + 64],
                                acc[:, tc * 128:(tc + 1) * 128], identf[0:64, 0:64])
                cx.inc("s_t", t.drain())
        blk.tensor(oT_t)
        ot0 = cx.val("s_t") - 8

        def oT_v(v):
            for grp in range(8):
                cx.wait(v, "s_t", ot0 + grp + 1)
                for j in range(4):
                    tc = grp * 4 + j
                    v.tensor_copy(outtm[:, tc * 64:(tc + 1) * 64],
                                  pst2[:, (grp % 2) * 512 + j * 128:(grp % 2) * 512 + j * 128 + 64])
                cx.inc("s_v", v.drain())
        blk.vector(oT_v)
    outtm_done = cx.val("s_v")
    emit_dump(cx, dram, "outtm", outtm[:, :], [128, NT * 64], ("s_v", outtm_done))
    es_b.close()

    def resid_v(v):
        cx.wait(v, "s_v", outtm_done)
        for tc in range(NT):
            v.tensor_add(xs[:, tc * 1024:(tc + 1) * 1024],
                         xs[:, tc * 1024:(tc + 1) * 1024],
                         AP(outtm, tc * 64, [[NT * 64, 128], [0, 16], [1, 64]]))
        cx.inc("s_v", v.drain())
    blk.vector(resid_v)
    emit_ln(cx, xs, stat, ("s_v", cx.val("s_v")))
    emit_dump(cx, dram, "x1", xs[:, :], [128, NT * 1024], ("s_v", cx.val("s_v")))
    es_p.close()
    return cx.val("s_v")


def phase3_l0_ffn(cx, C, dram, xs, ln0_done, safe_prefetch_sv):
    nc, blk = cx.nc, cx.blk
    stat = C["stat"]

    es = ExitStack()
    xqb = es.enter_context(nc.sbuf_tensor("xqb", [128, 8 * 1024], BF16))
    xqf = es.enter_context(nc.sbuf_tensor("xqf", [128, 8 * 1024], BF16))
    w0full = es.enter_context(nc.sbuf_tensor("w0full", [128, ND * 1024], BF16))
    ffo = es.enter_context(nc.sbuf_tensor("ffo", [128, 1024], F32))
    qsc = es.enter_context(nc.sbuf_tensor("qsc", [128, 1024], F32))
    pst3 = es.enter_context(nc.psum_tensor("pst3", [128, 2048], BF16))
    psf = es.enter_context(nc.psum_tensor("psf", [128, 2048], F32))

    def w0_g(g):
        cx.wait(g, "s_v", safe_prefetch_sv)
        if DBG:
            cx.wait(g, "sd_o2")
        for e in range(ND):
            cx.inc("sd_g", g.dma_start(
                out=w0full[:, e * 1024:(e + 1) * 1024],
                in_=dram["wint0"][e * 128:(e + 1) * 128, :]), 16)
    blk.gpsimd(w0_g)
    w0_loaded = cx.val("sd_g")

    for sl in range(4):
        tc0 = sl * 8

        def quant_v(v, sl=sl, tc0=tc0):
            if sl == 0:
                cx.wait(v, "s_v", ln0_done)
                cx.wait(v, "s_v", C["bcast_done"])
            if DBG:
                cx.wait(v, "sd_o2")
            for c in range(8):
                xc = xs[:, (tc0 + c) * 1024:(tc0 + c + 1) * 1024]
                v.scalar_tensor_tensor(xqb[:, c * 1024:(c + 1) * 1024], xc, 1.0, xc,
                                       ALU.mult, ALU.mult,
                                       accum_out=stat[:, 32 + c:33 + c])
                v.tensor_reduce(stat[:, 40 + c:41 + c], xc, AX.X, ALU.max,
                                apply_absolute_value=True)
            v.drain()
            v.tensor_scalar(stat[:, 48:56], stat[:, 32:40], 1.0 / D, 1e-6,
                            ALU.mult, ALU.add)
            v.drain()
            v.reciprocal(stat[:, 56:64], stat[:, 48:56])
            cx.inc("s_v", v.drain())
        blk.vector(quant_v)
        st_ready = cx.val("s_v")

        def rr_s(s, st_ready=st_ready):
            cx.wait(s, "s_v", st_ready)
            s.activation(stat[:, 64:72], stat[:, 56:64], AF.Sqrt)
            cx.inc("s_a", s.drain())
        blk.scalar(rr_s)
        rr_ready = cx.val("s_a")

        def quant2_v(v, sl=sl, tc0=tc0, rr_ready=rr_ready):
            cx.wait(v, "s_a", rr_ready)
            v.tensor_tensor(stat[:, 72:80], stat[:, 40:48], stat[:, 64:72], ALU.mult)
            v.drain()
            v.tensor_scalar_max(stat[:, 72:80], stat[:, 72:80], 1e-5)
            v.drain()
            v.reciprocal(stat[:, 80:88], stat[:, 72:80])
            v.drain()
            v.tensor_scalar_mul(stat[:, 80:88], stat[:, 80:88], 127.0)
            v.drain()
            v.tensor_tensor(stat[:, 88:96], stat[:, 64:72], stat[:, 80:88], ALU.mult)
            v.drain()
            for c in range(8):
                xc = xs[:, (tc0 + c) * 1024:(tc0 + c + 1) * 1024]
                v.tensor_scalar(qsc[:, :], xc,
                                stat[:, 88 + c:89 + c], MAGIC, ALU.mult, ALU.add)
                v.drain()
                v.tensor_scalar(xqb[:, c * 1024:(c + 1) * 1024], qsc[:, :],
                                MAGIC, 127.0, ALU.subtract, ALU.min)
                v.drain()
                v.tensor_scalar_max(xqb[:, c * 1024:(c + 1) * 1024],
                                    xqb[:, c * 1024:(c + 1) * 1024], -128.0)
                v.drain()
            v.reciprocal(stat[:, 96:104], stat[:, 80:88])
            v.drain()
            v.tensor_scalar(stat[:, 96:104], stat[:, 96:104],
                            C["scal"][:, 0:1], None, ALU.mult)
            cx.inc("s_v", v.drain())
        blk.vector(quant2_v)
        if sl == 0:
            emit_dump(cx, dram, "xqb0", xqb[:, :], [128, 8 * 1024], ("s_v", cx.val("s_v")))
            emit_dump(cx, dram, "qstat0", stat[:, 32:104], [128, 72], ("s_v", cx.val("s_v")))
        emit_tmfm_transpose(cx, xqb, xqf, C["ident"], pst3, cx.val("s_v"))
        xqf_ready = cx.val("s_v")

        for c in range(8):
            tc = tc0 + c
            half = c % 2
            prev_resid = cx.val("s_v")

            def mm_t(t, sl=sl, c=c, half=half, xqf_ready=xqf_ready,
                     prev_resid=prev_resid):
                if c == 0:
                    cx.wait(t, "s_v", xqf_ready)
                    if sl == 0:
                        cx.wait(t, "sd_g", w0_loaded)
                if c >= 2:
                    cx.wait(t, "s_v", prev_resid)
                for nb in range(2):
                    for e in range(ND):
                        t.matmul(psf[:, half * 1024 + nb * 512: half * 1024 + (nb + 1) * 512],
                                 xqf[:, e * 1024 + c * 128: e * 1024 + (c + 1) * 128],
                                 w0full[:, e * 1024 + nb * 512: e * 1024 + (nb + 1) * 512],
                                 start=(e == 0), stop=(e == ND - 1))
                cx.inc("s_t", t.drain())
            blk.tensor(mm_t)
            mm_done = cx.val("s_t")

            def epi_v(v, c=c, tc=tc, half=half, mm_done=mm_done):
                cx.wait(v, "s_t", mm_done)
                v.scalar_tensor_tensor(ffo[:, :], psf[:, half * 1024:(half + 1) * 1024],
                                       stat[:, 96 + c:97 + c], C["b0row"][:, :],
                                       ALU.mult, ALU.add)
                v.drain()
                v.tensor_add(xs[:, tc * 1024:(tc + 1) * 1024],
                             xs[:, tc * 1024:(tc + 1) * 1024], ffo[:, :])
                cx.inc("s_v", v.drain())
            blk.vector(epi_v)

    emit_ln(cx, xs, stat, ("s_v", cx.val("s_v")))
    es.close()
    return cx.val("s_v")



def build_program():
    nc = bass.Bass("TRN2", target_bir_lowering=False)

    def DI(name, shape, dt=F32):
        return nc.dram_tensor(name, list(shape), dt, kind="ExternalInput")

    dram = dict(
        x0=DI("x0", (T, D)),
        wqk0=DI("wqk0", (D, 128)),
        wv0=DI("wv0", (D, 64)),
        wint0=DI("wint0", (D, D), I32),
        ws0=DI("ws0", (1, 1)),
        bias0=DI("bias0", (1, D)),
        wqkv=DI("wqkv", (D, 3072)),
        bqkv=DI("bqkv", (1, 3072)),
        wg1=DI("wg1", (D, D)),
        bg1=DI("bg1", (1, D)),
        relpos_t=DI("relpos_t", (16, 512)),
        w1=DI("w1", (D, 4096)),
        b1=DI("b1", (1, 4096)),
        w2=DI("w2", (4096, 16)),
        b2=DI("b2", (1, 16)),
        wo=DI("wo", (D, D)),
        bo=DI("bo", (1, D)),
        clips=DI("clips", (1, 2)),
        wgn=DI("wgn", (2816, D)),
        wun=DI("wun", (2816, D)),
        wdn=DI("wdn", (D, 2816)),
    )
    out = nc.dram_tensor("out", [T, D], F32, kind="ExternalOutput")
    x1s = nc.dram_tensor("x1s", [T, D], F32)
    x2s = nc.dram_tensor("x2s", [T, D], F32)
    hs_scr = nc.dram_tensor("hs_scr", [2816, T], BF16)
    wgq = nc.dram_tensor("wgq", [2816, D], BF16)
    wuq = nc.dram_tensor("wuq", [2816, D], BF16)
    wdq = nc.dram_tensor("wdq", [D, 2816], BF16)
    wgT = nc.dram_tensor("wgT", [D, 2816], BF16)
    wuT = nc.dram_tensor("wuT", [D, 2816], BF16)
    wdT = nc.dram_tensor("wdT", [2816, D], BF16)
    cc1_in = nc.dram_tensor("cc1_in", [1, 16], F32)
    cc1_out = nc.dram_tensor("cc1_out", [1, 16], F32)
    cc2_in = nc.dram_tensor("cc2_in", [1, 16], F32)
    cc2_out = nc.dram_tensor("cc2_out", [1, 16], F32)
    q_scr = nc.dram_tensor("q_scr", [D, T], BF16)
    g_scr = nc.dram_tensor("g_scr", [D, T], BF16)
    k_scr = nc.dram_tensor("k_scr", [D, T], BF16)
    rtil_scr = nc.dram_tensor("rtil_scr", [16, 1792], BF16)
    scr = dict(x1s=x1s, x2s=x2s, hs=hs_scr, wgq=wgq, wuq=wuq, wdq=wdq,
               wgT=wgT, wuT=wuT, wdT=wdT, cc1_in=cc1_in, cc1_out=cc1_out,
               cc2_in=cc2_in, cc2_out=cc2_out, out=out,
               q_scr=q_scr, g_scr=g_scr, k_scr=k_scr, rtil=rtil_scr)

    es_top = ExitStack()
    with es_top:
        blk = es_top.enter_context(nc.Block())
        cx = Ctx(nc, blk)
        for s in ["sd_g", "sd_s", "sd_o", "sd_o2", "s_t", "s_v", "s_a", "s_g",
                  "sq0", "sq1", "st0", "st1", "gx0", "gx1", "gw0", "gw1"]:
            cx.sem(s)

        sb = lambda n, sh, dt: es_top.enter_context(nc.sbuf_tensor(n, sh, dt))
        C = dict(
            ident=sb("ident", [128, 128], BF16),
            identJ=sb("identJ", [128, 128], BF16),
            identf=sb("identf", [128, 128], F32),
            onesb=sb("onesb", [1, 128], BF16),
            onesf=sb("onesf2", [1, 128], F32),
            scal=sb("scal", [128, 16], F32),
            stat=sb("stat", [128, 192], F32),
        )
        es_l0 = ExitStack()
        C["b0row"] = es_l0.enter_context(nc.sbuf_tensor("b0row", [128, D], F32))
        C["wqk"] = es_l0.enter_context(nc.sbuf_tensor("wqk", [128, ND * 128], BF16))
        C["wv"] = es_l0.enter_context(nc.sbuf_tensor("wv", [128, ND * 64], BF16))

        es0 = phase0_setup(cx, C, dram)
        es0.close()

        es_xs = ExitStack()
        xs = es_xs.enter_context(nc.sbuf_tensor("xs", [128, NT * 1024], F32))

        def p1_s(s):
            cx.wait(s, "s_v", C["const_done"])
            cx.wait(s, "s_v", C["bcast_done"])
            for i in range(8):
                c0 = i * 4
                cx.inc("sd_s", s.dma_start(
                    out=xs[:, c0 * 1024:(c0 + 4) * 1024],
                    in_=AP(dram["x0"], c0 * 128 * D, [[D, 128], [128 * D, 4], [1, D]])), 16)
        blk.sync(p1_s)
        C["xs_loaded"] = cx.val("sd_s")

        if UPTO >= 2:
            ln0_done = phase2_l0_attn(cx, C, dram, xs)
        else:
            ln0_done = cx.val("s_v")
        if UPTO >= 5:
            ln1_done = phase3_l0_ffn(cx, C, dram, xs, ln0_done,
                                     safe_prefetch_sv=ln0_done - 2)
        else:
            ln1_done = ln0_done

        def pout_s(s):
            cx.wait(s, "s_v", ln1_done)
            for i in range(8):
                c0 = i * 4
                cx.inc("sd_o", s.dma_start(
                    out=AP(x1s, c0 * 128 * D, [[D, 128], [128 * D, 4], [1, D]]),
                    in_=xs[:, c0 * 1024:(c0 + 4) * 1024]), 16)
                if UPTO < 6:
                    cx.inc("sd_o", s.dma_start(
                        out=AP(out, c0 * 128 * D, [[D, 128], [128 * D, 4], [1, D]]),
                        in_=xs[:, c0 * 1024:(c0 + 4) * 1024]), 16)
            cx.wait(s, "sd_o")
        blk.sync(pout_s)
        x1s_done = cx.val("sd_o")

        if DBG:
            def dbg_fin(g):
                cx.wait(g, "sd_o2")
            blk.gpsimd(dbg_fin)

        es_xs.close()
        es_l0.close()

        if UPTO >= 6:
            wq_done = phase_wprep(cx, C, dram, scr)
            l1 = phase4_l1a(cx, C, dram, scr, x1s_done)
            if UPTO >= 7:
                x2 = phase5_l1b(cx, C, dram, scr, l1, wq_done,
                                write_out_dbg=(UPTO == 7))
                if UPTO >= 8:
                    l1["es"].close()
                    phase6_l1ffn(cx, C, dram, scr, l1, x2, wq_done)
                else:
                    l1["es"].close()

                    def fin_o(s):
                        cx.wait(s, "sd_o")
                        cx.wait(s, "sd_s")
                        cx.wait(s, "sd_g")
                    blk.sync(fin_o)
            else:
                l1["es"].close()

                def fin_o(s):
                    cx.wait(s, "sd_o")
                    cx.wait(s, "sd_s")
                    cx.wait(s, "sd_g")
                blk.sync(fin_o)
    return nc


def kernel(L0_attn_x, L0_attn_Wq, L0_attn_Wk, L0_attn_Wv, L0_ffn_w_int8,
           L0_ffn_weight_scale, L0_ffn_bias, L1_attn_Wqkv, L1_attn_bqkv,
           L1_attn_Wg, L1_attn_bg, L1_attn_relpos_emb, L1_attn_W1, L1_attn_b1,
           L1_attn_W2, L1_attn_b2, L1_attn_Wo, L1_attn_bo, L1_ffn_clip_x,
           L1_ffn_clip_down, L1_ffn_w_gate, L1_ffn_w_up, L1_ffn_w_down):
    a = lambda x: np.ascontiguousarray(np.asarray(x))
    x = a(L0_attn_x)

    shared = dict(
        wqk0=a(np.concatenate([np.asarray(L0_attn_Wq).T, np.asarray(L0_attn_Wk).T], axis=1)),
        wv0=a(np.asarray(L0_attn_Wv).T),
        wint0=a(np.asarray(L0_ffn_w_int8).T),
        ws0=a(np.asarray(L0_ffn_weight_scale, dtype=np.float32).reshape(1, 1)),
        bias0=a(np.asarray(L0_ffn_bias).reshape(1, D)),
        wqkv=a(L1_attn_Wqkv),
        bqkv=a(np.asarray(L1_attn_bqkv).reshape(1, 3072)),
        wg1=a(L1_attn_Wg),
        bg1=a(np.asarray(L1_attn_bg).reshape(1, D)),
        relpos_t=a(np.asarray(L1_attn_relpos_emb).T),
        w1=a(L1_attn_W1),
        b1=a(np.asarray(L1_attn_b1).reshape(1, 4096)),
        w2=a(L1_attn_W2),
        b2=a(np.asarray(L1_attn_b2).reshape(1, 16)),
        wo=a(L1_attn_Wo),
        bo=a(np.asarray(L1_attn_bo).reshape(1, D)),
        clips=a(np.concatenate([np.asarray(L1_ffn_clip_x).reshape(1),
                                np.asarray(L1_ffn_clip_down).reshape(1)]).reshape(1, 2)),
        wgn=a(L1_ffn_w_gate),
        wun=a(L1_ffn_w_up),
        wdn=a(L1_ffn_w_down),
    )
    nc = build_program()
    in_maps = [dict(x0=a(x[i]), **shared) for i in range(N_CORES)]
    res = run_bass_kernel_spmd(nc, in_maps, core_ids=list(range(N_CORES)))
    return np.stack([np.asarray(r["out"]) for r in res.results], axis=0)




def phase_wprep(cx, C, dram, scr):
    nc, blk = cx.nc, cx.blk
    es = ExitStack()
    wnat = es.enter_context(nc.sbuf_tensor("wnat", [128, 2 * 2816], F32))
    wtmp = es.enter_context(nc.sbuf_tensor("wtmp", [128, 2816], F32))
    wdeq = es.enter_context(nc.sbuf_tensor("wdeq", [128, 2816], BF16))
    wst = es.enter_context(nc.sbuf_tensor("wst", [128, 4], F32))
    tbuf = es.enter_context(nc.sbuf_tensor("tbuf", [128, 2816], BF16))

    specs = [(dram["wgn"], scr["wgq"], scr["wgT"], 2816, D),
             (dram["wun"], scr["wuq"], scr["wuT"], 2816, D),
             (dram["wdn"], scr["wdq"], scr["wdT"], D, 2816)]

    q_marks = []
    st_marks = []
    ci = 0
    for nat, qn, qt, R, Dc in specs:
        for ch in range(R // 128):
            buf = ci % 2

            def ld_s(s, nat=nat, ch=ch, buf=buf, Dc=Dc, ci=ci):
                if ci >= 2:
                    s.wait_ge(cx.h("s_v"), q_marks[ci - 2])
                cx.inc("sq%d" % buf, s.dma_start(
                    out=wnat[:, buf * 2816:buf * 2816 + Dc],
                    in_=nat[ch * 128:(ch + 1) * 128, :]), 16)
            blk.sync(ld_s)
            ld_done = cx.val("sq%d" % buf)

            def qv(v, buf=buf, Dc=Dc, ld_done=ld_done, ci=ci):
                cx.wait(v, "sq%d" % buf, ld_done)
                if ci >= 1:
                    cx.wait(v, "sd_g", st_marks[ci - 1])
                w = wnat[:, buf * 2816:buf * 2816 + Dc]
                v.tensor_reduce(wst[:, 0:1], w, AX.X, ALU.max,
                                apply_absolute_value=True)
                v.drain()
                v.reciprocal(wst[:, 1:2], wst[:, 0:1])
                v.drain()
                v.tensor_scalar_mul(wst[:, 1:2], wst[:, 1:2], 127.0)
                v.tensor_scalar_mul(wst[:, 2:3], wst[:, 0:1], 1.0 / 127.0)
                v.drain()
                v.tensor_scalar(wtmp[:, 0:Dc], w, wst[:, 1:2], MAGIC,
                                ALU.mult, ALU.add)
                v.drain()
                v.tensor_scalar(wtmp[:, 0:Dc], wtmp[:, 0:Dc], MAGIC, 127.0,
                                ALU.subtract, ALU.min)
                v.drain()
                v.tensor_scalar(wdeq[:, 0:Dc], wtmp[:, 0:Dc], -128.0,
                                wst[:, 2:3], ALU.max, ALU.mult)
                cx.inc("s_v", v.drain())
            blk.vector(qv)
            q_marks.append(cx.val("s_v"))

            def st_g(g, qn=qn, ch=ch, Dc=Dc, qd=q_marks[-1]):
                cx.wait(g, "s_v", qd)
                cx.inc("sd_g", g.dma_start(
                    out=qn[ch * 128:(ch + 1) * 128, :], in_=wdeq[:, 0:Dc]), 16)
            blk.gpsimd(st_g)
            st_marks.append(cx.val("sd_g"))
            ci += 1
    qn_all = cx.val("sd_g")

    tb_marks = []
    for nat, qn, qt, R, Dc in specs:
        for j in range(Dc // 128):
            def trs(s, qn=qn, j=j, R=R, first=(len(tb_marks) == 0)):
                if first:
                    cx.wait(s, "sd_g", qn_all)
                else:
                    cx.wait(s, "sd_g", tb_marks[-1])
                cx.inc("sd_s", s.dma_start_transpose(
                    tbuf[:, 0:R], qn[:, j * 128:(j + 1) * 128]), 16)
            blk.sync(trs)
            tr_done = cx.val("sd_s")

            def tst(g, qt=qt, j=j, R=R, tr_done=tr_done):
                cx.wait(g, "sd_s", tr_done)
                cx.inc("sd_g", g.dma_start(
                    out=qt[j * 128:(j + 1) * 128, :], in_=tbuf[:, 0:R]), 16)
            blk.gpsimd(tst)
            tb_marks.append(cx.val("sd_g"))
    wq_done_g = cx.val("sd_g")

    def wq_fin(g):
        cx.wait(g, "sd_g", wq_done_g)
    blk.gpsimd(wq_fin)
    es.close()
    return wq_done_g


def phase4_l1a(cx, C, dram, scr, x1s_done):
    nc, blk = cx.nc, cx.blk
    stat, ident = C["stat"], C["ident"]
    VW = 1040

    es_l1 = ExitStack()
    vtm = es_l1.enter_context(nc.sbuf_tensor("vtm65", [128, NT * VW], BF16))
    bias3 = es_l1.enter_context(nc.sbuf_tensor("bias3", [128, 48], F32))
    rtil = es_l1.enter_context(nc.sbuf_tensor("rtil", [16, 1792], BF16))
    relsb = es_l1.enter_context(nc.sbuf_tensor("relsb", [16, 512], F32))
    l1st = es_l1.enter_context(nc.sbuf_tensor("l1st", [16, 16], F32))
    qkrow = es_l1.enter_context(nc.sbuf_tensor("qkrow", [1, 64], F32))
    bcol = es_l1.enter_context(nc.sbuf_tensor("bcol", [128, 64], F32))
    kpre = es_l1.enter_context(nc.sbuf_tensor("kpre", [128, 8 * 128], BF16))

    es_a = ExitStack()
    browb = es_a.enter_context(nc.sbuf_tensor("browb", [1, 2048], BF16))
    brow = es_a.enter_context(nc.sbuf_tensor("brow2", [1, 1024], F32))
    wbuf = es_a.enter_context(nc.sbuf_tensor("wbuf", [128, 2 * 4096], BF16))
    xbt = es_a.enter_context(nc.sbuf_tensor("xbt", [128, 2 * 1024], BF16))
    xfm = es_a.enter_context(nc.sbuf_tensor("xfm4", [128, 8 * 1024], BF16))
    st2 = es_a.enter_context(nc.sbuf_tensor("st2", [128, 2 * 1024], BF16))
    pst = es_a.enter_context(nc.psum_tensor("pst4", [128, 2048], BF16))
    psp = es_a.enter_context(nc.psum_tensor("psp", [128, 2048], F32))
    psE = es_a.enter_context(nc.psum_tensor("psE", [128, 1024], F32))

    def s4_g(g):
        cx.wait(g, "sd_o", x1s_done)
        cx.inc("sd_g", g.dma_start(out=browb[:, 0:1024],
                                   in_=dram["bqkv"][:, 2048:3072]), 16)
        cx.inc("sd_g", g.dma_start(out=relsb[:, :], in_=dram["relpos_t"][:, :]), 16)
        cx.inc("sd_g", g.dma_start(
            out=AP(l1st, 0, [[16, 16], [1, 1]]), in_=dram["b2"][:, :]), 16)
        cx.inc("sd_g", g.dma_start(out=browb[:, 1024:2048], in_=dram["bo"][:, :]), 16)
    blk.gpsimd(s4_g)
    s4_ld = cx.val("sd_g")

    def s4_v(v):
        cx.wait(v, "sd_g", s4_ld)
        v.tensor_reduce(l1st[:, 1:2], relsb[:, :], AX.X, ALU.max)
        v.tensor_scalar(rtil[:, 0:638], AP(relsb, 0, [[512, 16], [0, 638]]),
                        1.0, None, ALU.mult)
        v.tensor_copy(rtil[:, 638:1149], relsb[:, 0:511])
        v.tensor_scalar(rtil[:, 1149:1792], AP(relsb, 510, [[512, 16], [0, 643]]),
                        1.0, None, ALU.mult)
        for tc in range(NT):
            v.memset(AP(vtm, tc * VW + 64, [[vtm.shape[1], 128], [65, 16], [1, 1]]), 1.0)
        cx.inc("s_v", v.drain())
    blk.vector(s4_v)
    s4_prep = cx.val("s_v")

    def rt_g(g):
        cx.wait(g, "s_v", s4_prep)
        cx.inc("sd_g", g.dma_start(out=scr["rtil"][:, :], in_=rtil[:, :]), 16)
    blk.gpsimd(rt_g)

    rowsrc = [(dram["bqkv"], 0, 0), (dram["bqkv"], 1024, 8), (dram["bg1"], 0, 16),
              (dram["w1"], -1, 24), (dram["w1"], -1, 28), (dram["w1"], -1, 32),
              (dram["w1"], -1, 36)]
    rowsrc = [(dram["bqkv"], 0, 0), (dram["bqkv"], 1024, 8), (dram["bg1"], 0, 16)]
    prev_tr = None
    sv_br0 = cx.val("s_v")
    for gi, (ten, c0, b0) in enumerate(rowsrc):
        def br_g(g, ten=ten, c0=c0, prev_tr=prev_tr):
            if prev_tr is not None:
                g.wait_ge(cx.h("s_t"), prev_tr)
            else:
                cx.wait(g, "s_v", s4_prep)
            cx.inc("sd_g", g.dma_start(out=brow[:, :], in_=ten[0:1, c0:c0 + 1024]), 16)
        blk.gpsimd(br_g)
        br_ld = cx.val("sd_g")

        def br_t(t, br_ld=br_ld, gi=gi):
            cx.wait(t, "sd_g", br_ld)
            for i in range(8):
                n = gi * 8 + i
                if n >= 2:
                    t.wait_ge(cx.h("s_v"), sv_br0 + n - 1)
                t.transpose(psp[:, (i % 2) * 1024:(i % 2) * 1024 + 1],
                            brow[0:1, i * 128:(i + 1) * 128], C["onesf"][0:1, 0:1])
                cx.inc("s_t", t.drain())
        blk.tensor(br_t)
        tr0 = cx.val("s_t") - 8
        prev_tr = cx.val("s_t")

        def br_v(v, b0=b0, tr0=tr0):
            for i in range(8):
                cx.wait(v, "s_t", tr0 + i + 1)
                v.tensor_copy(bcol[:, b0 + i:b0 + i + 1],
                              psp[:, (i % 2) * 1024:(i % 2) * 1024 + 1])
                cx.inc("s_v", v.drain())
        blk.vector(br_v)

    def s4_v2(v):
        v.tensor_scalar_mul(bcol[:, 0:8], bcol[:, 0:8], 0.125)
        cx.inc("s_v", v.drain())
    blk.vector(s4_v2)
    bcol_done = cx.val("s_v")

    prev_g_store_marks = None
    prev_ring_st = None
    prev_g_done_a = None
    for sl in range(4):
        tc0 = sl * 8

        st_base = cx.val("s_t")
        sv_base = cx.val("s_v")

        xld_marks = []

        def x_g(g, tc0=tc0, st_base=st_base):
            for c in range(8):
                if c >= 2:
                    g.wait_ge(cx.h("s_t"), st_base + (c - 1) * 2)
                xld_marks.append(cx.inc("gx%d" % (c % 2), g.dma_start(
                    out=xbt[:, (c % 2) * 1024:(c % 2 + 1) * 1024],
                    in_=scr["x1s"][(tc0 + c) * 128:(tc0 + c + 1) * 128, :]), 16))
        blk.gpsimd(x_g)

        def x_t(t, sl=sl, sv_base=sv_base, xld_marks=xld_marks):
            if sl == 0:
                cx.wait(t, "s_v", bcol_done)
            for c in range(8):
                t.wait_ge(cx.h("gx%d" % (c % 2)), xld_marks[c])
                for grp in range(2):
                    gi = c * 2 + grp
                    if gi >= 2:
                        t.wait_ge(cx.h("s_v"), sv_base + gi - 1)
                    for j in range(4):
                        dc = grp * 4 + j
                        slot = (gi % 2) * 8 + j
                        t.transpose(pst[:, slot * 128:(slot + 1) * 128],
                                    xbt[:, (c % 2) * 1024 + dc * 128:(c % 2) * 1024 + (dc + 1) * 128],
                                    ident[:, :])
                    cx.inc("s_t", t.drain())
        blk.tensor(x_t)

        def x_v(v, st_base=st_base):
            for gi in range(16):
                c, grp = gi // 2, gi % 2
                cx.wait(v, "s_t", st_base + gi + 1)
                for j in range(4):
                    dc = grp * 4 + j
                    slot = (gi % 2) * 8 + j
                    v.tensor_copy(xfm[:, dc * 1024 + c * 128: dc * 1024 + (c + 1) * 128],
                                  pst[:, slot * 128:(slot + 1) * 128])
                cx.inc("s_v", v.drain())
        blk.vector(x_v)
        xfm_ready = cx.val("s_v")

        wseq = [(dram["wqkv"], 1024), (dram["wqkv"], 1536),
                (dram["wqkv"], 2048), (dram["wqkv"], 2560),
                (dram["wqkv"], 0), (dram["wqkv"], 512),
                (dram["wg1"], 0), (dram["wg1"], 512)]
        incs_per_half = [4, 4, 8, 8, 9, 9, 4, 4] if sl == 0 else [4, 4, 8, 8, 4, 4, 4, 4]
        if sl == 0:
            incs_per_half = [4, 4, 8, 8, 4 + 4 * 3, 4 + 4 * 3, 4, 4]
        proj_st0 = cx.val("s_t")
        half_marks = [proj_st0 + sum(incs_per_half[:i + 1]) for i in range(8)]

        wld_marks = []

        def w_g(g, wseq=wseq, half_marks=half_marks, prs=prev_ring_st):
            for i, (ten, c0) in enumerate(wseq):
                if i >= 2:
                    g.wait_ge(cx.h("s_t"), half_marks[i - 2])
                elif prs is not None:
                    g.wait_ge(cx.h("s_t"), prs)
                wld_marks.append(cx.inc("gw%d" % (i % 2), g.dma_start(
                    out=AP(wbuf, (i % 2) * 4096, [[wbuf.shape[1], 128], [512, 8], [1, 512]]),
                    in_=AP(ten, c0, [[ten.shape[1], 128], [128 * ten.shape[1], 8], [1, 512]])), 16))
        blk.gpsimd(w_g)

        sa_k = cx.val("s_a")

        def k_t(t, xfm_ready=xfm_ready, sa_k=sa_k, wld_marks=wld_marks,
                pgd=prev_g_done_a):
            cx.wait(t, "s_v", xfm_ready)
            if pgd is not None:
                t.wait_ge(cx.h("s_a"), pgd)
            for dc in range(8):
                half = dc // 4
                if dc % 4 == 0:
                    t.wait_ge(cx.h("gw%d" % (half % 2)), wld_marks[half])
                if dc >= 2:
                    t.wait_ge(cx.h("s_a"), sa_k + dc - 1)
                for nb in range(2):
                    for e in range(ND):
                        t.matmul(psp[:, (dc % 2) * 1024 + nb * 512:(dc % 2) * 1024 + (nb + 1) * 512],
                                 wbuf[:, half * 4096 + e * 512 + (dc % 4) * 128: half * 4096 + e * 512 + (dc % 4) * 128 + 128],
                                 xfm[:, e * 1024 + nb * 512: e * 1024 + (nb + 1) * 512],
                                 start=(e == 0), stop=(e == ND - 1))
                cx.inc("s_t", t.drain())
        blk.tensor(k_t)
        kst0 = cx.val("s_t") - 8

        k_store_marks = []
        k_marks_a = []
        for dc in range(8):
            def k_a(s, sl=sl, dc=dc, kst0=kst0, pgs=prev_g_store_marks):
                cx.wait(s, "s_t", kst0 + dc + 1)
                if dc >= 2:
                    cx.wait(s, "st%d" % (dc % 2), k_store_marks[dc - 2])
                elif pgs is not None:
                    cx.wait(s, "st%d" % (dc % 2), pgs[6 + dc])
                s.activation(st2[:, (dc % 2) * 1024:(dc % 2 + 1) * 1024],
                             psp[:, (dc % 2) * 1024:(dc % 2 + 1) * 1024],
                             AF.Identity, bias=bcol[:, 8 + dc:9 + dc], scale=1.0)
                if sl == 0:
                    s.drain()
                    s.copy(kpre[:, dc * 128:(dc + 1) * 128],
                           st2[:, (dc % 2) * 1024:(dc % 2) * 1024 + 128])
                cx.inc("s_a", s.drain())
            blk.scalar(k_a)
            k_marks_a.append(cx.val("s_a"))

            def k_s(s, sl=sl, dc=dc, ka=k_marks_a[-1]):
                cx.wait(s, "s_a", ka)
                cx.inc("st%d" % (dc % 2), s.dma_start(
                    out=scr["k_scr"][dc * 128:(dc + 1) * 128, sl * 1024:(sl + 1) * 1024],
                    in_=st2[:, (dc % 2) * 1024:(dc % 2 + 1) * 1024]), 16)
            blk.sync(k_s)
            k_store_marks.append(cx.val("st%d" % (dc % 2)))
        k_done_a = cx.val("s_a")

        sv_v = cx.val("s_v")

        def v_t(t, sv_v=sv_v, k_done_a=k_done_a, wld_marks=wld_marks):
            cx.wait(t, "s_a", k_done_a)
            for vdb in range(2):
                t.wait_ge(cx.h("gw%d" % ((vdb + 2) % 2)), wld_marks[vdb + 2])
                for tc in range(8):
                    i = vdb * 8 + tc
                    if i >= 2:
                        t.wait_ge(cx.h("s_v"), sv_v + i - 1)
                    t.matmul(psp[:, (i % 2) * 1024:(i % 2) * 1024 + 512],
                             C["onesb"][0:1, 0:128],
                             browb[0:1, vdb * 512:(vdb + 1) * 512],
                             start=True, stop=False)
                    for e in range(ND):
                        t.matmul(psp[:, (i % 2) * 1024:(i % 2) * 1024 + 512],
                                 xfm[:, e * 1024 + tc * 128: e * 1024 + (tc + 1) * 128],
                                 wbuf[:, (vdb % 2) * 4096 + e * 512: (vdb % 2) * 4096 + (e + 1) * 512],
                                 start=False, stop=(e == ND - 1))
                    cx.inc("s_t", t.drain())
        blk.tensor(v_t)
        vst0 = cx.val("s_t") - 16

        def v_v(v, tc0=tc0, vst0=vst0):
            for vdb in range(2):
                for tc in range(8):
                    i = vdb * 8 + tc
                    cx.wait(v, "s_t", vst0 + i + 1)
                    v.tensor_copy(
                        AP(vtm, (tc0 + tc) * VW + vdb * 8 * 65,
                           [[vtm.shape[1], 128], [65, 8], [1, 64]]),
                        AP(psp, (i % 2) * 1024, [[psp.shape[1], 128], [64, 8], [1, 64]]))
                    cx.inc("s_v", v.drain())
        blk.vector(v_v)
        v_done_v = cx.val("s_v")

        q_marks_a = []
        q_store_marks = []
        pre_marks = []
        first_q = True
        for dc in range(8):
            half4 = 4 + dc // 4

            def q_t(t, dc=dc, half4=half4, first_q=first_q,
                    v_done_v=v_done_v, wld_marks=wld_marks):
                if first_q:
                    cx.wait(t, "s_v", v_done_v)
                if dc % 4 == 0:
                    t.wait_ge(cx.h("gw%d" % (half4 % 2)), wld_marks[half4])
                if dc >= 2:
                    cx.wait(t, "s_a", q_marks_a[dc - 2])
                for nb in range(2):
                    for e in range(ND):
                        t.matmul(psp[:, (dc % 2) * 1024 + nb * 512:(dc % 2) * 1024 + (nb + 1) * 512],
                                 wbuf[:, (half4 % 2) * 4096 + e * 512 + (dc % 4) * 128: (half4 % 2) * 4096 + e * 512 + (dc % 4) * 128 + 128],
                                 xfm[:, e * 1024 + nb * 512: e * 1024 + (nb + 1) * 512],
                                 start=(e == 0), stop=(e == ND - 1))
                cx.inc("s_t", t.drain())
            blk.tensor(q_t)
            first_q = False
            q_mm = cx.val("s_t")

            def q_a(s, sl=sl, dc=dc, q_mm=q_mm):
                cx.wait(s, "s_t", q_mm)
                if dc < 2:
                    cx.wait(s, "st%d" % (dc % 2), k_store_marks[6 + dc])
                else:
                    cx.wait(s, "st%d" % (dc % 2), q_store_marks[dc - 2])
                    if sl == 0:
                        cx.wait(s, "s_t", pre_marks[dc - 2])
                s.activation(st2[:, (dc % 2) * 1024:(dc % 2 + 1) * 1024],
                             psp[:, (dc % 2) * 1024:(dc % 2 + 1) * 1024],
                             AF.Identity, bias=bcol[:, dc:dc + 1], scale=0.125)
                cx.inc("s_a", s.drain())
            blk.scalar(q_a)
            q_marks_a.append(cx.val("s_a"))

            def q_s(s, sl=sl, dc=dc, qa=q_marks_a[-1]):
                cx.wait(s, "s_a", qa)
                cx.inc("st%d" % (dc % 2), s.dma_start(
                    out=scr["q_scr"][dc * 128:(dc + 1) * 128, sl * 1024:(sl + 1) * 1024],
                    in_=st2[:, (dc % 2) * 1024:(dc % 2 + 1) * 1024]), 16)
            blk.sync(q_s)
            q_store_marks.append(cx.val("st%d" % (dc % 2)))

            if sl == 0:
                def pre_t(t, dc=dc, qa=q_marks_a[-1]):
                    cx.wait(t, "s_a", qa)
                    t.matmul(psE[:, 0:512], kpre[0:64, dc * 128:(dc + 1) * 128],
                             st2[0:64, (dc % 2) * 1024:(dc % 2) * 1024 + 512],
                             start=True, stop=True, tile_position=(0, 0))
                    t.matmul(psE[:, 512:1024], kpre[64:128, dc * 128:(dc + 1) * 128],
                             st2[64:128, (dc % 2) * 1024:(dc % 2) * 1024 + 512],
                             start=True, stop=True, tile_position=(64, 0))
                    cx.inc("s_t", t.drain())
                blk.tensor(pre_t)
                pre_mm = cx.val("s_t")

                def pre_v(v, pre_mm=pre_mm):
                    cx.wait(v, "s_t", pre_mm)
                    v.tensor_reduce(stat[:, 90:91], psE[:, 0:512], AX.X, ALU.max)
                    v.tensor_reduce(stat[:, 91:92], psE[:, 512:1024], AX.X, ALU.max)
                    cx.inc("s_v", v.drain())
                blk.vector(pre_v)
                pre_red = cx.val("s_v")

                def pre_t2(t, pre_red=pre_red):
                    cx.wait(t, "s_v", pre_red)
                    t.transpose(psE[0:1, 0:128], stat[:, 90:91], C["identf"][:, :])
                    t.transpose(psE[0:1, 512:640], stat[:, 91:92], C["identf"][:, :])
                    cx.inc("s_t", t.drain())
                blk.tensor(pre_t2)
                pre_tr = cx.val("s_t")

                def pre_v2(v, dc=dc, pre_tr=pre_tr):
                    cx.wait(v, "s_t", pre_tr)
                    v.tensor_reduce(qkrow[0:1, 2 * dc:2 * dc + 1], psE[0:1, 0:128],
                                    AX.X, ALU.max)
                    v.tensor_reduce(qkrow[0:1, 2 * dc + 1:2 * dc + 2],
                                    psE[0:1, 512:640], AX.X, ALU.max)
                    cx.inc("s_v", v.drain())
                blk.vector(pre_v2)

                def pre_fin(t, fin=cx.val("s_v")):
                    cx.wait(t, "s_v", fin)
                    cx.inc("s_t", t.drain())
                blk.tensor(pre_fin)
                pre_marks.append(cx.val("s_t"))
            else:
                pre_marks.append(cx.val("s_t"))
        q_done_a = cx.val("s_a")

        g_marks_a = []
        g_store_marks = []
        for dc in range(8):
            half6 = 6 + dc // 4

            def g_t(t, dc=dc, half6=half6, q_done_a=q_done_a,
                    wld_marks=wld_marks):
                if dc == 0:
                    cx.wait(t, "s_a", q_done_a)
                if dc % 4 == 0:
                    t.wait_ge(cx.h("gw%d" % (half6 % 2)), wld_marks[half6])
                if dc >= 2:
                    cx.wait(t, "s_a", g_marks_a[dc - 2])
                for nb in range(2):
                    for e in range(ND):
                        t.matmul(psp[:, (dc % 2) * 1024 + nb * 512:(dc % 2) * 1024 + (nb + 1) * 512],
                                 wbuf[:, (half6 % 2) * 4096 + e * 512 + (dc % 4) * 128: (half6 % 2) * 4096 + e * 512 + (dc % 4) * 128 + 128],
                                 xfm[:, e * 1024 + nb * 512: e * 1024 + (nb + 1) * 512],
                                 start=(e == 0), stop=(e == ND - 1))
                cx.inc("s_t", t.drain())
            blk.tensor(g_t)
            g_mm = cx.val("s_t")

            def g_a(s, g_mm=g_mm, dc=dc):
                cx.wait(s, "s_t", g_mm)
                if dc < 2:
                    cx.wait(s, "st%d" % (dc % 2), q_store_marks[6 + dc])
                else:
                    cx.wait(s, "st%d" % (dc % 2), g_store_marks[dc - 2])
                s.activation(st2[:, (dc % 2) * 1024:(dc % 2 + 1) * 1024],
                             psp[:, (dc % 2) * 1024:(dc % 2 + 1) * 1024],
                             AF.Sigmoid, bias=bcol[:, 16 + dc:17 + dc], scale=1.0)
                cx.inc("s_a", s.drain())
            blk.scalar(g_a)
            g_marks_a.append(cx.val("s_a"))

            def g_s(s, sl=sl, dc=dc, ga=g_marks_a[-1]):
                cx.wait(s, "s_a", ga)
                cx.inc("st%d" % (dc % 2), s.dma_start(
                    out=scr["g_scr"][dc * 128:(dc + 1) * 128, sl * 1024:(sl + 1) * 1024],
                    in_=st2[:, (dc % 2) * 1024:(dc % 2 + 1) * 1024]), 16)
            blk.sync(g_s)
            g_store_marks.append(cx.val("st%d" % (dc % 2)))
        g_done_a = cx.val("s_a")
        prev_g_store_marks = g_store_marks
        prev_g_done_a = g_done_a

        ring_done_st = cx.val("s_t")

        if sl == 0:
            last = cx.val("s_v")

            def c0_t(t, last=last):
                cx.wait(t, "s_v", last)
                t.transpose(psE[0:16, 0:1], qkrow[0:1, 0:16], C["onesf"][0:1, 0:1])
                cx.inc("s_t", t.drain())
            blk.tensor(c0_t)
            c0_tr = cx.val("s_t")

            def c0_v(v, c0_tr=c0_tr):
                cx.wait(v, "s_t", c0_tr)
                v.tensor_copy(l1st[:, 3:4], psE[0:16, 0:1])
                v.drain()
                v.tensor_add(l1st[:, 4:5], l1st[:, 3:4], l1st[:, 1:2])
                v.drain()
                v.tensor_scalar_mul(l1st[:, 5:6], l1st[:, 4:5], -1.0)
                v.tensor_sub(l1st[:, 6:7], relsb[:, 0:1], l1st[:, 4:5])
                v.tensor_sub(l1st[:, 7:8], relsb[:, 510:511], l1st[:, 4:5])
                cx.inc("s_v", v.drain())
            blk.vector(c0_v)
            b3m = cx.val("s_v")

            def b3_t(t, b3m=b3m):
                cx.wait(t, "s_v", b3m)
                for vv in range(3):
                    t.transpose(psE[0:1, 512 + vv * 16:512 + vv * 16 + 16],
                                l1st[:, 5 + vv:6 + vv], C["identf"][0:16, 0:16])
                cx.inc("s_t", t.drain())
            blk.tensor(b3_t)
            b3t = cx.val("s_t")

            def b3_v(v, b3t=b3t):
                cx.wait(v, "s_t", b3t)
                v.tensor_copy(qkrow[0:1, 16:64], psE[0:1, 512:560])
                cx.inc("s_v", v.drain())
            blk.vector(b3_v)
            b3v = cx.val("s_v")

            def b3_t2(t, b3v=b3v):
                cx.wait(t, "s_v", b3v)
                t.matmul(psE[:, 0:48], C["onesf"][0:1, 0:128], qkrow[0:1, 16:64],
                         start=True, stop=True)
                cx.inc("s_t", t.drain())
            blk.tensor(b3_t2)
            b3t2 = cx.val("s_t")

            def b3_v2(v, b3t2=b3t2):
                cx.wait(v, "s_t", b3t2)
                v.tensor_copy(bias3[:, :], psE[:, 0:48])
                cx.inc("s_v", v.drain())
            blk.vector(b3_v2)

        prev_ring_st = ring_done_st

    done_v = cx.val("s_v")
    done_s = cx.val("sd_s")
    done_a = cx.val("s_a")
    done_st = (cx.val("st0"), cx.val("st1"))
    es_a.close()
    return dict(es=es_l1, vtm=vtm, bias3=bias3, rtil=rtil,
                relsb=relsb, l1st=l1st, VW=VW,
                done_v=done_v, done_s=done_s, done_a=done_a, done_st=done_st)


def phase5_l1b(cx, C, dram, scr, l1, wq_done_g, write_out_dbg=False):
    nc, blk = cx.nc, cx.blk
    stat, ident = C["stat"], C["ident"]
    vtm, bias3 = l1["vtm"], l1["bias3"]
    rtil, relsb, l1st = l1["rtil"], l1["relsb"], l1["l1st"]
    VW = l1["VW"]

    es = ExitStack()
    kbuf = es.enter_context(nc.sbuf_tensor("kbuf", [128, 2 * T], BF16))
    qsl = es.enter_context(nc.sbuf_tensor("qsl", [128, 8 * 512], BF16))
    gsl = es.enter_context(nc.sbuf_tensor("gsl", [128, 8 * 512], BF16))
    mbuf = es.enter_context(nc.sbuf_tensor("mbuf", [128, 2 * 1664], BF16))
    EA5 = es.enter_context(nc.sbuf_tensor("EA5", [128, 2 * 512], BF16))
    EB5 = es.enter_context(nc.sbuf_tensor("EB5", [128, 2 * 512], BF16))
    gated = es.enter_context(nc.sbuf_tensor("gated", [128, 8 * 512], BF16))
    pvB = es.enter_context(nc.sbuf_tensor("pvB", [64, 512], F32))
    lr65 = es.enter_context(nc.sbuf_tensor("lr65", [65, 1024], F32))
    ones65 = es.enter_context(nc.sbuf_tensor("ones65", [65, 64], F32))
    shid = es.enter_context(nc.sbuf_tensor("shid", [64, 128], F32))
    rbc = es.enter_context(nc.sbuf_tensor("rbc", [128, 512], F32))
    grbc = es.enter_context(nc.sbuf_tensor("grbc", [128, 512], F32))
    borow = es.enter_context(nc.sbuf_tensor("borow", [1, 1024], BF16))
    wo64 = es.enter_context(nc.sbuf_tensor("wo64", [128, 8 * 1024], BF16))
    x1c = es.enter_context(nc.sbuf_tensor("x1c", [128, 2 * 1024], F32))
    lnx = es.enter_context(nc.sbuf_tensor("lnx", [128, 4 * 1024], F32))
    ln5q = es.enter_context(nc.sbuf_tensor("ln5q", [128, 1024], BF16))
    outc = es.enter_context(nc.sbuf_tensor("outc5", [128, 1024], F32))
    psA = es.enter_context(nc.psum_tensor("psA5", [128, 1024], F32))
    psB = es.enter_context(nc.psum_tensor("psB5", [128, 1024], F32))
    pso = es.enter_context(nc.psum_tensor("pso5", [65, 1024], F32))

    def wo_g(g):
        cx.wait(g, "s_v", l1["done_v"])
        cx.wait(g, "st0", l1["done_st"][0])
        cx.wait(g, "st1", l1["done_st"][1])
        cx.wait(g, "sd_g", wq_done_g)
        cx.inc("sd_g", g.dma_start(out=borow[:, :], in_=dram["bo"][:, :]), 16)
        for e in range(ND):
            cx.inc("sd_g", g.dma_start(
                out=wo64[:, e * 1024:(e + 1) * 1024],
                in_=dram["wo"][e * 128:(e + 1) * 128, :]), 16)
    blk.gpsimd(wo_g)
    wo_ld = cx.val("sd_g")

    def c5_v(v):
        cx.wait(v, "s_v", l1["done_v"])
        cx.wait(v, "st0", l1["done_st"][0])
        cx.wait(v, "st1", l1["done_st"][1])
        v.memset(ones65[64:65, :], 1.0)
        v.memset(shid[:, :], 0.0)
        v.drain()
        v.tensor_copy(shid[:, 64:128], C["identf"][0:64, 0:64])
        cx.inc("s_v", v.drain())
    blk.vector(c5_v)
    c5_done = cx.val("s_v")

    x2_marks = []
    prev_wo_marks = None
    pair_pp_st = 0
    for sl in range(8):
        prev_sl_st = cx.val("s_t")

        def qg_s(s, sl=sl, prev_sl_st=prev_sl_st):
            s.wait_ge(cx.h("s_t"), prev_sl_st)
            if sl == 0:
                cx.wait(s, "sd_s", l1["done_s"])
                cx.wait(s, "st0", l1["done_st"][0])
                cx.wait(s, "st1", l1["done_st"][1])
            for dc in range(8):
                cx.inc("sd_s", s.dma_start(
                    out=qsl[:, dc * 512:(dc + 1) * 512],
                    in_=scr["q_scr"][dc * 128:(dc + 1) * 128, sl * 512:(sl + 1) * 512]), 16)
            for dc in range(8):
                cx.inc("sd_s", s.dma_start(
                    out=gsl[:, dc * 512:(dc + 1) * 512],
                    in_=scr["g_scr"][dc * 128:(dc + 1) * 128, sl * 512:(sl + 1) * 512]), 16)
        blk.sync(qg_s)
        qg_ld = cx.val("sd_s")

        for p in range(8):
            pair_prev_st = cx.val("s_t")
            sv_epi_prev = cx.val("s_v")

            def me_g(g, sl=sl, p=p, pair_prev_st=pair_prev_st,
                     pp_st=pair_pp_st, first=(sl == 0 and p == 0)):
                if first:
                    cx.wait(g, "st0", l1["done_st"][0])
                    cx.wait(g, "st1", l1["done_st"][1])
                    cx.wait(g, "s_v", c5_done)
                g.wait_ge(cx.h("s_t"), pp_st)
                cx.inc("gw%d" % (p % 2), g.dma_start(
                    out=kbuf[:, (p % 2) * T:(p % 2 + 1) * T],
                    in_=scr["k_scr"][p * 128:(p + 1) * 128, :]), 16)
                g.wait_ge(cx.h("s_t"), pair_prev_st)
                cx.inc("sd_g", g.dma_start(
                    out=mbuf[:, 0:1664],
                    in_=AP(scr["rtil"], (2 * p) * 1792, [[1, 128], [1, 1664]])), 16)
                cx.inc("sd_g", g.dma_start(
                    out=mbuf[:, 1664:3328],
                    in_=AP(scr["rtil"], (2 * p + 1) * 1792, [[1, 128], [1, 1664]])), 16)
            blk.gpsimd(me_g)
            k_ld = cx.val("gw%d" % (p % 2))
            me_ld = cx.val("sd_g")

            st_j0 = cx.val("s_t")
            sa_j0 = cx.val("s_a")

            def pair_t(t, sl=sl, p=p, me_ld=me_ld, k_ld=k_ld, st_j0=st_j0,
                       sa_j0=sa_j0, qg_ld=qg_ld, sv_epi_prev=sv_epi_prev,
                       wo_ld=wo_ld):
                cx.wait(t, "sd_g", me_ld)
                t.wait_ge(cx.h("gw%d" % (p % 2)), k_ld)
                cx.wait(t, "sd_s", qg_ld)
                cx.wait(t, "s_v", sv_epi_prev)
                for jc in range(33):
                    if jc < 32:
                        par = jc % 2
                        d0 = sl * 512 - jc * 128
                        nd = -766 < d0 < 382
                        if jc >= 2:
                            t.wait_ge(cx.h("s_a"), sa_j0 + jc - 1)
                        if nd:
                            ms = d0 + 766
                            t.matmul(psA[:, par * 512:(par + 1) * 512],
                                     C["identJ"][:, :], mbuf[:, ms:ms + 512],
                                     start=True, stop=False)
                            t.matmul(psB[:, par * 512:(par + 1) * 512],
                                     C["identJ"][:, :], mbuf[:, 1664 + ms:1664 + ms + 512],
                                     start=True, stop=False)
                        t.matmul(psA[:, par * 512:(par + 1) * 512],
                                 kbuf[0:64, (p % 2) * T + jc * 128: (p % 2) * T + (jc + 1) * 128],
                                 qsl[0:64, p * 512:(p + 1) * 512],
                                 start=not nd, stop=True, tile_position=(0, 0))
                        t.matmul(psB[:, par * 512:(par + 1) * 512],
                                 kbuf[64:128, (p % 2) * T + jc * 128: (p % 2) * T + (jc + 1) * 128],
                                 qsl[64:128, p * 512:(p + 1) * 512],
                                 start=not nd, stop=True, tile_position=(64, 0))
                        cx.inc("s_t", t.drain())
                    if jc >= 1:
                        jp = jc - 1
                        parp = jp % 2
                        t.wait_ge(cx.h("s_a"), sa_j0 + jp + 1)
                        t.matmul(pso[:, 0:512],
                                 vtm[:, jp * VW + (2 * p) * 65: jp * VW + (2 * p + 1) * 65],
                                 EA5[:, parp * 512:(parp + 1) * 512],
                                 start=(jp == 0), stop=(jp == 31))
                        t.matmul(pso[:, 512:1024],
                                 vtm[:, jp * VW + (2 * p + 1) * 65: jp * VW + (2 * p + 2) * 65],
                                 EB5[:, parp * 512:(parp + 1) * 512],
                                 start=(jp == 0), stop=(jp == 31))
                        cx.inc("s_t", t.drain())
            blk.tensor(pair_t)
            pv_all = cx.val("s_t")

            def pair_s(s, sl=sl, p=p, st_j0=st_j0):
                for jc in range(32):
                    par = jc % 2
                    d0 = sl * 512 - jc * 128
                    if -766 < d0 < 382:
                        vv = 0
                    elif d0 >= 382:
                        vv = 2
                    else:
                        vv = 1
                    cx.wait(s, "s_t", st_j0 + (1 if jc == 0 else 2 * jc))
                    s.activation(EA5[:, par * 512:(par + 1) * 512],
                                 psA[:, par * 512:(par + 1) * 512], AF.Exp,
                                 bias=bias3[:, vv * 16 + 2 * p:vv * 16 + 2 * p + 1],
                                 scale=1.0)
                    s.activation(EB5[:, par * 512:(par + 1) * 512],
                                 psB[:, par * 512:(par + 1) * 512], AF.Exp,
                                 bias=bias3[:, vv * 16 + 2 * p + 1:vv * 16 + 2 * p + 2],
                                 scale=1.0)
                    cx.inc("s_a", s.drain())
            blk.scalar(pair_s)

            def ep_v1(v, pv_all=pv_all):
                cx.wait(v, "s_t", pv_all)
                v.tensor_copy(lr65[64:65, 0:1024], pso[64:65, 0:1024])
                v.drain()
                v.reciprocal(lr65[64:65, 0:1024], lr65[64:65, 0:1024])
                v.drain()
                v.tensor_copy(pvB[:, :], pso[0:64, 512:1024])
                cx.inc("s_v", v.drain())
            blk.vector(ep_v1)
            m1 = cx.val("s_v")

            def ep_t1(t, m1=m1):
                cx.wait(t, "s_v", m1)
                t.matmul(psA[0:64, 0:512], ones65[64:65, 0:64],
                         lr65[64:65, 0:512], start=True, stop=True,
                         tile_position=(64, 0))
                t.matmul(psA[64:128, 512:1024], ones65[64:65, 0:64],
                         lr65[64:65, 512:1024], start=True, stop=True,
                         tile_position=(64, 64))
                t.matmul(psB[:, 0:512], shid[:, :], pvB[:, :],
                         start=True, stop=True)
                cx.inc("s_t", t.drain())
            blk.tensor(ep_t1)
            m2 = cx.val("s_t")

            def ep_v2(v, p=p, m2=m2):
                cx.wait(v, "s_t", m2)
                v.tensor_copy(rbc[0:64, :], psA[0:64, 0:512])
                v.tensor_copy(rbc[64:128, :], psA[64:128, 512:1024])
                v.drain()
                v.tensor_tensor(grbc[0:64, :], gsl[0:64, p * 512:(p + 1) * 512],
                                rbc[0:64, :], ALU.mult)
                v.tensor_tensor(grbc[64:128, :], gsl[64:128, p * 512:(p + 1) * 512],
                                rbc[64:128, :], ALU.mult)
                v.drain()
                v.scalar_tensor_tensor(gated[0:64, p * 512:(p + 1) * 512],
                                       pso[0:64, 0:512], 1.0, grbc[0:64, :],
                                       ALU.mult, ALU.mult)
                v.scalar_tensor_tensor(gated[64:128, p * 512:(p + 1) * 512],
                                       psB[64:128, 0:512], 1.0, grbc[64:128, :],
                                       ALU.mult, ALU.mult)
                cx.inc("s_v", v.drain())
            blk.vector(ep_v2)
            pair_pp_st = pair_prev_st

        pairs_done_v = cx.val("s_v")
        pairs_done_st = cx.val("s_t")

        xz0 = cx.val("sd_s")

        x1_marks = []
        sv_wo0 = cx.val("s_v")

        def x1_s(s, sl=sl, sv_wo0=sv_wo0, pwm=prev_wo_marks):
            for tokc in range(4):
                if tokc >= 2:
                    s.wait_ge(cx.h("s_v"), sv_wo0 + tokc - 1)
                elif pwm is not None:
                    s.wait_ge(cx.h("s_v"), pwm[tokc + 2])
                x1_marks.append(cx.inc("sq%d" % (tokc % 2), s.dma_start(
                    out=x1c[:, (tokc % 2) * 1024:(tokc % 2 + 1) * 1024],
                    in_=scr["x1s"][(sl * 4 + tokc) * 128:(sl * 4 + tokc + 1) * 128, :]), 16))
        blk.sync(x1_s)

        def wo_t(t, sl=sl, pairs_done_v=pairs_done_v, sv_wo0=sv_wo0):
            cx.wait(t, "s_v", pairs_done_v)
            for tokc in range(4):
                ps = psA if tokc % 2 == 0 else psB
                if tokc >= 2:
                    t.wait_ge(cx.h("s_v"), sv_wo0 + tokc - 1)
                for nb in range(2):
                    t.matmul(ps[:, nb * 512:(nb + 1) * 512],
                             C["onesb"][0:1, 0:128],
                             borow[0:1, nb * 512:(nb + 1) * 512],
                             start=True, stop=False)
                for dc in range(8):
                    for nb in range(2):
                        t.matmul(ps[:, nb * 512:(nb + 1) * 512],
                                 gated[:, dc * 512 + tokc * 128: dc * 512 + (tokc + 1) * 128],
                                 wo64[:, dc * 1024 + nb * 512: dc * 1024 + (nb + 1) * 512],
                                 start=False, stop=(dc == 7))
                cx.inc("s_t", t.drain())
        blk.tensor(wo_t)
        wo_st0 = cx.val("s_t") - 4

        def wo_v(v, sl=sl, wo_st0=wo_st0, x1_marks=x1_marks):
            for tokc in range(4):
                ps = psA if tokc % 2 == 0 else psB
                cx.wait(v, "s_t", wo_st0 + tokc + 1)
                cx.wait(v, "sq%d" % (tokc % 2), x1_marks[tokc])
                v.tensor_add(lnx[:, tokc * 1024:(tokc + 1) * 1024], ps[:, :],
                             x1c[:, (tokc % 2) * 1024:(tokc % 2 + 1) * 1024])
                v.drain()
                v.reduce_sum(stat[:, 168 + tokc:169 + tokc],
                             lnx[:, tokc * 1024:(tokc + 1) * 1024], AX.X)
                v.scalar_tensor_tensor(ln5q[:, :],
                                       lnx[:, tokc * 1024:(tokc + 1) * 1024], 1.0,
                                       lnx[:, tokc * 1024:(tokc + 1) * 1024],
                                       ALU.mult, ALU.mult,
                                       accum_out=stat[:, 172 + tokc:173 + tokc])
                cx.inc("s_v", v.drain())
            v.tensor_scalar_mul(stat[:, 168:172], stat[:, 168:172], 1.0 / D)
            v.tensor_scalar_mul(stat[:, 172:176], stat[:, 172:176], 1.0 / D)
            v.drain()
            v.scalar_tensor_tensor(stat[:, 176:180], stat[:, 168:172], 1.0,
                                   stat[:, 168:172], ALU.mult, ALU.mult)
            v.drain()
            v.tensor_sub(stat[:, 172:176], stat[:, 172:176], stat[:, 176:180])
            v.drain()
            v.tensor_scalar_add(stat[:, 172:176], stat[:, 172:176], 1e-5)
            v.drain()
            v.reciprocal(stat[:, 176:180], stat[:, 172:176])
            cx.inc("s_v", v.drain())
        blk.vector(wo_v)
        lnst = cx.val("s_v")

        def ln_s5(s, lnst=lnst):
            cx.wait(s, "s_v", lnst)
            s.activation(stat[:, 172:176], stat[:, 176:180], AF.Sqrt)
            cx.inc("s_a", s.drain())
        blk.scalar(ln_s5)
        rstd5 = cx.val("s_a")

        for tokc in range(4):
            def ap_v(v, sl=sl, tokc=tokc, rstd5=rstd5):
                cx.wait(v, "s_a", rstd5)
                if x2_marks:
                    cx.wait(v, "st0", x2_marks[-1])
                    if write_out_dbg:
                        cx.wait(v, "sd_o")
                v.tensor_scalar(outc[:, :], lnx[:, tokc * 1024:(tokc + 1) * 1024],
                                stat[:, 168 + tokc:169 + tokc],
                                stat[:, 172 + tokc:173 + tokc],
                                ALU.subtract, ALU.mult)
                v.drain()
                v.tensor_reduce(stat[:, sl * 4 + tokc:sl * 4 + tokc + 1],
                                outc[:, :], AX.X, ALU.max,
                                apply_absolute_value=True)
                cx.inc("s_v", v.drain())
            blk.vector(ap_v)
            apd = cx.val("s_v")

            def x2_s(s, sl=sl, tokc=tokc, apd=apd):
                cx.wait(s, "s_v", apd)
                cx.inc("st0", s.dma_start(
                    out=scr["x2s"][(sl * 4 + tokc) * 128:(sl * 4 + tokc + 1) * 128, :],
                    in_=outc[:, :]), 16)
                if write_out_dbg:
                    cx.inc("sd_o", s.dma_start(
                        out=scr["out"][(sl * 4 + tokc) * 128:(sl * 4 + tokc + 1) * 128, :],
                        in_=outc[:, :]), 16)
            blk.sync(x2_s)
            x2_marks.append(cx.val("st0"))
        prev_wo_marks = [sv_wo0 + i for i in range(5)]

    done_v = cx.val("s_v")
    done_s = cx.val("st0")
    es.close()
    return dict(done_v=done_v, done_s=done_s)


def phase6_l1ffn(cx, C, dram, scr, l1, x2, wq_done_g):
    nc, blk = cx.nc, cx.blk
    stat, ident = C["stat"], C["ident"]
    NI = 22

    es = ExitStack()
    row6 = es.enter_context(nc.sbuf_tensor("row6", [1, 16], F32))
    bc6 = es.enter_context(nc.sbuf_tensor("bc6", [128, 4], F32))
    clipsb = es.enter_context(nc.sbuf_tensor("clipsb", [1, 2], F32))
    es_ar1 = ExitStack()
    ps6 = es_ar1.enter_context(nc.psum_tensor("ps6", [128, 512], F32))

    def ar1_v(v, x2=x2):
        cx.wait(v, "s_v", x2["done_v"])
        v.tensor_reduce(stat[:, 32:33], stat[:, 0:32], AX.X, ALU.max)
        cx.inc("s_v", v.drain())
    blk.vector(ar1_v)
    m1 = cx.val("s_v")

    def ar1_t(t, m1=m1):
        cx.wait(t, "s_v", m1)
        t.transpose(ps6[0:1, 0:128], stat[:, 32:33], C["identf"][:, :])
        cx.inc("s_t", t.drain())
    blk.tensor(ar1_t)
    m2 = cx.val("s_t")

    def ar1_v2(v, m2=m2):
        cx.wait(v, "s_t", m2)
        v.tensor_reduce(row6[0:1, 0:1], ps6[0:1, 0:128], AX.X, ALU.max)
        v.drain()
        v.tensor_copy(row6[0:1, 0:16], AP(row6, 0, [[16, 1], [0, 16], [1, 1]]))
        cx.inc("s_v", v.drain())
    blk.vector(ar1_v2)
    m3 = cx.val("s_v")

    def ar1_g(g, m3=m3):
        cx.wait(g, "s_v", m3)
        cx.inc("sd_o2", g.dma_start(out=scr["cc1_in"][:, :], in_=row6[0:1, 0:16]), 16)
        cx.wait(g, "sd_o2")
        g.collective_compute("AllReduce", ALU.max,
                             replica_groups=[list(range(N_CORES))],
                             ins=[scr["cc1_in"].ap().opt()],
                             outs=[scr["cc1_out"].ap().opt()]).then_inc(cx.h("s_g"), 1)
        cx._cnt["s_g"] += 1
        cx.wait(g, "s_g")
        cx.inc("sd_o2", g.dma_start(out=row6[0:1, 0:16], in_=scr["cc1_out"][:, :]), 16)
        cx.inc("sd_o2", g.dma_start(out=clipsb[:, :], in_=dram["clips"][:, :]), 16)
    blk.gpsimd(ar1_g)
    m4 = cx.val("sd_o2")

    def ar1_s(s, m4=m4):
        cx.wait(s, "sd_o2", m4)
        s.activation(clipsb[0:1, 0:1], clipsb[0:1, 0:1], AF.Sigmoid)
        cx.inc("s_a", s.drain())
    blk.scalar(ar1_s)
    m5 = cx.val("s_a")

    def ar1_v3(v, m5=m5):
        cx.wait(v, "s_a", m5)
        v.tensor_tensor(row6[0:1, 1:2], row6[0:1, 0:1], clipsb[0:1, 0:1], ALU.mult)
        v.drain()
        v.tensor_scalar_mul(row6[0:1, 1:2], row6[0:1, 1:2], 1.0 / 127.0)
        v.drain()
        v.reciprocal(row6[0:1, 2:3], row6[0:1, 1:2])
        cx.inc("s_v", v.drain())
    blk.vector(ar1_v3)
    m6 = cx.val("s_v")

    def ar1_t2(t, m6=m6):
        cx.wait(t, "s_v", m6)
        t.matmul(ps6[:, 0:2], C["onesf"][0:1, 0:128], row6[0:1, 1:3],
                 start=True, stop=True)
        cx.inc("s_t", t.drain())
    blk.tensor(ar1_t2)
    m7 = cx.val("s_t")

    def ar1_v4(v, m7=m7):
        cx.wait(v, "s_t", m7)
        v.tensor_copy(bc6[:, 0:2], ps6[:, 0:2])
        cx.inc("s_v", v.drain())
    blk.vector(ar1_v4)
    sx_done = cx.val("s_v")
    es_ar1.close()

    es_a = ExitStack()
    x2c = es_a.enter_context(nc.sbuf_tensor("x2c", [128, 2 * 1024], F32))
    qsc6 = es_a.enter_context(nc.sbuf_tensor("qsc6", [128, 1024], F32))
    xqb6 = es_a.enter_context(nc.sbuf_tensor("xqb6", [128, 8 * 1024], BF16))
    xqf6 = es_a.enter_context(nc.sbuf_tensor("xqf6", [128, 8 * 1024], BF16))
    wgsb = es_a.enter_context(nc.sbuf_tensor("wgsb", [128, 8 * 2816], BF16))
    wusb = es_a.enter_context(nc.sbuf_tensor("wusb", [128, 8 * 2816], BF16))
    sg6 = es_a.enter_context(nc.sbuf_tensor("sg6", [128, 2 * 1024], BF16))
    hc6 = es_a.enter_context(nc.sbuf_tensor("hc6", [128, 2 * 1024], BF16))
    pst6 = es_a.enter_context(nc.psum_tensor("pst6", [128, 2048], BF16))
    psg = es_a.enter_context(nc.psum_tensor("psg", [128, 2048], F32))
    psu = es_a.enter_context(nc.psum_tensor("psu", [128, 1024], F32))

    prev_q_last = None
    prev_gu_last = None
    h_marks_s = []
    for sl in range(4):
        tc0 = sl * 8

        wg_prev_st = cx.val("s_t")

        def w6_g(g, sl=sl, wg_prev_st=wg_prev_st, x2=x2):
            if sl == 0:
                cx.wait(g, "sd_g", wq_done_g)
                cx.wait(g, "s_v", x2["done_v"])
                cx.wait(g, "st0", x2["done_s"])
            else:
                g.wait_ge(cx.h("s_t"), wg_prev_st)
            for e in range(ND):
                cx.inc("sd_g", g.dma_start(
                    out=wgsb[:, e * 2816:(e + 1) * 2816],
                    in_=scr["wgT"][e * 128:(e + 1) * 128, :]), 16)
            for e in range(ND):
                cx.inc("sd_g", g.dma_start(
                    out=wusb[:, e * 2816:(e + 1) * 2816],
                    in_=scr["wuT"][e * 128:(e + 1) * 128, :]), 16)
        blk.gpsimd(w6_g)
        w6_ld = cx.val("sd_g")

        q_marks = []
        for c in range(8):
            def x2_s6(s, sl=sl, tc0=tc0, c=c, x2=x2, prev_q_last=prev_q_last):
                if sl == 0 and c == 0:
                    cx.wait(s, "st0", x2["done_s"])
                if c >= 2:
                    cx.wait(s, "s_v", q_marks[c - 2])
                elif prev_q_last is not None:
                    cx.wait(s, "s_v", prev_q_last)
                cx.inc("sq%d" % (c % 2), s.dma_start(
                    out=x2c[:, (c % 2) * 1024:(c % 2 + 1) * 1024],
                    in_=scr["x2s"][(tc0 + c) * 128:(tc0 + c + 1) * 128, :]), 16)
            blk.sync(x2_s6)
            xld = cx.val("sq%d" % (c % 2))

            def q_v6(v, c=c, xld=xld, sx_done=sx_done):
                if c == 0:
                    cx.wait(v, "s_v", sx_done)
                cx.wait(v, "sq%d" % (c % 2), xld)
                v.tensor_scalar(qsc6[:, :], x2c[:, (c % 2) * 1024:(c % 2 + 1) * 1024],
                                bc6[:, 1:2], MAGIC, ALU.mult, ALU.add)
                v.drain()
                v.tensor_scalar(xqb6[:, c * 1024:(c + 1) * 1024], qsc6[:, :],
                                MAGIC, 127.0, ALU.subtract, ALU.min)
                v.drain()
                v.tensor_scalar_max(xqb6[:, c * 1024:(c + 1) * 1024],
                                    xqb6[:, c * 1024:(c + 1) * 1024], -128.0)
                cx.inc("s_v", v.drain())
            blk.vector(q_v6)
            q_marks.append(cx.val("s_v"))
        prev_q_last = q_marks[-1]
        emit_tmfm_transpose(cx, xqb6, xqf6, ident, pst6, cx.val("s_v"))
        xqf_done = cx.val("s_v")

        h_marks_v = []
        for dc in range(NI):
            def gu_t(t, sl=sl, dc=dc, w6_ld=w6_ld, xqf_done=xqf_done,
                     prev_gu_last=prev_gu_last):
                if dc == 0:
                    cx.wait(t, "s_v", xqf_done)
                    cx.wait(t, "sd_g", w6_ld)
                if dc >= 2:
                    cx.wait(t, "s_v", h_marks_v[dc - 2])
                elif prev_gu_last is not None:
                    cx.wait(t, "s_v", prev_gu_last)
                par = dc % 2
                for nb in range(2):
                    for e in range(ND):
                        t.matmul(psg[:, par * 1024 + nb * 512: par * 1024 + (nb + 1) * 512],
                                 wgsb[:, e * 2816 + dc * 128:(e) * 2816 + (dc + 1) * 128],
                                 xqf6[:, e * 1024 + nb * 512: e * 1024 + (nb + 1) * 512],
                                 start=(e == 0), stop=(e == ND - 1))
                cx.inc("s_t", t.drain())
                if dc >= 1:
                    cx.wait(t, "s_v", h_marks_v[dc - 1])
                for nb in range(2):
                    for e in range(ND):
                        t.matmul(psu[:, nb * 512:(nb + 1) * 512],
                                 wusb[:, e * 2816 + dc * 128:(e) * 2816 + (dc + 1) * 128],
                                 xqf6[:, e * 1024 + nb * 512: e * 1024 + (nb + 1) * 512],
                                 start=(e == 0), stop=(e == ND - 1))
                cx.inc("s_t", t.drain())
            blk.tensor(gu_t)
            g_mm = cx.val("s_t") - 1
            u_mm = cx.val("s_t")

            def gu_a(s, dc=dc, g_mm=g_mm):
                cx.wait(s, "s_t", g_mm)
                par = dc % 2
                s.activation(sg6[:, par * 1024:(par + 1) * 1024],
                             psg[:, par * 1024:(par + 1) * 1024],
                             AF.Silu, bias=0.0, scale=bc6[:, 0:1])
                cx.inc("s_a", s.drain())
            blk.scalar(gu_a)
            sg_done = cx.val("s_a")

            def gu_v(v, sl=sl, dc=dc, u_mm=u_mm, sg_done=sg_done,
                     h_marks_s=h_marks_s):
                cx.wait(v, "s_t", u_mm)
                cx.wait(v, "s_a", sg_done)
                same_par = [m for m in h_marks_s if m[0] == dc % 2]
                if same_par:
                    cx.wait(v, "gx%d" % (dc % 2), same_par[-1][1])
                par = dc % 2
                v.scalar_tensor_tensor(hc6[:, par * 1024:(par + 1) * 1024],
                                       psu[:, :],
                                       bc6[:, 0:1], sg6[:, par * 1024:(par + 1) * 1024],
                                       ALU.mult, ALU.mult)
                v.drain()
                v.tensor_reduce(stat[:, 96 + dc:97 + dc],
                                hc6[:, par * 1024:(par + 1) * 1024], AX.X, ALU.max,
                                apply_absolute_value=True)
                cx.inc("s_v", v.drain())
            blk.vector(gu_v)
            h_marks_v.append(cx.val("s_v"))

            def h_g(g, sl=sl, dc=dc, hv=h_marks_v[-1]):
                cx.wait(g, "s_v", hv)
                par = dc % 2
                cx.inc("gx%d" % par, g.dma_start(
                    out=scr["hs"][dc * 128:(dc + 1) * 128, sl * 1024:(sl + 1) * 1024],
                    in_=hc6[:, par * 1024:(par + 1) * 1024]), 16)
            blk.gpsimd(h_g)
            h_marks_s.append((par_ := dc % 2, cx.val("gx%d" % (dc % 2))))

        prev_gu_last = h_marks_v[-1]

        def hm_v(v, sl=sl, last=h_marks_v[-1]):
            cx.wait(v, "s_v", last)
            v.tensor_reduce(stat[:, 33 + sl:34 + sl], stat[:, 96:96 + NI],
                            AX.X, ALU.max)
            cx.inc("s_v", v.drain())
        blk.vector(hm_v)
    h_all_v = cx.val("s_v")
    h_all_g = (cx.val("gx0"), cx.val("gx1"))
    es_a.close()

    es_ar2 = ExitStack()
    ps6 = es_ar2.enter_context(nc.psum_tensor("ps6b", [128, 512], F32))
    def ar2_v(v, h_all_v=h_all_v):
        cx.wait(v, "s_v", h_all_v)
        v.tensor_reduce(stat[:, 37:38], stat[:, 33:37], AX.X, ALU.max)
        cx.inc("s_v", v.drain())
    blk.vector(ar2_v)
    n1 = cx.val("s_v")

    def ar2_t(t, n1=n1):
        cx.wait(t, "s_v", n1)
        t.transpose(ps6[0:1, 0:128], stat[:, 37:38], C["identf"][:, :])
        cx.inc("s_t", t.drain())
    blk.tensor(ar2_t)
    n2 = cx.val("s_t")

    def ar2_v2(v, n2=n2):
        cx.wait(v, "s_t", n2)
        v.tensor_reduce(row6[0:1, 4:5], ps6[0:1, 0:128], AX.X, ALU.max)
        v.drain()
        v.tensor_copy(row6[0:1, 0:16], AP(row6, 4, [[16, 1], [0, 16], [1, 1]]))
        cx.inc("s_v", v.drain())
    blk.vector(ar2_v2)
    n3 = cx.val("s_v")

    def ar2_g(g, n3=n3):
        cx.wait(g, "s_v", n3)
        cx.inc("sd_o2", g.dma_start(out=scr["cc2_in"][:, :], in_=row6[0:1, 0:16]), 16)
        cx.wait(g, "sd_o2")
        g.collective_compute("AllReduce", ALU.max,
                             replica_groups=[list(range(N_CORES))],
                             ins=[scr["cc2_in"].ap().opt()],
                             outs=[scr["cc2_out"].ap().opt()]).then_inc(cx.h("s_g"), 1)
        cx._cnt["s_g"] += 1
        cx.wait(g, "s_g")
        cx.inc("sd_o2", g.dma_start(out=row6[0:1, 0:16], in_=scr["cc2_out"][:, :]), 16)
    blk.gpsimd(ar2_g)
    n4 = cx.val("sd_o2")

    def ar2_s(s, n4=n4):
        cx.wait(s, "sd_o2", n4)
        s.activation(clipsb[0:1, 1:2], clipsb[0:1, 1:2], AF.Sigmoid)
        cx.inc("s_a", s.drain())
    blk.scalar(ar2_s)
    n5 = cx.val("s_a")

    def ar2_v3(v, n5=n5):
        cx.wait(v, "s_a", n5)
        v.tensor_tensor(row6[0:1, 5:6], row6[0:1, 0:1], clipsb[0:1, 1:2], ALU.mult)
        v.drain()
        v.tensor_scalar_mul(row6[0:1, 5:6], row6[0:1, 5:6], 1.0 / 127.0)
        v.drain()
        v.reciprocal(row6[0:1, 6:7], row6[0:1, 5:6])
        cx.inc("s_v", v.drain())
    blk.vector(ar2_v3)
    n6 = cx.val("s_v")

    def ar2_t2(t, n6=n6):
        cx.wait(t, "s_v", n6)
        t.matmul(ps6[:, 0:2], C["onesf"][0:1, 0:128], row6[0:1, 5:7],
                 start=True, stop=True)
        cx.inc("s_t", t.drain())
    blk.tensor(ar2_t2)
    n7 = cx.val("s_t")

    def ar2_v4(v, n7=n7):
        cx.wait(v, "s_t", n7)
        v.tensor_copy(bc6[:, 2:4], ps6[:, 0:2])
        cx.inc("s_v", v.drain())
    blk.vector(ar2_v4)
    sh_done = cx.val("s_v")
    es_ar2.close()

    es_b = ExitStack()
    hbt = es_b.enter_context(nc.sbuf_tensor("hbt", [128, 2 * 1024], BF16))
    qsc6b = es_b.enter_context(nc.sbuf_tensor("qsc6b", [128, 1024], F32))
    hq = es_b.enter_context(nc.sbuf_tensor("hq", [128, NI * 1024], BF16))
    wdsb = es_b.enter_context(nc.sbuf_tensor("wdsb", [128, NI * 1024], BF16))
    x2c2 = es_b.enter_context(nc.sbuf_tensor("x2c2", [128, 2 * 1024], F32))
    lnx6 = es_b.enter_context(nc.sbuf_tensor("lnx6", [128, 8 * 1024], F32))
    ln6q = es_b.enter_context(nc.sbuf_tensor("ln6q", [128, 1024], BF16))
    outc6 = es_b.enter_context(nc.sbuf_tensor("outc6", [128, 1024], F32))
    psz = es_b.enter_context(nc.psum_tensor("psz", [128, 2048], F32))

    prev_dn_st = None
    prev_ep = []
    prev_hq_marks = None
    out_marks = []
    for sl in range(4):
        tc0 = sl * 8
        wd_prev = cx.val("s_t")

        def wd_g(g, sl=sl, wd_prev=wd_prev, h_all_g=h_all_g):
            if sl == 0:
                g.wait_ge(cx.h("gx0"), h_all_g[0])
                g.wait_ge(cx.h("gx1"), h_all_g[1])
            else:
                g.wait_ge(cx.h("s_t"), wd_prev)
            for ic in range(NI):
                cx.inc("sd_g", g.dma_start(
                    out=wdsb[:, ic * 1024:(ic + 1) * 1024],
                    in_=scr["wdT"][ic * 128:(ic + 1) * 128, :]), 16)
        blk.gpsimd(wd_g)
        wd_ld = cx.val("sd_g")

        hq_marks = []
        for ic in range(NI):
            def h_s6(s, sl=sl, ic=ic, x2=x2, phm=prev_hq_marks,
                     pep=prev_ep[:]):
                if ic >= 2:
                    cx.wait(s, "s_v", hq_marks[ic - 2])
                elif sl == 0:
                    cx.wait(s, "s_v", sh_done)
                    g0, g1 = h_all_g
                    s.wait_ge(cx.h("gx0"), g0)
                    s.wait_ge(cx.h("gx1"), g1)
                else:
                    cx.wait(s, "s_v", max(phm[ic - 2], pep[ic - 2]))
                cx.inc("sq%d" % (ic % 2), s.dma_start(
                    out=hbt[:, (ic % 2) * 1024:(ic % 2 + 1) * 1024],
                    in_=scr["hs"][ic * 128:(ic + 1) * 128, sl * 1024:(sl + 1) * 1024]), 16)
            blk.sync(h_s6)
            hld = cx.val("sq%d" % (ic % 2))

            def hq_v(v, ic=ic, hld=hld, sh_done=sh_done, prev_dn_st=prev_dn_st):
                if ic == 0:
                    cx.wait(v, "s_v", sh_done)
                    if prev_dn_st is not None:
                        cx.wait(v, "s_t", prev_dn_st)
                cx.wait(v, "sq%d" % (ic % 2), hld)
                v.tensor_scalar(qsc6b[:, :], hbt[:, (ic % 2) * 1024:(ic % 2 + 1) * 1024],
                                bc6[:, 3:4], MAGIC, ALU.mult, ALU.add)
                v.drain()
                v.tensor_scalar(hq[:, ic * 1024:(ic + 1) * 1024], qsc6b[:, :],
                                MAGIC, 127.0, ALU.subtract, ALU.min)
                v.drain()
                v.tensor_scalar_max(hq[:, ic * 1024:(ic + 1) * 1024],
                                    hq[:, ic * 1024:(ic + 1) * 1024], -128.0)
                cx.inc("s_v", v.drain())
            blk.vector(hq_v)
            hq_marks.append(cx.val("s_v"))
        hq_done = cx.val("s_v")
        prev_hq_marks = hq_marks

        ep_marks = []
        for tokc in range(8):
            def x2_s6b(s, sl=sl, tokc=tokc, x2=x2, prev_ep=prev_ep,
                       hq_done=hq_done):
                if tokc >= 2:
                    cx.wait(s, "s_v", ep_marks[tokc - 2])
                else:
                    m = hq_done
                    if prev_ep:
                        m = max(m, prev_ep[tokc - 2])
                    cx.wait(s, "s_v", m)
                cx.inc("sq%d" % (tokc % 2), s.dma_start(
                    out=x2c2[:, (tokc % 2) * 1024:(tokc % 2 + 1) * 1024],
                    in_=scr["x2s"][(sl * 8 + tokc) * 128:(sl * 8 + tokc + 1) * 128, :]), 16)
            blk.sync(x2_s6b)
            x2ld = cx.val("sq%d" % (tokc % 2))

            def dn_t(t, sl=sl, tokc=tokc, hq_done=hq_done, wd_ld=wd_ld,
                     x2ld=x2ld):
                if tokc == 0:
                    cx.wait(t, "s_v", hq_done)
                    cx.wait(t, "sd_g", wd_ld)
                par = tokc % 2
                if tokc >= 2:
                    t.wait_ge(cx.h("s_v"), ep_marks[tokc - 2])
                for ic in range(NI):
                    for nb in range(2):
                        t.matmul(psz[:, par * 1024 + nb * 512: par * 1024 + (nb + 1) * 512],
                                 hq[:, ic * 1024 + tokc * 128: ic * 1024 + (tokc + 1) * 128],
                                 wdsb[:, ic * 1024 + nb * 512: ic * 1024 + (nb + 1) * 512],
                                 start=(ic == 0), stop=(ic == NI - 1))
                cx.inc("s_t", t.drain())
            blk.tensor(dn_t)
            dn_mm = cx.val("s_t")

            def dn_v(v, sl=sl, tokc=tokc, dn_mm=dn_mm, x2ld=x2ld):
                cx.wait(v, "s_t", dn_mm)
                cx.wait(v, "sq%d" % (tokc % 2), x2ld)
                par = tokc % 2
                v.scalar_tensor_tensor(lnx6[:, tokc * 1024:(tokc + 1) * 1024],
                                       psz[:, par * 1024:(par + 1) * 1024],
                                       bc6[:, 2:3],
                                       x2c2[:, par * 1024:(par + 1) * 1024],
                                       ALU.mult, ALU.add)
                v.drain()
                v.reduce_sum(stat[:, 104 + tokc:105 + tokc],
                             lnx6[:, tokc * 1024:(tokc + 1) * 1024], AX.X)
                v.scalar_tensor_tensor(ln6q[:, :],
                                       lnx6[:, tokc * 1024:(tokc + 1) * 1024], 1.0,
                                       lnx6[:, tokc * 1024:(tokc + 1) * 1024],
                                       ALU.mult, ALU.mult,
                                       accum_out=stat[:, 112 + tokc:113 + tokc])
                cx.inc("s_v", v.drain())
            blk.vector(dn_v)
            ep_marks.append(cx.val("s_v"))

        prev_dn_st = cx.val("s_t")
        prev_ep = ep_marks

        def ln6_v(v, last=ep_marks[-1]):
            cx.wait(v, "s_v", last)
            v.tensor_scalar_mul(stat[:, 104:112], stat[:, 104:112], 1.0 / D)
            v.tensor_scalar_mul(stat[:, 112:120], stat[:, 112:120], 1.0 / D)
            v.drain()
            v.scalar_tensor_tensor(stat[:, 120:128], stat[:, 104:112], 1.0,
                                   stat[:, 104:112], ALU.mult, ALU.mult)
            v.drain()
            v.tensor_sub(stat[:, 112:120], stat[:, 112:120], stat[:, 120:128])
            v.drain()
            v.tensor_scalar_add(stat[:, 112:120], stat[:, 112:120], 1e-5)
            v.drain()
            v.reciprocal(stat[:, 120:128], stat[:, 112:120])
            cx.inc("s_v", v.drain())
        blk.vector(ln6_v)
        l6 = cx.val("s_v")

        def ln6_s(s, l6=l6):
            cx.wait(s, "s_v", l6)
            s.activation(stat[:, 112:120], stat[:, 120:128], AF.Sqrt)
            cx.inc("s_a", s.drain())
        blk.scalar(ln6_s)
        r6 = cx.val("s_a")

        for tokc in range(8):
            def fo_v(v, sl=sl, tokc=tokc, r6=r6):
                cx.wait(v, "s_a", r6)
                if out_marks:
                    cx.wait(v, "sd_o", out_marks[-1])
                v.tensor_scalar(outc6[:, :], lnx6[:, tokc * 1024:(tokc + 1) * 1024],
                                stat[:, 104 + tokc:105 + tokc],
                                stat[:, 112 + tokc:113 + tokc],
                                ALU.subtract, ALU.mult)
                cx.inc("s_v", v.drain())
            blk.vector(fo_v)
            fod = cx.val("s_v")

            def fo_s(s, sl=sl, tokc=tokc, fod=fod):
                cx.wait(s, "s_v", fod)
                cx.inc("sd_o", s.dma_start(
                    out=scr["out"][(sl * 8 + tokc) * 128:(sl * 8 + tokc + 1) * 128, :],
                    in_=outc6[:, :]), 16)
            blk.sync(fo_s)
            out_marks.append(cx.val("sd_o"))

    es_b.close()
    es.close()

    def fin_s(s):
        cx.wait(s, "sd_o")
        cx.wait(s, "sd_s")
        cx.wait(s, "sd_g")
    blk.sync(fin_s)
